# version 02
# baseline (speedup 1.0000x reference)
from concourse.bass_utils import run_bass_kernel_spmd

import numpy as np
import concourse.bass as bass
import concourse.mybir as mybir
from contextlib import ExitStack

F32 = mybir.dt.float32
BF16 = mybir.dt.bfloat16
I32 = mybir.dt.int32
AF = mybir.ActivationFunctionType
ALU = mybir.AluOpType

E = 1024
NCH = 8
W = 256
H = 16
DH = 64
TOK = 4096
NC_OWN = 16
NC_LOC = 18
NEG = -1.0e30


class Sem:
    def __init__(self, ctx, nc, name):
        self.h = ctx.enter_context(nc.semaphore(name))
        self.n = 0

    def inc(self, ins, k=1):
        ins.then_inc(self.h, k)
        self.n += k
        return self.n

    def dma(self, ins):
        ins.then_inc(self.h, 16)
        self.n += 16
        return self.n


def emit_l0attn(nc, block, semctx, pfx, io, gate):
    xh = io["xh"]
    wqT = io["wqT"]
    wkT = io["wkT"]
    wvT = io["wvT"]
    bq_r = io["bq_r"]
    bk_r = io["bk_r"]
    bv_row = io["bv_row"]
    ones_col = io["ones_col"]
    ones_row = io["ones_row"]
    ones_c16 = io["ones_c16"]
    ones64f = io["ones64f"]
    ek = io["ek"]
    mlo = io["mlo"]
    mhi = io["mhi"]
    x1_out = io["x1"]
    lnrows = nc.dram_tensor(pfx + "lnrows", [NC_LOC, 2, W], F32, kind="Internal").ap()

    xh3 = xh.rearrange("(c p) t -> p c t", p=128)
    x1o3 = x1_out.rearrange("(c p) t -> p c t", p=128)

    with ExitStack() as ctx:
        emit_gate(block, gate)

        def sb(name, shape, dt):
            return ctx.enter_context(nc.sbuf_tensor(pfx + name, shape, dt))

        def S(eng, fn):
            getattr(block, eng)(fn)

        def sem(name):
            s = Sem(semctx, nc, pfx + name)
            PHASE_SEMS.append(s)
            return s

        wq_sb = sb("wq_sb", [128, NCH, E], BF16)
        wk_sb = sb("wk_sb", [128, NCH, E], BF16)
        wv_sb = sb("wv_sb", [128, NCH, E], BF16)
        wtmp = [sb(f"wtmp{i}", [128, E], F32) for i in range(2)]
        bqs_sb = sb("bqs_sb", [128, NCH], F32)
        bk_sb = sb("bk_sb", [128, NCH], F32)
        bvb_sb = sb("bvb_sb", [1, E], BF16)
        bvf_sb = sb("bvf_sb", [1, E], F32)
        onec_sb = sb("onec_sb", [128, 1], BF16)
        oner_sb = sb("oner_sb", [1, 128], BF16)
        onec16_sb = sb("onec16_sb", [128, 16], BF16)
        ones64_sb = sb("ones64_sb", [1, 64], F32)
        e0_sb = sb("e0_sb", [128, 1], F32)
        e1_sb = sb("e1_sb", [128, 1], F32)
        mlo_sb = sb("mlo_sb", [128, 2, W], F32)
        mhi_sb = sb("mhi_sb", [128, 2, W], F32)
        k_ring = sb("k_ring", [128, 4, NCH, W], BF16)
        v_ring = sb("v_ring", [128, 4, 2, H, 65], BF16)
        st_row = [sb(f"st_row{i}", [1, W], F32) for i in range(6)]
        mr_row = sb("mr_row", [1, W], F32)
        xt = [sb(f"xt{i}", [128, NCH, W], F32) for i in range(2)]
        xres = [sb(f"xres{i}", [128, NCH, W], F32) for i in range(2)]
        xsq = sb("xsq", [128, NCH, W], BF16)
        xbq = sb("xbq", [128, NCH, W], BF16)
        rstd_rep = [sb(f"rstd_rep{i}", [128, W], F32) for i in range(2)]
        mr_rep = [sb(f"mr_rep{i}", [128, W], F32) for i in range(2)]
        tnorm = sb("tnorm", [128, NCH, W], F32)
        xn = [sb(f"xn{i}", [128, NCH, W], BF16) for i in range(2)]
        qt = [sb(f"qt{i}", [128, NCH, W], BF16) for i in range(2)]
        probs = [sb(f"probs{i}", [128, 3 * W], BF16) for i in range(2)]
        rz = [sb(f"rz{i}", [1, W], F32) for i in range(2)]
        zrep = [sb(f"zrep{i}", [128, W], F32) for i in range(2)]
        t64 = [sb(f"t64{i}", [128, W], F32) for i in range(2)]
        x1t = [sb(f"x1t{i}", [128, NCH, W], F32) for i in range(2)]

        s_ldc = sem("s_ldc")
        s_wl = [sem(f"s_wl{i}") for i in range(2)]
        s_xt = [sem(f"s_xt{i}") for i in range(2)]
        s_lr = [sem(f"s_lr{i}") for i in range(2)]
        s_xr = [sem(f"s_xr{i}") for i in range(2)]
        s_rw = sem("s_rw")
        s_st1 = sem("s_st1")
        s_w = sem("s_w")
        s_cst = sem("s_cst")
        s_csv = sem("s_csv")
        s_st = sem("s_st")
        s_row = sem("s_row")
        s_norm = sem("s_norm")
        s_pkv = sem("s_pkv")
        s_ev = sem("s_ev")
        s_sc = sem("s_sc")
        s_msk = sem("s_msk")
        s_exp = sem("s_exp")
        s_av = sem("s_av")
        s_rz = sem("s_rz")
        s_t64 = sem("s_t64")
        s_x1 = sem("s_x1")
        s_tk = sem("s_tk")

        def init_loads(e):
            for dst, src in [(bk_sb[:], bk_r), (bqs_sb[:], bq_r), (bvf_sb[:], bv_row),
                             (onec_sb[:], ones_col), (oner_sb[:], ones_row),
                             (onec16_sb[:], ones_c16), (ones64_sb[:], ones64f)]:
                s_ldc.dma(e.dma_start(out=dst, in_=src))
            s_ldc.dma(e.dma_start(out=e0_sb[:], in_=ek[0:1, 0:1].partition_broadcast(128)))
            s_ldc.dma(e.dma_start(out=e1_sb[:], in_=ek[0:1, 1:2].partition_broadcast(128)))
            for si in range(2):
                s_ldc.dma(e.dma_start(out=mlo_sb[:, si, :], in_=mlo[si]))
                s_ldc.dma(e.dma_start(out=mhi_sb[:, si, :], in_=mhi[si]))
        S("sync", init_loads)
        LDC = s_ldc.n

        w3 = [(wqT, wq_sb), (wkT, wk_sb), (wvT, wv_sb)]

        def wload(e):
            for wi, (wsrc, _) in enumerate(w3):
                ws = wsrc.rearrange("(c p) o -> p c o", p=128)
                for c in range(NCH):
                    idx = wi * NCH + c
                    if idx >= 2:
                        e.wait_ge(s_w.h, idx - 1)
                    s_wl[idx % 2].dma(e.dma_start(out=wtmp[idx % 2][:], in_=ws[:, c, :]))
        S("scalar", wload)

        def wcast(e):
            for wi, (_, wdst) in enumerate(w3):
                for c in range(NCH):
                    idx = wi * NCH + c
                    e.wait_ge(s_wl[idx % 2].h, (idx // 2 + 1) * 16)
                    ins = e.tensor_copy(out=wdst[:, c, :], in_=wtmp[idx % 2][:])
                    s_w.inc(ins)
        S("vector", wcast)

        def bias_prep(e):
            e.wait_ge(s_ldc.h, LDC)
            e.tensor_scalar(out=bqs_sb[:], in0=bqs_sb[:], scalar1=0.125, scalar2=None, op0=ALU.mult)
            e.tensor_copy(out=bvb_sb[:], in_=bvf_sb[:])
            ins = None
            for sl in range(4):
                for hf in range(2):
                    ins = e.tensor_copy(out=v_ring[:, sl, hf, :, 64], in_=onec16_sb[:])
            s_w.inc(ins)
        S("vector", bias_prep)
        W_DONE = s_w.n

        with nc.psum_tensor(pfx + "ps_s1", [1, W], F32) as ps_s1, nc.psum_tensor(pfx + "ps_s2", [1, W], F32) as ps_s2:
            for j in range(NC_LOC):
                jb = j % 2

                def ld(e, j=j, jb=jb):
                    if j >= 2:
                        e.wait_ge(s_st.h, (j - 1) * 2)
                    s_xt[jb].dma(e.dma_start(out=xt[jb][:], in_=xh3[:, :, j * W:(j + 1) * W]))
                S("sync", ld)
                XT_J = s_xt[jb].n

                def casts(e, j=j, jb=jb, XT_J=XT_J):
                    e.wait_ge(s_xt[jb].h, XT_J)
                    if j >= 1:
                        e.wait_ge(s_st.h, j * 2)
                    ins = e.activation(out=xsq[:], in_=xt[jb][:], func=AF.Square)
                    s_cst.inc(ins)
                S("scalar", casts)

                def castb(e, j=j, jb=jb, XT_J=XT_J):
                    e.wait_ge(s_xt[jb].h, XT_J)
                    if j >= 1:
                        e.wait_ge(s_st.h, j * 2)
                    if j == 0:
                        e.wait_ge(s_w.h, W_DONE)
                    ins = e.tensor_copy(out=xbq[:], in_=xt[jb][:])
                    s_csv.inc(ins)
                S("vector", castb)

                def stats(e, j=j, CST=s_cst.n, CSV=s_csv.n):
                    e.wait_ge(s_cst.h, CST)
                    e.wait_ge(s_csv.h, CSV)
                    if j >= 1:
                        e.wait_ge(s_row.h, (j - 1) * 3 + 1)
                    for c in range(NCH):
                        e.matmul(ps_s1[:], onec_sb[:], xbq[:, c, :], start=(c == 0), stop=(c == NCH - 1))
                    for c in range(NCH):
                        ins = e.matmul(ps_s2[:], onec_sb[:], xsq[:, c, :], start=(c == 0), stop=(c == NCH - 1))
                    s_st.inc(ins, 2)
                S("tensor", stats)

                def rowmath(e, ST=s_st.n, RALL=s_row.n, base=s_tk.n):
                    e.wait_ge(s_st.h, ST)
                    if RALL:
                        e.wait_ge(s_row.h, RALL)
                    k = base
                    ins = e.tensor_scalar(out=st_row[0][:], in0=ps_s1[:], scalar1=1.0 / E, scalar2=None, op0=ALU.mult)
                    ins.then_inc(s_tk.h, 1); k += 1
                    ins = e.tensor_scalar(out=st_row[1][:], in0=ps_s2[:], scalar1=1.0 / E, scalar2=None, op0=ALU.mult)
                    ins.then_inc(s_tk.h, 1); k += 1
                    e.wait_ge(s_tk.h, k)
                    ins = e.tensor_tensor(out=st_row[2][:], in0=st_row[0][:], in1=st_row[0][:], op=ALU.mult)
                    ins.then_inc(s_tk.h, 1); k += 1
                    e.wait_ge(s_tk.h, k)
                    ins = e.tensor_tensor(out=st_row[3][:], in0=st_row[1][:], in1=st_row[2][:], op=ALU.subtract)
                    ins.then_inc(s_tk.h, 1); k += 1
                    e.wait_ge(s_tk.h, k)
                    ins = e.tensor_scalar(out=st_row[3][:], in0=st_row[3][:], scalar1=1e-5, scalar2=None, op0=ALU.add)
                    ins.then_inc(s_tk.h, 1); k += 1
                    e.wait_ge(s_tk.h, k)
                    ins = e.reciprocal(out=st_row[4][:], in_=st_row[3][:])
                    s_row.inc(ins)
                    s_tk.n = k
                S("vector", rowmath)

                def rsqrt(e, RM=s_row.n, RW=s_rw.n):
                    e.wait_ge(s_row.h, RM)
                    if RW:
                        e.wait_ge(s_rw.h, RW)
                    ins = e.activation(out=st_row[5][:], in_=st_row[4][:], func=AF.Sqrt)
                    s_row.inc(ins)
                S("scalar", rsqrt)

                def mrmul(e, RS=s_row.n):
                    e.wait_ge(s_row.h, RS)
                    if s_rw.n > 0:
                        e.wait_ge(s_rw.h, s_rw.n)
                    ins = e.tensor_tensor(out=mr_row[:], in0=st_row[0][:], in1=st_row[5][:], op=ALU.mult)
                    s_row.inc(ins)
                S("vector", mrmul)

                def rowwrite(e, j=j, MR=s_row.n):
                    e.wait_ge(s_row.h, MR)
                    s_rw.dma(e.dma_start(out=lnrows[j, 0:1, :], in_=mr_row[:]))
                    s_rw.dma(e.dma_start(out=lnrows[j, 1:2, :], in_=st_row[5][:]))
                S("gpsimd", rowwrite)

        A1_RW = s_rw.n
        A1_ST = s_st.n

        with nc.psum_tensor(pfx + "ps_sc0", [128, 1024], F32) as ps_sc0, \
             nc.psum_tensor(pfx + "ps_sc1", [128, 1024], F32) as ps_sc1, \
             nc.psum_tensor(pfx + "ps_p0", [128, 512], F32) as ps_p0, \
             nc.psum_tensor(pfx + "ps_p1", [128, 512], F32) as ps_p1, \
             nc.psum_tensor(pfx + "ps_o0", [128, 512], F32) as ps_o0, \
             nc.psum_tensor(pfx + "ps_o1", [128, 512], F32) as ps_o1:
            ps_sc = [ps_sc0, ps_sc1]
            ps_p = [ps_p0, ps_p1]
            ps_o = [ps_o0, ps_o1]
            pp_ev = []
            pp_i = [0]
            norm_done = {}
            exp_at = [0, 0]
            av_at = [0, 0]
            t64_at = [0, 0]
            ev_at_zrep = [0, 0]

            def qkv_group(mm_emit, ev_emit, pre_mm=None):
                b = pp_i[0] % 2

                def mm(e, b=b, prev=(pp_ev[-2] if len(pp_ev) >= 2 else None)):
                    if pre_mm:
                        pre_mm(e)
                    if prev is not None:
                        e.wait_ge(s_ev.h, prev)
                    ins = mm_emit(e, ps_p[b])
                    s_pkv.inc(ins)
                S("tensor", mm)
                PKV = s_pkv.n

                def ev(e, b=b, PKV=PKV):
                    e.wait_ge(s_pkv.h, PKV)
                    ins = ev_emit(e, ps_p[b])
                    s_ev.inc(ins)
                S("scalar", ev)
                pp_ev.append(s_ev.n)
                pp_i[0] += 1
                return s_ev.n

            for j in range(NC_LOC):
                jb = j % 2
                sl = j % 4
                c_att = j - 2

                def ld(e, j=j, jb=jb, c_att=c_att):
                    if j < 2:
                        e.wait_ge(s_st.h, A1_ST)
                    else:
                        e.wait_ge(s_norm.h, norm_done[j - 2])
                    s_xt[jb].dma(e.dma_start(out=xt[jb][:], in_=xh3[:, :, j * W:(j + 1) * W]))
                    e.wait_ge(s_rw.h, A1_RW)
                    s_lr[jb].dma(e.dma_start(out=rstd_rep[jb][:], in_=lnrows[j, 1:2, :].partition_broadcast(128)))
                    s_lr[jb].dma(e.dma_start(out=mr_rep[jb][:], in_=lnrows[j, 0:1, :].partition_broadcast(128)))
                    if 0 <= c_att < NC_OWN:
                        if c_att >= 2:
                            e.wait_ge(s_st1.h, (c_att - 1) * 16)
                        s_xr[c_att % 2].dma(e.dma_start(out=xres[c_att % 2][:], in_=xh3[:, :, (c_att + 1) * W:(c_att + 2) * W]))
                S("sync", ld)
                XT_J = s_xt[jb].n
                LR_J = s_lr[jb].n
                XR_C = s_xr[c_att % 2].n if 0 <= c_att < NC_OWN else 0

                def norm(e, jb=jb, XT_J=XT_J, LR_J=LR_J):
                    e.wait_ge(s_xt[jb].h, XT_J)
                    e.wait_ge(s_lr[jb].h, LR_J)
                    e.wait_ge(s_pkv.h, s_pkv.n)
                    e.tensor_tensor(out=tnorm[:], in0=xt[jb][:], in1=rstd_rep[jb][:].unsqueeze(1).broadcast_to([128, NCH, W]), op=ALU.mult)
                    ins = e.tensor_tensor(out=xn[jb][:], in0=tnorm[:], in1=mr_rep[jb][:].unsqueeze(1).broadcast_to([128, NCH, W]), op=ALU.subtract)
                    s_norm.inc(ins, 2)
                S("vector", norm)
                NORM_J = s_norm.n
                norm_done[j] = s_norm.n

                def ring_guard(e, j=j):
                    cprev = j - 4
                    if 0 <= cprev < NC_OWN:
                        e.wait_ge(s_av.h, (cprev + 1) * H * 6)

                for dc in range(NCH):
                    def kmm(e, ps, jb=jb, dc=dc):
                        for a in range(NCH):
                            ins = e.matmul(ps[:, 0:W], wk_sb[:, a, dc * 128:(dc + 1) * 128], xn[jb][:, a, :], start=(a == 0), stop=(a == NCH - 1))
                        return ins

                    def kev(e, ps, sl=sl, dc=dc, j=j):
                        ring_guard(e, j)
                        return e.activation(out=k_ring[:, sl, dc, :], in_=ps[:, 0:W], func=AF.Identity, bias=bk_sb[:, dc:dc + 1], scale=1.0)

                    qkv_group(kmm, kev, pre_mm=(lambda e, NORM_J=NORM_J: e.wait_ge(s_norm.h, NORM_J)))

                for hf in range(2):
                    for es in range(2):
                        def vmm(e, ps, jb=jb, hf=hf, es=es):
                            for a in range(NCH):
                                e.matmul(ps[:], xn[jb][:, a, hf * 128:(hf + 1) * 128], wv_sb[:, a, es * 512:(es + 1) * 512], start=(a == 0), stop=False)
                            return e.matmul(ps[:], oner_sb[:], bvb_sb[:, es * 512:(es + 1) * 512], start=False, stop=True)

                        def vev(e, ps, sl=sl, hf=hf, es=es, j=j):
                            ring_guard(e, j)
                            dst = v_ring[:, sl, hf, es * 8:(es + 1) * 8, 0:64]
                            return e.activation(out=dst, in_=ps[:].rearrange("p (h o) -> p h o", o=64), func=AF.Copy)

                        qkv_group(vmm, vev, pre_mm=(lambda e, NORM_J=NORM_J: e.wait_ge(s_norm.h, NORM_J)))

                EDGE_N = None
                if j == 0 or j == NC_LOC - 1:
                    ecol = e0_sb if j == 0 else e1_sb
                    EV_J = s_ev.n

                    def edge(e, sl=sl, ecol=ecol, EV_J=EV_J):
                        e.wait_ge(s_ev.h, EV_J)
                        e.tensor_scalar(out=k_ring[:, sl, :, :], in0=k_ring[:, sl, :, :], scalar1=ecol[:], scalar2=None, op0=ALU.mult)
                        ins = e.tensor_scalar(out=v_ring[:, sl, :, :, 0:64], in0=v_ring[:, sl, :, :, 0:64], scalar1=ecol[:], scalar2=None, op0=ALU.mult)
                        s_norm.inc(ins, 2)
                    S("vector", edge)
                    EDGE_N = s_norm.n

                cq = j - 1
                if 0 <= cq < NC_OWN:
                    for dc in range(NCH):
                        def qmm(e, ps, jq=jb, dc=dc):
                            for a in range(NCH):
                                ins = e.matmul(ps[:, 0:W], wq_sb[:, a, dc * 128:(dc + 1) * 128], xn[jq][:, a, :], start=(a == 0), stop=(a == NCH - 1))
                            return ins

                        def qev(e, ps, cq=cq, dc=dc):
                            if cq >= 2:
                                e.wait_ge(s_sc.h, (cq - 1) * H * 6)
                            return e.activation(out=qt[cq % 2][:, dc, :], in_=ps[:, 0:W], func=AF.Identity, bias=bqs_sb[:, dc:dc + 1], scale=0.125)

                        qkv_group(qmm, qev)
                Q_EV = s_ev.n

                if 0 <= c_att < NC_OWN:
                    c = c_att
                    cb = c % 2
                    for h in range(H):
                        hb = h % 2
                        hc = h // 2
                        hr = 64 * (h % 2)
                        for g in range(2):
                            def smm(e, c=c, cb=cb, h=h, hc=hc, hr=hr, g=g, Q_EV=Q_EV, EDGE_N=EDGE_N, EXPG=exp_at[g]):
                                e.wait_ge(s_ev.h, Q_EV)
                                if EDGE_N is not None:
                                    e.wait_ge(s_norm.h, EDGE_N)
                                if EXPG:
                                    e.wait_ge(s_exp.h, EXPG)
                                for si in range(3):
                                    s = 3 * g + si
                                    slj = (c + s // 2) % 4
                                    half = s % 2
                                    ins = e.matmul(
                                        ps_sc[g][:, si * W:(si + 1) * W],
                                        k_ring[hr:hr + 64, slj, hc, half * 128:(half + 1) * 128],
                                        qt[cb][hr:hr + 64, hc, :],
                                        start=True, stop=True)
                                s_sc.inc(ins, 3)
                            S("tensor", smm)
                            SC_H = s_sc.n

                            need_mask = (c == 0 and g == 0) or (c == NC_OWN - 1 and g == 1)
                            if need_mask:
                                def mask(e, c=c, g=g, SC_H=SC_H):
                                    e.wait_ge(s_sc.h, SC_H)
                                    if c == 0:
                                        for si in range(2):
                                            ins = e.tensor_tensor(out=ps_sc[0][:, si * W:(si + 1) * W], in0=ps_sc[0][:, si * W:(si + 1) * W], in1=mlo_sb[:, si, :], op=ALU.add)
                                    else:
                                        for si in range(2):
                                            ins = e.tensor_tensor(out=ps_sc[1][:, (1 + si) * W:(2 + si) * W], in0=ps_sc[1][:, (1 + si) * W:(2 + si) * W], in1=mhi_sb[:, si, :], op=ALU.add)
                                    s_msk.inc(ins, 2)
                                S("vector", mask)
                            MSK_H = s_msk.n

                            def expf(e, g=g, SC_H=SC_H, MSK_H=MSK_H, need_mask=need_mask, AVG=av_at[g]):
                                e.wait_ge(s_sc.h, SC_H)
                                if need_mask:
                                    e.wait_ge(s_msk.h, MSK_H)
                                if AVG:
                                    e.wait_ge(s_av.h, AVG)
                                ins = e.activation(out=probs[g][:], in_=ps_sc[g][:, 0:3 * W], func=AF.Exp)
                                s_exp.inc(ins)
                            S("scalar", expf)
                            exp_at[g] = s_exp.n

                            def avmm(e, c=c, h=h, hb=hb, g=g, EXP_G=s_exp.n, T64B=t64_at[hb]):
                                e.wait_ge(s_exp.h, EXP_G)
                                if g == 0 and T64B:
                                    e.wait_ge(s_t64.h, T64B)
                                for si in range(3):
                                    s = 3 * g + si
                                    slj = (c + s // 2) % 4
                                    half = s % 2
                                    ins = e.matmul(
                                        ps_o[hb][0:65, 0:W],
                                        v_ring[:, slj, half, h, :],
                                        probs[g][:, si * W:(si + 1) * W],
                                        start=(s == 0), stop=(s == 5))
                                s_av.inc(ins, 3)
                            S("tensor", avmm)
                            av_at[g] = s_av.n
                        AV_H = s_av.n

                        def recip(e, hb=hb, AV_H=AV_H, EVZ=ev_at_zrep[hb]):
                            e.wait_ge(s_av.h, AV_H)
                            if EVZ:
                                e.wait_ge(s_ev.h, EVZ)
                            ins = e.reciprocal(out=rz[hb][:], in_=ps_o[hb][64:65, 0:W])
                            s_rz.inc(ins)
                        S("vector", recip)
                        RZ_H = s_rz.n

                        def zmm(e, ps, hb=hb, RZ_H=RZ_H):
                            e.wait_ge(s_rz.h, RZ_H)
                            return e.matmul(ps[0:64, 0:W], ones64_sb[:], rz[hb][:], start=True, stop=True)

                        def zev(e, ps, hb=hb, hr=hr, T64B=t64_at[hb]):
                            if T64B:
                                e.wait_ge(s_t64.h, T64B)
                            return e.activation(out=zrep[hb][hr:hr + 64, :], in_=ps[0:64, 0:W], func=AF.Copy)

                        ev_at_zrep[hb] = qkv_group(zmm, zev)
                        ZEV_H = s_ev.n

                        def t64mul(e, hb=hb, hr=hr, ZEV_H=ZEV_H):
                            e.wait_ge(s_ev.h, ZEV_H)
                            ins = e.tensor_tensor(out=t64[hb][hr:hr + 64, :], in0=ps_o[hb][0:64, 0:W], in1=zrep[hb][hr:hr + 64, :], op=ALU.mult)
                            s_t64.inc(ins)
                        S("vector", t64mul)
                        t64_at[hb] = s_t64.n
                        T64_H = s_t64.n

                        def xadd(e, c=c, hb=hb, hc=hc, hr=hr, T64_H=T64_H, XR_C=XR_C):
                            e.wait_ge(s_t64.h, T64_H)
                            e.wait_ge(s_xr[c % 2].h, XR_C)
                            ins = e.tensor_tensor(out=x1t[c % 2][hr:hr + 64, hc, :], in0=xres[c % 2][hr:hr + 64, hc, :], in1=t64[hb][hr:hr + 64, :], op=ALU.add)
                            s_x1.inc(ins)
                        S("gpsimd", xadd)

                    X1_C = s_x1.n

                    def store(e, c=c, X1_C=X1_C):
                        e.wait_ge(s_x1.h, X1_C)
                        s_st1.dma(e.dma_start(out=x1o3[:, :, c * W:(c + 1) * W], in_=x1t[c % 2][:]))
                    S("gpsimd", store)

            def fin(e):
                e.wait_ge(s_st1.h, s_st1.n)
            S("sync", fin)

    return phase_terminals()



def l0attn_inputs(xfull_fm_b, inp, th):
    import ml_dtypes
    x = xfull_fm_b
    cols = np.zeros((E, NC_LOC * W), np.float32)
    lo = th * TOK - W
    for j in range(NC_LOC):
        g0 = lo + j * W
        if 0 <= g0 and g0 + W <= 8192:
            cols[:, j * W:(j + 1) * W] = x[:, g0:g0 + W]
    m = {
        "xh": cols,
        "wqT": np.ascontiguousarray(inp["L0_attn_Wq"].T),
        "wkT": np.ascontiguousarray(inp["L0_attn_Wk"].T),
        "wvT": np.ascontiguousarray(inp["L0_attn_Wv"].T),
        "bq_r": np.ascontiguousarray(inp["L0_attn_bq"].reshape(NCH, 128).T),
        "bk_r": np.ascontiguousarray(inp["L0_attn_bk"].reshape(NCH, 128).T),
        "bv_row": inp["L0_attn_bv"].reshape(1, E).copy(),
        "ones_col": np.ones((128, 1), ml_dtypes.bfloat16).view(np.uint16),
        "ones_row": np.ones((1, 128), ml_dtypes.bfloat16).view(np.uint16),
        "ones_c16": np.ones((128, 16), ml_dtypes.bfloat16).view(np.uint16),
        "ones64f": np.ones((1, 64), np.float32),
        "ek": np.array([[0.0 if th == 0 else 1.0, 1.0 if th == 0 else 0.0]], np.float32),
    }
    mlo = np.zeros((2, 128, W), np.float32)
    mhi = np.zeros((2, 128, W), np.float32)
    rr = np.arange(128)[:, None]
    ii = np.arange(W)[None, :]
    if th == 0:
        mlo[0] = np.where(rr + ii < 256, NEG, 0.0)
        mlo[1] = np.where(rr + ii < 128, NEG, 0.0)
    if th == 1:
        mhi[0] = np.where(rr + ii >= 255, NEG, 0.0)
        mhi[1] = np.where(rr + ii >= 127, NEG, 0.0)
    m["mlo"] = mlo
    m["mhi"] = mhi
    return m



PHASE_SEMS = []


class Env:
    def __init__(self, nc, ctx, block, semctx=None, pfx=""):
        self.nc, self.ctx, self.block = nc, ctx, block
        self.semctx = semctx if semctx is not None else ctx
        self.pfx = pfx

    def sb(self, name, shape, dt):
        return self.ctx.enter_context(self.nc.sbuf_tensor(self.pfx + name, shape, dt))

    def sem(self, name):
        s = Sem(self.semctx, self.nc, self.pfx + name)
        PHASE_SEMS.append(s)
        return s

    def S(self, eng, fn):
        getattr(self.block, eng)(fn)


def emit_gate(block, gate):
    if not gate:
        return
    for _eng in ("sync", "scalar", "vector", "tensor", "gpsimd"):
        def _g(e, gate=gate):
            for (s, v) in gate:
                e.wait_ge(s, v)
        getattr(block, _eng)(_g)


def phase_terminals():
    terms = [(s.h, s.n) for s in PHASE_SEMS if s.n > 0]
    PHASE_SEMS.clear()
    return terms


def emit_ln_stats(env, src3, ntiles, TW, lnrows, nch=NCH, pre_wait=None):
    nc, S, sem = env.nc, env.S, env.sem
    lctx = ExitStack()
    def sb(name, shape, dt):
        return lctx.enter_context(nc.sbuf_tensor(env.pfx + name, shape, dt))
    _lnxt = sb("lnxt", [128, nch, TW], F32)
    xt = [_lnxt, _lnxt]
    xsq = sb("lnxsq", [128, nch, TW], BF16)
    xbq = sb("lnxbq", [128, nch, TW], BF16)
    onec = env.onec_sb
    _rows = [sb(f"lnrow{i}", [1, TW], F32) for i in range(4)]
    st_row = [_rows[0], _rows[1], _rows[2], _rows[3], _rows[3], _rows[2]]
    mr_row = sb("lnmr", [1, TW], F32)
    s_xt = [sem(f"lns_xt{i}") for i in range(2)]
    s_cst = sem("lns_cst")
    s_csv = sem("lns_csv")
    s_st = sem("lns_st")
    s_row = sem("lns_row")
    s_rw = sem("lns_rw")
    s_tk = sem("lns_tk")
    with nc.psum_tensor(env.pfx + "lnps1", [1, TW], F32) as ps1, nc.psum_tensor(env.pfx + "lnps2", [1, TW], F32) as ps2:
        for j in range(ntiles):
            jb = j % 2

            def ld(e, j=j, jb=jb):
                if j == 0 and pre_wait is not None:
                    pre_wait(e)
                if j >= 1:
                    e.wait_ge(s_st.h, j * 2)
                s_xt[jb].dma(e.dma_start(out=xt[jb][:], in_=src3[:, :, j * TW:(j + 1) * TW]))
            S("sync", ld)
            XT_J = s_xt[jb].n

            def casts(e, j=j, jb=jb, XT_J=XT_J):
                e.wait_ge(s_xt[jb].h, XT_J)
                if j >= 1:
                    e.wait_ge(s_st.h, j * 2)
                ins = e.activation(out=xsq[:], in_=xt[jb][:], func=AF.Square)
                s_cst.inc(ins)
            S("scalar", casts)

            def castb(e, j=j, jb=jb, XT_J=XT_J):
                e.wait_ge(s_xt[jb].h, XT_J)
                if j >= 1:
                    e.wait_ge(s_st.h, j * 2)
                if j == 0 and pre_wait is not None:
                    pre_wait(e)
                ins = e.tensor_copy(out=xbq[:], in_=xt[jb][:])
                s_csv.inc(ins)
            S("vector", castb)

            def stats(e, j=j, CST=s_cst.n, CSV=s_csv.n):
                e.wait_ge(s_cst.h, CST)
                e.wait_ge(s_csv.h, CSV)
                if j >= 1:
                    e.wait_ge(s_row.h, (j - 1) * 3 + 1)
                for c in range(nch):
                    e.matmul(ps1[:], onec[:], xbq[:, c, :], start=(c == 0), stop=(c == nch - 1))
                for c in range(nch):
                    ins = e.matmul(ps2[:], onec[:], xsq[:, c, :], start=(c == 0), stop=(c == nch - 1))
                s_st.inc(ins, 2)
            S("tensor", stats)

            def rowmath(e, ST=s_st.n, RALL=s_row.n, RW=s_rw.n, base=s_tk.n):
                e.wait_ge(s_st.h, ST)
                if RALL:
                    e.wait_ge(s_row.h, RALL)
                if RW:
                    e.wait_ge(s_rw.h, RW)
                k = base
                ins = e.tensor_scalar(out=st_row[0][:], in0=ps1[:], scalar1=1.0 / (128 * nch), scalar2=None, op0=ALU.mult)
                ins.then_inc(s_tk.h, 1); k += 1
                ins = e.tensor_scalar(out=st_row[1][:], in0=ps2[:], scalar1=1.0 / (128 * nch), scalar2=None, op0=ALU.mult)
                ins.then_inc(s_tk.h, 1); k += 1
                e.wait_ge(s_tk.h, k)
                ins = e.tensor_tensor(out=st_row[2][:], in0=st_row[0][:], in1=st_row[0][:], op=ALU.mult)
                ins.then_inc(s_tk.h, 1); k += 1
                e.wait_ge(s_tk.h, k)
                ins = e.tensor_tensor(out=st_row[3][:], in0=st_row[1][:], in1=st_row[2][:], op=ALU.subtract)
                ins.then_inc(s_tk.h, 1); k += 1
                e.wait_ge(s_tk.h, k)
                ins = e.tensor_scalar(out=st_row[3][:], in0=st_row[3][:], scalar1=1e-5, scalar2=None, op0=ALU.add)
                ins.then_inc(s_tk.h, 1); k += 1
                e.wait_ge(s_tk.h, k)
                ins = e.reciprocal(out=st_row[4][:], in_=st_row[3][:])
                s_row.inc(ins)
                s_tk.n = k
            S("vector", rowmath)

            def rsqrt(e, RM=s_row.n, RW=s_rw.n):
                e.wait_ge(s_row.h, RM)
                if RW:
                    e.wait_ge(s_rw.h, RW)
                ins = e.activation(out=st_row[5][:], in_=st_row[4][:], func=AF.Sqrt)
                s_row.inc(ins)
            S("scalar", rsqrt)

            def mrmul(e, RS=s_row.n, RW=s_rw.n):
                e.wait_ge(s_row.h, RS)
                if RW:
                    e.wait_ge(s_rw.h, RW)
                ins = e.tensor_tensor(out=mr_row[:], in0=st_row[0][:], in1=st_row[5][:], op=ALU.mult)
                s_row.inc(ins)
            S("vector", mrmul)

            def rowwrite(e, j=j, MR=s_row.n):
                e.wait_ge(s_row.h, MR)
                s_rw.dma(e.dma_start(out=lnrows[j, 0:1, :], in_=mr_row[:]))
                s_rw.dma(e.dma_start(out=lnrows[j, 1:2, :], in_=st_row[5][:]))
            S("gpsimd", rowwrite)
    lctx.close()
    return s_rw, s_st



def emit_l0ffn(nc, block, semctx, pfx, io, gate):
    x1 = io["x1"]
    u1 = io["u1"]
    v1 = io["v1"]
    u2 = io["u2"]
    v2 = io["v2"]
    b1_r = io["b1_r"]
    b2_r = io["b2_r"]
    ones_col = io["ones_col"]
    x2_out = io["x2"]
    lnrows = nc.dram_tensor(pfx + "lnrows", [8, 2, 512], F32, kind="Internal").ap()

    x13 = x1.rearrange("(c p) t -> p c t", p=128)
    x2o3 = x2_out.rearrange("(c p) t -> p c t", p=128)
    TW = 512
    NT = TOK // TW

    with ExitStack() as ctx:
        emit_gate(block, gate)
        env = Env(nc, ctx, block, semctx, pfx)
        sb, sem, S = env.sb, env.sem, env.S

        u1_sb = sb("u1_sb", [128, NCH, 256], BF16)
        v1_sb = sb("v1_sb", [128, 2, 4096], BF16)
        u2_sb = sb("u2_sb", [128, 32, 256], BF16)
        v2_sb = sb("v2_sb", [128, 2, E], BF16)
        b1_sb = sb("b1_sb", [128, 32], F32)
        b2_sb = sb("b2_sb", [128, NCH], F32)
        env.onec_sb = sb("onec_sb", [128, 1], BF16)
        xn = [sb(f"xn{i}", [128, NCH, TW], BF16) for i in range(2)]
        xt = [sb(f"xt{i}", [128, NCH, TW], F32) for i in range(2)]
        rstd_rep = [sb(f"rstd_rep{i}", [128, TW], F32) for i in range(2)]
        mr_rep = [sb(f"mr_rep{i}", [128, TW], F32) for i in range(2)]
        tnorm = sb("tnorm", [128, NCH, TW], F32)
        _pt = sb("pt", [128, 2, TW], BF16)
        pt = [_pt, _pt]
        hbt0 = sb("hbt0", [128, 32, TW], BF16)
        hbt = [hbt0, hbt0]
        _q2t = sb("q2t", [128, 2, TW], BF16)
        q2t = [_q2t, _q2t]
        _x2t = sb("x2t", [128, NCH, TW], F32)
        x2t = [_x2t, _x2t]

        s_ldc = sem("s_ldc")
        s_wl = [sem(f"s_wl{i}") for i in range(2)]
        s_w = sem("s_w")
        s_xt2 = [sem(f"s_xt2{i}") for i in range(2)]
        s_lr = [sem(f"s_lr{i}") for i in range(2)]
        s_norm = sem("s_norm")
        s_mm = sem("s_mm")
        s_ev = sem("s_ev")
        s_x2 = sem("s_x2")
        s_st2 = sem("s_st2")

        def init_loads(e):
            for dst, src in [(b1_sb[:], b1_r), (b2_sb[:], b2_r), (env.onec_sb[:], ones_col)]:
                s_ldc.dma(e.dma_start(out=dst, in_=src))
        S("sync", init_loads)
        LDC = s_ldc.n

        wspecs = [(u1, u1_sb, 8, 256), (v1, v1_sb, 2, 4096), (u2, u2_sb, 32, 256), (v2, v2_sb, 2, E)]
        wctx = ExitStack()
        wtmp = [wctx.enter_context(nc.sbuf_tensor(pfx + f"wtmp{i}", [128, 4096], F32)) for i in range(2)]

        def wload(e):
            idx = 0
            for wsrc, wdst, nchk, cols in wspecs:
                ws = wsrc.rearrange("(c p) o -> p c o", p=128)
                for c in range(nchk):
                    if idx >= 2:
                        e.wait_ge(s_w.h, idx - 1)
                    s_wl[idx % 2].dma(e.dma_start(out=wtmp[idx % 2][:, 0:cols], in_=ws[:, c, :]))
                    idx += 1
        S("scalar", wload)

        def wcast(e):
            idx = 0
            for wsrc, wdst, nchk, cols in wspecs:
                for c in range(nchk):
                    e.wait_ge(s_wl[idx % 2].h, (idx // 2 + 1) * 16)
                    ins = e.tensor_copy(out=wdst[:, c, :], in_=wtmp[idx % 2][:, 0:cols])
                    s_w.inc(ins)
                    idx += 1
        S("vector", wcast)
        W_DONE = s_w.n
        wctx.close()

        s_rw, s_stA = emit_ln_stats(env, x13, NT, TW, lnrows,
                                    pre_wait=lambda e: e.wait_ge(s_w.h, W_DONE))
        A_RW = s_rw.n
        A_ST = s_stA.n
        norm_done = {}

        with nc.psum_tensor(pfx + "ps_a", [128, 512], F32) as ps_a, \
             nc.psum_tensor(pfx + "ps_b", [128, 512], F32) as ps_b, \
             nc.psum_tensor(pfx + "ps_q0", [128, 512], F32) as ps_q0, \
             nc.psum_tensor(pfx + "ps_q1", [128, 512], F32) as ps_q1:
            ps_pp = [ps_a, ps_b]
            ps_q2 = [ps_q0, ps_q1]
            pp_ev = []
            pp_i = [0]
            q2ev_done = [0]

            def group(mm_emit, ev_emit, pre_mm=None):
                b = pp_i[0] % 2

                def mm(e, b=b, prev=(pp_ev[-2] if len(pp_ev) >= 2 else None)):
                    if pre_mm:
                        pre_mm(e)
                    if prev is not None:
                        e.wait_ge(s_ev.h, prev)
                    ins = mm_emit(e, ps_pp[b])
                    s_mm.inc(ins)
                S("tensor", mm)
                MM = s_mm.n

                def ev(e, b=b, MM=MM):
                    e.wait_ge(s_mm.h, MM)
                    ins = ev_emit(e, ps_pp[b])
                    s_ev.inc(ins)
                S("scalar", ev)
                pp_ev.append(s_ev.n)
                pp_i[0] += 1
                return s_ev.n

            for j in range(NT):
                jb = j % 2

                def ld(e, j=j, jb=jb):
                    if j < 2:
                        e.wait_ge(s_stA.h, A_ST)
                    else:
                        e.wait_ge(s_x2.h, j - 1)
                    s_xt2[jb].dma(e.dma_start(out=xt[jb][:], in_=x13[:, :, j * TW:(j + 1) * TW]))
                    e.wait_ge(s_rw.h, A_RW)
                    s_lr[jb].dma(e.dma_start(out=rstd_rep[jb][:], in_=lnrows[j, 1:2, :].partition_broadcast(128)))
                    s_lr[jb].dma(e.dma_start(out=mr_rep[jb][:], in_=lnrows[j, 0:1, :].partition_broadcast(128)))
                S("sync", ld)
                XT_J = s_xt2[jb].n
                LR_J = s_lr[jb].n

                def norm(e, j=j, jb=jb, XT_J=XT_J, LR_J=LR_J):
                    e.wait_ge(s_xt2[jb].h, XT_J)
                    e.wait_ge(s_lr[jb].h, LR_J)
                    e.tensor_tensor(out=tnorm[:], in0=xt[jb][:], in1=rstd_rep[jb][:].unsqueeze(1).broadcast_to([128, NCH, TW]), op=ALU.mult)
                    ins = e.tensor_tensor(out=xn[jb][:], in0=tnorm[:], in1=mr_rep[jb][:].unsqueeze(1).broadcast_to([128, NCH, TW]), op=ALU.subtract)
                    s_norm.inc(ins, 2)
                S("vector", norm)
                norm_done[j] = s_norm.n
                NORM_J = s_norm.n

                for rc in range(2):
                    def pmm(e, ps, jb=jb, rc=rc):
                        for a in range(NCH):
                            ins = e.matmul(ps[:], u1_sb[:, a, rc * 128:(rc + 1) * 128], xn[jb][:, a, :], start=(a == 0), stop=(a == NCH - 1))
                        return ins

                    def pev(e, ps, jb=jb, rc=rc):
                        return e.activation(out=pt[jb][:, rc, :], in_=ps[:], func=AF.Copy)

                    group(pmm, pev, pre_mm=(lambda e, NORM_J=NORM_J: e.wait_ge(s_norm.h, NORM_J)))
                PEV = s_ev.n

                for fc in range(32):
                    def hmm(e, ps, jb=jb, fc=fc, PEV=PEV):
                        e.wait_ge(s_ev.h, PEV)
                        for rc in range(2):
                            ins = e.matmul(ps[:], v1_sb[:, rc, fc * 128:(fc + 1) * 128], pt[jb][:, rc, :], start=(rc == 0), stop=(rc == 1))
                        return ins

                    def hev(e, ps, jb=jb, fc=fc):
                        return e.activation(out=hbt[jb][:, fc, :], in_=ps[:], func=AF.Gelu_apprx_tanh, bias=b1_sb[:, fc:fc + 1], scale=1.0)

                    HEV = group(hmm, hev)

                    def q2mm(e, jb=jb, fc=fc, HEV=HEV, Q2P=q2ev_done[0]):
                        e.wait_ge(s_ev.h, HEV)
                        if fc == 0 and Q2P:
                            e.wait_ge(s_ev.h, Q2P)
                        for rc in range(2):
                            ins = e.matmul(ps_q2[rc][:], u2_sb[:, fc, rc * 128:(rc + 1) * 128], hbt[jb][:, fc, :], start=(fc == 0), stop=(fc == 31))
                        s_mm.inc(ins)
                    S("tensor", q2mm)

                Q2MM = s_mm.n

                def q2ev(e, jb=jb, Q2MM=Q2MM):
                    e.wait_ge(s_mm.h, Q2MM)
                    for rc in range(2):
                        ins = e.activation(out=q2t[jb][:, rc, :], in_=ps_q2[rc][:], func=AF.Copy)
                    s_ev.inc(ins, 1)
                S("scalar", q2ev)
                Q2EV = s_ev.n
                q2ev_done[0] = Q2EV
                pp_ev.append(s_ev.n)

                for dc in range(NCH):
                    def omm(e, ps, jb=jb, dc=dc, Q2EV=Q2EV):
                        e.wait_ge(s_ev.h, Q2EV)
                        for rc in range(2):
                            ins = e.matmul(ps[:], v2_sb[:, rc, dc * 128:(dc + 1) * 128], q2t[jb][:, rc, :], start=(rc == 0), stop=(rc == 1))
                        return ins

                    def oev(e, ps, j=j, jb=jb, dc=dc):
                        if dc == 0 and j >= 1:
                            e.wait_ge(s_st2.h, j * 16)
                        return e.activation(out=x2t[jb][:, dc, :], in_=ps[:], func=AF.Identity, bias=b2_sb[:, dc:dc + 1], scale=1.0)

                    group(omm, oev)
                OEV_J = s_ev.n

                def resadd(e, jb=jb, OEV_J=OEV_J):
                    e.wait_ge(s_ev.h, OEV_J)
                    ins = e.tensor_tensor(out=x2t[jb][:], in0=x2t[jb][:], in1=xt[jb][:], op=ALU.add)
                    s_x2.inc(ins)
                S("vector", resadd)
                X2_J = s_x2.n

                def store(e, j=j, jb=jb, X2_J=X2_J):
                    e.wait_ge(s_x2.h, X2_J)
                    s_st2.dma(e.dma_start(out=x2o3[:, :, j * TW:(j + 1) * TW], in_=x2t[jb][:]))
                S("gpsimd", store)

            def fin(e):
                e.wait_ge(s_st2.h, s_st2.n)
            S("sync", fin)

    return phase_terminals()


def l0ffn_inputs(x1_fm, inp):
    import ml_dtypes
    return {
        "x1": x1_fm,
        "u1": inp["L0_ffn_U1"].copy(),
        "v1": inp["L0_ffn_V1"].copy(),
        "u2": inp["L0_ffn_U2"].copy(),
        "v2": inp["L0_ffn_V2"].copy(),
        "b1_r": np.ascontiguousarray(inp["L0_ffn_b1"].reshape(32, 128).T),
        "b2_r": np.ascontiguousarray(inp["L0_ffn_b2"].reshape(NCH, 128).T),
        "ones_col": np.ones((128, 1), ml_dtypes.bfloat16).view(np.uint16),
    }



def emit_l1pre(nc, block, semctx, pfx, io, gate):
    x2 = io["x2"]
    win = io["win"]
    bin_r = io["bin_r"]
    ones_col = io["ones_col"]
    xg_out = io["xg"]
    lnrows = nc.dram_tensor(pfx + "lnrows", [8, 2, 512], F32, kind="Internal").ap()

    x23 = x2.rearrange("(c p) t -> p c t", p=128)
    xgo3 = xg_out.rearrange("(c p) t -> p c t", p=128)
    TW = 512
    NT = TOK // TW

    with ExitStack() as ctx:
        emit_gate(block, gate)
        env = Env(nc, ctx, block, semctx, pfx)
        sb, sem, S = env.sb, env.sem, env.S

        win_sb = sb("win_sb", [128, NCH, 2048], BF16)
        bin_sb = sb("bin_sb", [128, 16], F32)
        env.onec_sb = sb("onec_sb", [128, 1], BF16)
        xn = [sb(f"xn{i}", [128, NCH, TW], BF16) for i in range(2)]
        xt = [sb(f"xt{i}", [128, NCH, TW], F32) for i in range(2)]
        rstd_rep = [sb(f"rstd_rep{i}", [128, TW], F32) for i in range(2)]
        mr_rep = [sb(f"mr_rep{i}", [128, TW], F32) for i in range(2)]
        tnorm = sb("tnorm", [128, NCH, TW], F32)
        _xgt = sb("xgt", [128, 16, TW], F32)
        xgt = [_xgt, _xgt]

        s_ldc = sem("s_ldc")
        s_wl = [sem(f"s_wl{i}") for i in range(2)]
        s_w = sem("s_w")
        s_xt2 = [sem(f"s_xt2{i}") for i in range(2)]
        s_lr = [sem(f"s_lr{i}") for i in range(2)]
        s_norm = sem("s_norm")
        s_mm = sem("s_mm")
        s_ev = sem("s_ev")
        s_st2 = sem("s_st2")

        def init_loads(e):
            s_ldc.dma(e.dma_start(out=bin_sb[:], in_=bin_r))
            s_ldc.dma(e.dma_start(out=env.onec_sb[:], in_=ones_col))
        S("sync", init_loads)

        wctx = ExitStack()
        wtmp = [wctx.enter_context(nc.sbuf_tensor(pfx + f"wtmp{i}", [128, 2048], F32)) for i in range(2)]

        def wload(e):
            ws = win.rearrange("(c p) o -> p c o", p=128)
            for c in range(NCH):
                if c >= 2:
                    e.wait_ge(s_w.h, c - 1)
                s_wl[c % 2].dma(e.dma_start(out=wtmp[c % 2][:], in_=ws[:, c, :]))
        S("scalar", wload)

        def wcast(e):
            for c in range(NCH):
                e.wait_ge(s_wl[c % 2].h, (c // 2 + 1) * 16)
                ins = e.tensor_copy(out=win_sb[:, c, :], in_=wtmp[c % 2][:])
                s_w.inc(ins)
        S("vector", wcast)
        W_DONE = s_w.n
        wctx.close()

        s_rw, s_stA = emit_ln_stats(env, x23, NT, TW, lnrows,
                                    pre_wait=lambda e: e.wait_ge(s_w.h, W_DONE))
        A_RW = s_rw.n
        A_ST = s_stA.n
        norm_done = {}

        with nc.psum_tensor(pfx + "ps_a", [128, 512], F32) as ps_a, \
             nc.psum_tensor(pfx + "ps_b", [128, 512], F32) as ps_b:
            ps_pp = [ps_a, ps_b]
            pp_ev = []
            pp_i = [0]

            def group(mm_emit, ev_emit, pre_mm=None):
                b = pp_i[0] % 2

                def mm(e, b=b, prev=(pp_ev[-2] if len(pp_ev) >= 2 else None)):
                    if pre_mm:
                        pre_mm(e)
                    if prev is not None:
                        e.wait_ge(s_ev.h, prev)
                    ins = mm_emit(e, ps_pp[b])
                    s_mm.inc(ins)
                S("tensor", mm)
                MM = s_mm.n

                def ev(e, b=b, MM=MM):
                    e.wait_ge(s_mm.h, MM)
                    ins = ev_emit(e, ps_pp[b])
                    s_ev.inc(ins)
                S("scalar", ev)
                pp_ev.append(s_ev.n)
                pp_i[0] += 1
                return s_ev.n

            for j in range(NT):
                jb = j % 2

                def ld(e, j=j, jb=jb):
                    if j < 2:
                        e.wait_ge(s_stA.h, A_ST)
                    else:
                        e.wait_ge(s_norm.h, norm_done[j - 2])
                    s_xt2[jb].dma(e.dma_start(out=xt[jb][:], in_=x23[:, :, j * TW:(j + 1) * TW]))
                    e.wait_ge(s_rw.h, A_RW)
                    s_lr[jb].dma(e.dma_start(out=rstd_rep[jb][:], in_=lnrows[j, 1:2, :].partition_broadcast(128)))
                    s_lr[jb].dma(e.dma_start(out=mr_rep[jb][:], in_=lnrows[j, 0:1, :].partition_broadcast(128)))
                S("sync", ld)
                XT_J = s_xt2[jb].n
                LR_J = s_lr[jb].n

                def norm(e, jb=jb, XT_J=XT_J, LR_J=LR_J):
                    e.wait_ge(s_xt2[jb].h, XT_J)
                    e.wait_ge(s_lr[jb].h, LR_J)
                    e.wait_ge(s_mm.h, s_mm.n)
                    e.tensor_tensor(out=tnorm[:], in0=xt[jb][:], in1=rstd_rep[jb][:].unsqueeze(1).broadcast_to([128, NCH, TW]), op=ALU.mult)
                    ins = e.tensor_tensor(out=xn[jb][:], in0=tnorm[:], in1=mr_rep[jb][:].unsqueeze(1).broadcast_to([128, NCH, TW]), op=ALU.subtract)
                    s_norm.inc(ins, 2)
                S("vector", norm)
                norm_done[j] = s_norm.n
                NORM_J = s_norm.n

                for oc in range(16):
                    def omm(e, ps, jb=jb, oc=oc):
                        for a in range(NCH):
                            ins = e.matmul(ps[:], win_sb[:, a, oc * 128:(oc + 1) * 128], xn[jb][:, a, :], start=(a == 0), stop=(a == NCH - 1))
                        return ins

                    def oev(e, ps, j=j, jb=jb, oc=oc):
                        if oc == 0 and j >= 1:
                            e.wait_ge(s_st2.h, j * 16)
                        return e.activation(out=xgt[jb][:, oc, :], in_=ps[:], func=AF.Identity, bias=bin_sb[:, oc:oc + 1], scale=1.0)

                    group(omm, oev, pre_mm=(lambda e, NORM_J=NORM_J: e.wait_ge(s_norm.h, NORM_J)))
                OEV_J = s_ev.n

                def store(e, j=j, jb=jb, OEV_J=OEV_J):
                    e.wait_ge(s_ev.h, OEV_J)
                    s_st2.dma(e.dma_start(out=xgo3[:, :, j * TW:(j + 1) * TW], in_=xgt[jb][:]))
                S("gpsimd", store)

            def fin(e):
                e.wait_ge(s_st2.h, s_st2.n)
            S("sync", fin)
    return phase_terminals()


def l1pre_inputs(x2_fm, inp):
    import ml_dtypes
    return {
        "x2": x2_fm,
        "win": inp["L1_attn_W_in"].copy(),
        "bin_r": np.ascontiguousarray(inp["L1_attn_b_in"].reshape(16, 128).T),
        "ones_col": np.ones((128, 1), ml_dtypes.bfloat16).view(np.uint16),
    }



def emit_conv(nc, block, semctx, pfx, io, gate):
    S8 = 8192
    xs = io["xs"]
    cw = io["cw"]
    NG = io.get("ng", 4)
    cb = io["cb"]
    zer = io["zer"]
    xcb_out = io["xcb"]
    xs3 = xs.rearrange("(g p) t -> p g t", p=128)
    xo3 = xcb_out.rearrange("(g p) t -> p g t", p=128)

    with ExitStack() as ctx:
        emit_gate(block, gate)
        env = Env(nc, ctx, block, semctx, pfx)
        sb, sem, S = env.sb, env.sem, env.S
        cw_sb = sb("cw_sb", [128, NG, 4], F32)
        cb_sb = sb("cb_sb", [128, NG], F32)
        xpad = [sb(f"xpad{i}", [128, 4 + S8], F32) for i in range(2)]
        acc = [sb(f"acc{i}", [128, S8], F32) for i in range(2)]
        xcb_sb = [sb(f"xcb{i}", [128, S8], BF16) for i in range(2)]
        s_ldc = sem("s_ldc")
        s_xp = [sem(f"s_xp{i}") for i in range(2)]
        s_cv = sem("s_cv")
        s_si = sem("s_si")
        s_st = sem("s_st")

        def init_loads(e):
            s_ldc.dma(e.dma_start(out=cw_sb[:], in_=cw))
            s_ldc.dma(e.dma_start(out=cb_sb[:], in_=cb))
            s_ldc.dma(e.dma_start(out=xpad[0][:, 0:4], in_=zer))
            s_ldc.dma(e.dma_start(out=xpad[1][:, 0:4], in_=zer))
        S("sync", init_loads)
        LDC = s_ldc.n

        for g in range(NG):
            gb = g % 2

            def ld(e, g=g, gb=gb):
                if g < 2:
                    e.wait_ge(s_ldc.h, LDC)
                else:
                    e.wait_ge(s_cv.h, (g - 1) * 4)
                s_xp[gb].dma(e.dma_start(out=xpad[gb][:, 4:], in_=xs3[:, g, :]))
            S("sync", ld)
            XP_G = s_xp[gb].n

            def conv(e, g=g, gb=gb, XP_G=XP_G):
                e.wait_ge(s_xp[gb].h, XP_G)
                if g >= 2:
                    e.wait_ge(s_si.h, g - 1)
                e.tensor_scalar(out=acc[gb][:], in0=xpad[gb][:, 4:4 + S8], scalar1=cw_sb[:, g, 3:4], scalar2=None, op0=ALU.mult)
                for k in range(3):
                    ins = e.scalar_tensor_tensor(out=acc[gb][:], in0=xpad[gb][:, 1 + k:1 + k + S8], scalar=cw_sb[:, g, k:k + 1], in1=acc[gb][:], op0=ALU.mult, op1=ALU.add)
                s_cv.inc(ins, 4)
            S("vector", conv)
            CV_G = s_cv.n

            def silu(e, g=g, gb=gb, CV_G=CV_G):
                e.wait_ge(s_cv.h, CV_G)
                if g >= 2:
                    e.wait_ge(s_st.h, (g - 1) * 16)
                ins = e.activation(out=xcb_sb[gb][:], in_=acc[gb][:], func=AF.Silu, bias=cb_sb[:, g:g + 1], scale=1.0)
                s_si.inc(ins)
            S("scalar", silu)
            SI_G = s_si.n

            def store(e, g=g, gb=gb, SI_G=SI_G):
                e.wait_ge(s_si.h, SI_G)
                s_st.dma(e.dma_start(out=xo3[:, g, :], in_=xcb_sb[gb][:]))
            S("gpsimd", store)

        def fin(e):
            e.wait_ge(s_st.h, s_st.n)
        S("sync", fin)
    return phase_terminals()


def conv_inputs(xs_half, inp, dh):
    cwh = inp["L1_attn_conv_w"][dh * 512:(dh + 1) * 512]
    cbh = inp["L1_attn_conv_b"][dh * 512:(dh + 1) * 512]
    return {
        "xs": xs_half,
        "cw": np.ascontiguousarray(cwh.reshape(4, 128, 4).transpose(1, 0, 2)),
        "cb": np.ascontiguousarray(cbh.reshape(4, 128).T),
        "zer": np.zeros((128, 4), np.float32),
    }



def emit_bc(nc, block, semctx, pfx, io, gate):
    xcb = io["xcb"]
    wx = io["wx"]
    bx = io["bx"]
    bc_out = io["bc"]
    NCOLS = io.get("ncols", TOK)
    xc3 = xcb.rearrange("(c p) t -> p c t", p=128)
    TW = 512
    NT = NCOLS // TW

    with ExitStack() as ctx:
        emit_gate(block, gate)
        env = Env(nc, ctx, block, semctx, pfx)
        sb, sem, S = env.sb, env.sem, env.S
        wx_sb = sb("wx_sb", [128, NCH, 32], BF16)
        wxf = sb("wxf", [128, NCH, 32], F32)
        bx_sb = sb("bx_sb", [32, 1], F32)
        xct = [sb(f"xct{i}", [128, NCH, TW], BF16) for i in range(2)]
        bct = [sb(f"bct{i}", [32, TW], BF16) for i in range(2)]
        s_ldc = sem("s_ldc")
        s_xc = [sem(f"s_xc{i}") for i in range(2)]
        s_w = sem("s_w")
        s_mm = sem("s_mm")
        s_ev = sem("s_ev")
        s_st = sem("s_st")

        def init_loads(e):
            s_ldc.dma(e.dma_start(out=wxf[:], in_=wx.rearrange("(c p) o -> p c o", p=128)))
            s_ldc.dma(e.dma_start(out=bx_sb[:], in_=bx))
        S("sync", init_loads)
        LDC = s_ldc.n

        def wcast(e):
            e.wait_ge(s_ldc.h, LDC)
            ins = e.tensor_copy(out=wx_sb[:], in_=wxf[:])
            s_w.inc(ins)
        S("vector", wcast)
        W_DONE = s_w.n

        with nc.psum_tensor(pfx + "ps_a", [128, 512], F32) as ps_a, nc.psum_tensor(pfx + "ps_b", [128, 512], F32) as ps_b:
            ps_pp = [ps_a, ps_b]
            for j in range(NT):
                jb = j % 2

                def ld(e, j=j, jb=jb):
                    if j >= 2:
                        e.wait_ge(s_mm.h, j - 1)
                    s_xc[jb].dma(e.dma_start(out=xct[jb][:], in_=xc3[:, :, j * TW:(j + 1) * TW]))
                S("sync", ld)
                XC_J = s_xc[jb].n

                def mm(e, j=j, jb=jb, XC_J=XC_J):
                    e.wait_ge(s_xc[jb].h, XC_J)
                    if j == 0:
                        e.wait_ge(s_w.h, W_DONE)
                    if j >= 2:
                        e.wait_ge(s_ev.h, j - 1)
                    for a in range(NCH):
                        ins = e.matmul(ps_pp[jb][0:32, :], wx_sb[:, a, :], xct[jb][:, a, :], start=(a == 0), stop=(a == NCH - 1))
                    s_mm.inc(ins)
                S("tensor", mm)
                MM_J = s_mm.n

                def ev(e, jb=jb, MM_J=MM_J, j=j):
                    e.wait_ge(s_mm.h, MM_J)
                    if j >= 2:
                        e.wait_ge(s_st.h, (j - 1) * 16)
                    ins = e.activation(out=bct[jb][:], in_=ps_pp[jb][0:32, :], func=AF.Identity, bias=bx_sb[:], scale=1.0)
                    s_ev.inc(ins)
                S("scalar", ev)
                EV_J = s_ev.n

                def store(e, j=j, jb=jb, EV_J=EV_J):
                    e.wait_ge(s_ev.h, EV_J)
                    s_st.dma(e.dma_start(out=bc_out[:, j * TW:(j + 1) * TW], in_=bct[jb][:]))
                S("gpsimd", store)

            def fin(e):
                e.wait_ge(s_st.h, s_st.n)
            S("sync", fin)
    return phase_terminals()


def bc_inputs(xcb_full, inp):
    return {
        "xcb": xcb_full,
        "wx": inp["L1_attn_W_x"].copy(),
        "bx": inp["L1_attn_b_x"].reshape(32, 1).copy(),
    }



def emit_scan(nc, block, semctx, pfx, io, gate):
    S8 = 8192
    TC = 4096
    NTC = S8 // TC
    xcb = io["xcb"]
    bc = io["bc"]
    alane = io["alane"]
    dcol = io["dcol"]
    sel = io["sel"]
    zcol = io["zcol"]
    y_out = io["y"]
    y3 = y_out.rearrange("(g p) t -> p g t", p=128)

    with ExitStack() as ctx:
        emit_gate(block, gate)
        env = Env(nc, ctx, block, semctx, pfx)
        sb, sem, S = env.sb, env.sem, env.S
        a_sb = sb("a_sb", [128, 64], F32)
        at_sb = sb("at_sb", [128, 64], F32)
        d_sb = sb("d_sb", [128, 4], F32)
        sel_sb = sb("sel_sb", [128, 16, 128], BF16)
        z_sb = sb("z_sb", [128, 1], F32)
        hstv = [sb(f"hstv{i}", [128, 1], F32) for i in range(64)]
        bexp = sb("bexp", [128, TC], BF16)
        cexp = sb("cexp", [128, TC], BF16)
        xce = [sb(f"xce{i}", [128, TC], BF16) for i in range(2)]
        u_sb = sb("u_sb", [128, TC], BF16)
        h_sb2 = [sb(f"h_sb{i}", [128, TC], BF16) for i in range(2)]
        tmpb = [sb(f"tmpb{i}", [128, TC], BF16) for i in range(16)]
        xcd = [sb(f"xcd{i}", [128, 512], BF16) for i in range(2)]
        yt = [sb(f"yt{i}", [128, 512], F32) for i in range(2)]

        s_ldc = sem("s_ldc")
        s_bc = sem("s_bc")
        s_xc = [sem(f"s_xc{i}") for i in range(2)]
        s_xcd = [sem(f"s_xcd{i}") for i in range(2)]
        s_a = sem("s_a")
        s_tmp = sem("s_tmp")
        s_scn = sem("s_scn")
        s_hst = sem("s_hst")
        s_ymm = sem("s_ymm")
        s_yev = sem("s_yev")
        s_st = sem("s_st")

        def init_loads(e):
            s_ldc.dma(e.dma_start(out=at_sb[:], in_=alane))
            s_ldc.dma(e.dma_start(out=d_sb[:], in_=dcol))
            s_ldc.dma(e.dma_start(out=z_sb[:], in_=zcol))
            for gg in range(16):
                s_ldc.dma(e.dma_start(out=sel_sb[:, gg, :], in_=sel[gg]))
        S("sync", init_loads)
        LDC = s_ldc.n

        def aprep(e):
            e.wait_ge(s_ldc.h, LDC)
            e.activation(out=a_sb[:], in_=at_sb[:], func=AF.Exp)
            ins = e.activation(out=a_sb[:], in_=a_sb[:], func=AF.Exp, scale=-1.0)
            s_a.inc(ins, 2)
        S("scalar", aprep)
        A_DONE = s_a.n

        tmp_done_prev_db = [0]
        ps_yctx = ExitStack()
        ps_y = [ps_yctx.enter_context(nc.psum_tensor(f"ps_y{i}", [128, 512], F32)) for i in range(2)]
        for tc in range(NTC):
            TMP_ALL = s_tmp.n

            def bcl(e, tc=tc, TMP_ALL=TMP_ALL):
                if tc > 0:
                    e.wait_ge(s_tmp.h, TMP_ALL)
                for n in range(16):
                    s_bc.dma(e.dma_start(out=bexp[n * 8:(n + 1) * 8, :], in_=bc[n:n + 1, tc * TC:(tc + 1) * TC].partition_broadcast(8)))
                for n in range(16):
                    s_bc.dma(e.dma_start(out=cexp[n * 8:(n + 1) * 8, :], in_=bc[16 + n:17 + n, tc * TC:(tc + 1) * TC].partition_broadcast(8)))
            S("sync", bcl)
            BC_TC = s_bc.n

            for db in range(4):
                for gg in range(16):
                    g = db * 16 + gg
                    gb = g % 2

                    def xcl(e, g=g, gb=gb, tc=tc, T2=s_tmp.n):
                        if T2 >= 2:
                            e.wait_ge(s_tmp.h, T2 - 1)
                        for n in range(16):
                            s_xc[gb].dma(e.dma_start(out=xce[gb][n * 8:(n + 1) * 8, :], in_=xcb[g * 8:(g + 1) * 8, tc * TC:(tc + 1) * TC]))
                    S("sync", xcl)
                    XC_G = s_xc[gb].n

                    def chain(e, g=g, gb=gb, gg=gg, tc=tc, XC_G=XC_G, BC_TC=BC_TC, YP=tmp_done_prev_db[0]):
                        e.wait_ge(s_xc[gb].h, XC_G)
                        e.wait_ge(s_bc.h, BC_TC)
                        if g == 0 and tc == 0:
                            e.wait_ge(s_a.h, A_DONE)
                        if YP:
                            e.wait_ge(s_ymm.h, YP)
                        need = 0
                        if g >= 2:
                            need = max(need, tc * 64 + g - 1)
                        elif tc > 0:
                            need = max(need, (tc - 1) * 64 + 64)
                        if tc > 0:
                            need = max(need, (tc - 1) * 64 + g + 1)
                        if need:
                            e.wait_ge(s_hst.h, need)
                        e.tensor_tensor(out=u_sb[:], in0=xce[gb][:], in1=bexp[:], op=ALU.mult)
                        init_ap = z_sb[:] if tc == 0 else hstv[g][:]
                        sc = e.tensor_tensor_scan(out=h_sb2[gb][:], data0=a_sb[:, g:g + 1].broadcast_to([128, TC]), data1=u_sb[:], initial=init_ap, op0=ALU.mult, op1=ALU.add)
                        s_scn.inc(sc)
                        ins = e.tensor_tensor(out=tmpb[gg][:], in0=h_sb2[gb][:], in1=cexp[:], op=ALU.mult)
                        s_tmp.inc(ins)
                    S("vector", chain)

                    def hcap(e, g=g, gb=gb, SCN=s_scn.n):
                        e.wait_ge(s_scn.h, SCN)
                        ins = e.copy(out=hstv[g][:], in_=h_sb2[gb][:, TC - 1:TC])
                        s_hst.inc(ins)
                    S("scalar", hcap)

                TMP_DB = s_tmp.n
                for ts8 in range(TC // 512):
                    ypb = ts8 % 2

                    def xcdl(e, db=db, tc=tc, ts8=ts8, ypb=ypb, YEV=s_yev.n):
                        if YEV >= 2:
                            e.wait_ge(s_yev.h, YEV - 1)
                        s_xcd[ypb].dma(e.dma_start(out=xcd[ypb][:], in_=xcb[db * 128:(db + 1) * 128, tc * TC + ts8 * 512: tc * TC + (ts8 + 1) * 512]))
                    S("sync", xcdl)
                    XCD = s_xcd[ypb].n

                    def ymm(e, db=db, ts8=ts8, ypb=ypb, TMP_DB=TMP_DB, YEV=s_yev.n):
                        e.wait_ge(s_tmp.h, TMP_DB)
                        if YEV >= 2:
                            e.wait_ge(s_yev.h, YEV - 1)
                        for gg in range(16):
                            ins = e.matmul(ps_y[ypb][:], sel_sb[:, gg, :], tmpb[gg][:, ts8 * 512:(ts8 + 1) * 512], start=(gg == 0), stop=(gg == 15))
                        s_ymm.inc(ins)
                    S("tensor", ymm)
                    YMM = s_ymm.n

                    def yev(e, db=db, ypb=ypb, YMM=YMM, XCD=XCD, ST=s_st.n):
                        e.wait_ge(s_ymm.h, YMM)
                        e.wait_ge(s_xcd[ypb].h, XCD)
                        if ST >= 32:
                            e.wait_ge(s_st.h, ST - 16)
                        ins = e.scalar_tensor_tensor(out=yt[ypb][:], in0=xcd[ypb][:], scalar=d_sb[:, db:db + 1], in1=ps_y[ypb][:], op0=ALU.mult, op1=ALU.add)
                        s_yev.inc(ins)
                    S("vector", yev)
                    YEV2 = s_yev.n

                    def yst(e, db=db, tc=tc, ts8=ts8, ypb=ypb, YEV2=YEV2):
                        e.wait_ge(s_yev.h, YEV2)
                        s_st.dma(e.dma_start(out=y3[:, db, tc * TC + ts8 * 512: tc * TC + (ts8 + 1) * 512], in_=yt[ypb][:]))
                    S("gpsimd", yst)
                tmp_done_prev_db[0] = s_ymm.n

        def fin(e):
            e.wait_ge(s_st.h, s_st.n)
        S("sync", fin)
        ps_yctx.close()
    return phase_terminals()


def scan_inputs(xcb_half, bc_full, inp, dh):
    import ml_dtypes
    al = inp["L1_attn_A_log"][dh * 512:(dh + 1) * 512]
    alane = np.zeros((128, 64), np.float32)
    for g in range(64):
        for oct in range(8):
            for n in range(16):
                alane[n * 8 + oct, g] = al[g * 8 + oct, n]
    dp = inp["L1_attn_D_param"][dh * 512:(dh + 1) * 512]
    selm = np.zeros((16, 128, 128), np.float32)
    for gg in range(16):
        for n in range(16):
            for oct in range(8):
                selm[gg, n * 8 + oct, gg * 8 + oct] = 1.0
    return {
        "xcb": xcb_half,
        "bc": bc_full,
        "alane": alane,
        "dcol": np.ascontiguousarray(dp.reshape(4, 128).T),
        "sel": selm.astype(ml_dtypes.bfloat16).view(np.uint16),
        "zcol": np.zeros((128, 1), np.float32),
    }



def emit_l1post(nc, block, semctx, pfx, io, gate):
    y_in = io["y"]
    gate_t = io["gate"]
    x2 = io["x2"]
    wo = io["wo"]
    bo_r = io["bo_r"]
    x3_out = io["x3"]
    y3 = y_in.rearrange("(c p) t -> p c t", p=128)
    g3 = gate_t.rearrange("(c p) t -> p c t", p=128)
    x23 = x2.rearrange("(c p) t -> p c t", p=128)
    xo3 = x3_out.rearrange("(c p) t -> p c t", p=128)
    TW = 512
    NT = TOK // TW

    with ExitStack() as ctx:
        emit_gate(block, gate)
        env = Env(nc, ctx, block, semctx, pfx)
        sb, sem, S = env.sb, env.sem, env.S
        wo_sb = sb("wo_sb", [128, NCH, E], BF16)
        bo_sb = sb("bo_sb", [128, NCH], F32)
        yt = [sb(f"yt{i}", [128, NCH, TW], F32) for i in range(2)]
        gt = [sb(f"gt{i}", [128, NCH, TW], F32) for i in range(2)]
        x2t = [sb(f"x2t{i}", [128, NCH, TW], F32) for i in range(2)]
        ysg = [sb(f"ysg{i}", [128, NCH, TW], BF16) for i in range(2)]
        ot = sb("ot", [128, NCH, TW], F32)
        s_ldc = sem("s_ldc")
        s_wl = [sem(f"s_wl{i}") for i in range(2)]
        s_w = sem("s_w")
        s_ld = [sem(f"s_ld{i}") for i in range(2)]
        s_sg = sem("s_sg")
        s_ys = sem("s_ys")
        s_mm = sem("s_mm")
        s_ev = sem("s_ev")
        s_x3 = sem("s_x3")
        s_st = sem("s_st")

        def init_loads(e):
            s_ldc.dma(e.dma_start(out=bo_sb[:], in_=bo_r))
        S("sync", init_loads)

        wctx = ExitStack()
        wtmp = [wctx.enter_context(nc.sbuf_tensor(pfx + f"wtmp{i}", [128, E], F32)) for i in range(2)]

        def wload(e):
            ws = wo.rearrange("(c p) o -> p c o", p=128)
            for c in range(NCH):
                if c >= 2:
                    e.wait_ge(s_w.h, c - 1)
                s_wl[c % 2].dma(e.dma_start(out=wtmp[c % 2][:], in_=ws[:, c, :]))
        S("scalar", wload)

        def wcast(e):
            for c in range(NCH):
                e.wait_ge(s_wl[c % 2].h, (c // 2 + 1) * 16)
                ins = e.tensor_copy(out=wo_sb[:, c, :], in_=wtmp[c % 2][:])
                s_w.inc(ins)
        S("vector", wcast)
        W_DONE = s_w.n
        wctx.close()

        with nc.psum_tensor(pfx + "ps_a", [128, 512], F32) as ps_a, nc.psum_tensor(pfx + "ps_b", [128, 512], F32) as ps_b:
            ps_pp = [ps_a, ps_b]
            pp_ev = []
            pp_i = [0]

            def group(mm_emit, ev_emit, pre_mm=None):
                b = pp_i[0] % 2

                def mm(e, b=b, prev=(pp_ev[-2] if len(pp_ev) >= 2 else None)):
                    if pre_mm:
                        pre_mm(e)
                    if prev is not None:
                        e.wait_ge(s_ev.h, prev)
                    ins = mm_emit(e, ps_pp[b])
                    s_mm.inc(ins)
                S("tensor", mm)
                MM = s_mm.n

                def ev(e, b=b, MM=MM):
                    e.wait_ge(s_mm.h, MM)
                    ins = ev_emit(e, ps_pp[b])
                    s_ev.inc(ins)
                S("scalar", ev)
                pp_ev.append(s_ev.n)
                pp_i[0] += 1
                return s_ev.n

            for j in range(NT):
                jb = j % 2

                def ld(e, j=j, jb=jb):
                    if j == 0:
                        e.wait_ge(s_w.h, W_DONE)
                    if j >= 2:
                        e.wait_ge(s_ys.h, j - 1)
                        e.wait_ge(s_x3.h, j - 1)
                        e.wait_ge(s_st.h, (j - 1) * 16)
                    s_ld[jb].dma(e.dma_start(out=yt[jb][:], in_=y3[:, :, j * TW:(j + 1) * TW]))
                    s_ld[jb].dma(e.dma_start(out=gt[jb][:], in_=g3[:, :, j * TW:(j + 1) * TW]))
                    s_ld[jb].dma(e.dma_start(out=x2t[jb][:], in_=x23[:, :, j * TW:(j + 1) * TW]))
                S("sync", ld)
                LD_J = s_ld[jb].n

                def sg(e, jb=jb, LD_J=LD_J, j=j):
                    e.wait_ge(s_ld[jb].h, LD_J)
                    if j >= 2:
                        e.wait_ge(s_ys.h, j - 1)
                    ins = e.activation(out=gt[jb][:], in_=gt[jb][:], func=AF.Silu)
                    s_sg.inc(ins)
                S("scalar", sg)
                SG_J = s_sg.n

                def ysb(e, jb=jb, SG_J=SG_J, MM_ALL=s_mm.n):
                    e.wait_ge(s_sg.h, SG_J)
                    e.wait_ge(s_mm.h, MM_ALL)
                    ins = e.tensor_tensor(out=ysg[jb][:], in0=yt[jb][:], in1=gt[jb][:], op=ALU.mult)
                    s_ys.inc(ins)
                S("vector", ysb)
                YS_J = s_ys.n

                for dc in range(NCH):
                    def omm(e, ps, jb=jb, dc=dc, YS_J=YS_J):
                        e.wait_ge(s_ys.h, YS_J)
                        for a in range(NCH):
                            ins = e.matmul(ps[:], wo_sb[:, a, dc * 128:(dc + 1) * 128], ysg[jb][:, a, :], start=(a == 0), stop=(a == NCH - 1))
                        return ins

                    def oev(e, ps, j=j, dc=dc):
                        if dc == 0 and j >= 1:
                            e.wait_ge(s_x3.h, j)
                        return e.activation(out=ot[:, dc, :], in_=ps[:], func=AF.Identity, bias=bo_sb[:, dc:dc + 1], scale=1.0)

                    group(omm, oev)
                OEV_J = s_ev.n

                def resadd(e, jb=jb, OEV_J=OEV_J):
                    e.wait_ge(s_ev.h, OEV_J)
                    if s_st.n:
                        e.wait_ge(s_st.h, s_st.n)
                    ins = e.tensor_tensor(out=x2t[jb][:], in0=x2t[jb][:], in1=ot[:], op=ALU.add)
                    s_x3.inc(ins)
                S("vector", resadd)
                X3_J = s_x3.n

                def store(e, j=j, jb=jb, X3_J=X3_J):
                    e.wait_ge(s_x3.h, X3_J)
                    s_st.dma(e.dma_start(out=xo3[:, :, j * TW:(j + 1) * TW], in_=x2t[jb][:]))
                S("gpsimd", store)

            def fin(e):
                e.wait_ge(s_st.h, s_st.n)
            S("sync", fin)
    return phase_terminals()


def l1post_inputs(y_fm, gate_fm, x2_fm, inp):
    return {
        "y": y_fm, "gate": gate_fm, "x2": x2_fm,
        "wo": inp["L1_attn_W_out"].copy(),
        "bo_r": np.ascontiguousarray(inp["L1_attn_b_out"].reshape(NCH, 128).T),
    }



def emit_ln4(nc, block, semctx, pfx, io, gate):
    x3 = io["x3"]
    ones_col = io["ones_col"]
    xn4_out = io["xn4"]
    mx_out = io["mx"]
    lnrows = nc.dram_tensor(pfx + "lnrows", [8, 2, 512], F32, kind="Internal").ap()
    colscr = nc.dram_tensor(pfx + "colscr", [128, 1], F32, kind="Internal").ap()

    x33 = x3.rearrange("(c p) t -> p c t", p=128)
    xo3 = xn4_out.rearrange("(c p) t -> p c t", p=128)
    TW = 512
    NT = TOK // TW

    with ExitStack() as ctx:
        emit_gate(block, gate)
        env = Env(nc, ctx, block, semctx, pfx)
        sb, sem, S = env.sb, env.sem, env.S
        env.onec_sb = sb("onec_sb", [128, 1], BF16)
        xt = [sb(f"xt{i}", [128, NCH, TW], F32) for i in range(2)]
        rstd_rep = [sb(f"rstd_rep{i}", [128, TW], F32) for i in range(2)]
        mr_rep = [sb(f"mr_rep{i}", [128, TW], F32) for i in range(2)]
        xnt = [sb(f"xnt{i}", [128, NCH, TW], F32) for i in range(2)]
        tmax = sb("tmax", [128, 1], F32)
        rmax = sb("rmax", [128, 1], F32)
        rrow = sb("rrow", [1, 128], F32)
        mx_sb = sb("mx_sb", [1, 1], F32)
        s_ldc = sem("s_ldc")
        s_xt2 = [sem(f"s_xt2{i}") for i in range(2)]
        s_lr = [sem(f"s_lr{i}") for i in range(2)]
        s_norm = sem("s_norm")
        s_mx = sem("s_mx")
        s_st2 = sem("s_st2")
        s_cw = sem("s_cw")
        s_tk = sem("s_tk")

        def init_loads(e):
            s_ldc.dma(e.dma_start(out=env.onec_sb[:], in_=ones_col))
        S("sync", init_loads)
        LDC = s_ldc.n

        s_rw, s_stA = emit_ln_stats(env, x33, NT, TW, lnrows,
                                    pre_wait=lambda e: e.wait_ge(s_ldc.h, LDC))
        A_RW = s_rw.n
        A_ST = s_stA.n
        norm_done = {}

        for j in range(NT):
            jb = j % 2

            def ld(e, j=j, jb=jb):
                if j < 2:
                    e.wait_ge(s_stA.h, A_ST)
                else:
                    e.wait_ge(s_norm.h, norm_done[j - 2])
                s_xt2[jb].dma(e.dma_start(out=xt[jb][:], in_=x33[:, :, j * TW:(j + 1) * TW]))
                e.wait_ge(s_rw.h, A_RW)
                s_lr[jb].dma(e.dma_start(out=rstd_rep[jb][:], in_=lnrows[j, 1:2, :].partition_broadcast(128)))
                s_lr[jb].dma(e.dma_start(out=mr_rep[jb][:], in_=lnrows[j, 0:1, :].partition_broadcast(128)))
            S("sync", ld)
            XT_J = s_xt2[jb].n
            LR_J = s_lr[jb].n

            def norm(e, j=j, jb=jb, XT_J=XT_J, LR_J=LR_J):
                e.wait_ge(s_xt2[jb].h, XT_J)
                e.wait_ge(s_lr[jb].h, LR_J)
                if j >= 2:
                    e.wait_ge(s_st2.h, (j - 1) * 16)
                e.tensor_tensor(out=xt[jb][:], in0=xt[jb][:], in1=rstd_rep[jb][:].unsqueeze(1).broadcast_to([128, NCH, TW]), op=ALU.mult)
                e.tensor_tensor(out=xnt[jb][:], in0=xt[jb][:], in1=mr_rep[jb][:].unsqueeze(1).broadcast_to([128, NCH, TW]), op=ALU.subtract)
                ins = e.tensor_reduce(out=tmax[:], in_=xnt[jb][:].rearrange('p c t -> p (c t)'), axis=mybir.AxisListType.X, op=ALU.max)
                s_tk.inc(ins)
                e.wait_ge(s_tk.h, s_tk.n)
                if j == 0:
                    ins = e.tensor_copy(out=rmax[:], in_=tmax[:])
                else:
                    ins = e.tensor_tensor(out=rmax[:], in0=rmax[:], in1=tmax[:], op=ALU.max)
                s_norm.inc(ins, 2)
            S("vector", norm)
            norm_done[j] = s_norm.n
            NORM_J = s_norm.n

            def store(e, j=j, jb=jb, NORM_J=NORM_J):
                e.wait_ge(s_norm.h, NORM_J)
                s_st2.dma(e.dma_start(out=xo3[:, :, j * TW:(j + 1) * TW], in_=xnt[jb][:]))
            S("gpsimd", store)

        LAST_NORM = s_norm.n

        def colw(e, LAST_NORM=LAST_NORM):
            e.wait_ge(s_norm.h, LAST_NORM)
            s_cw.dma(e.dma_start(out=colscr, in_=rmax[:]))
        S("gpsimd", colw)
        CW = s_cw.n

        def colr(e, CW=CW):
            e.wait_ge(s_cw.h, CW)
            s_cw.dma(e.dma_start(out=rrow[:], in_=colscr.rearrange("p o -> o p")))
        S("sync", colr)
        CR = s_cw.n

        def finmax(e, CR=CR):
            e.wait_ge(s_cw.h, CR)
            ins = e.tensor_reduce(out=mx_sb[:], in_=rrow[:], axis=mybir.AxisListType.X, op=ALU.max)
            s_mx.inc(ins)
        S("vector", finmax)
        MX = s_mx.n

        def mxw(e, MX=MX):
            e.wait_ge(s_mx.h, MX)
            s_cw.dma(e.dma_start(out=mx_out, in_=mx_sb[:]))
        S("gpsimd", mxw)

        def fin(e):
            e.wait_ge(s_st2.h, s_st2.n)
            e.wait_ge(s_cw.h, s_cw.n)
        S("sync", fin)
    return phase_terminals()


def ln4_inputs(x3_fm):
    import ml_dtypes
    return {"x3": x3_fm, "ones_col": np.ones((128, 1), ml_dtypes.bfloat16).view(np.uint16)}



def emit_qw1(nc, block, semctx, pfx, io, gate):
    xn4 = io["xn4"]
    mx8 = io["mx8"]
    w1 = io["w1"]
    b1_r = io["b1_r"]
    h_out = io["h"]
    hm_out = io["hm"]
    rowscr = nc.dram_tensor(pfx + "rowscr", [1, 2], F32, kind="Internal").ap()
    colscr = nc.dram_tensor(pfx + "colscr", [128, 1], F32, kind="Internal").ap()

    xn3 = xn4.rearrange("(c p) t -> p c t", p=128)
    w13 = w1.rearrange("(c p) o -> p c o", p=128)
    ho3 = h_out.rearrange("(c p) t -> p c t", p=128)
    TW = 512
    NT = TOK // TW
    EPS = 1e-8

    with ExitStack() as ctx:
        emit_gate(block, gate)
        env = Env(nc, ctx, block, semctx, pfx)
        sb, sem, S = env.sb, env.sem, env.S
        w1t_sb = sb("w1t_sb", [128, NCH, 4096], BF16)
        b1q_sb = sb("b1q_sb", [128, 32], F32)
        scol = sb("scol", [128, 1], F32)
        rcol = sb("rcol", [128, 1], F32)
        mrow = sb("mrow", [1, 8], F32)
        grow = sb("grow", [1, 2], F32)
        xq = [sb(f"xq{i}", [128, NCH, TW], BF16) for i in range(2)]
        xtt = [sb(f"xtt{i}", [128, NCH, TW], F32) for i in range(2)]
        xqi = sb("xqi", [128, NCH, TW], I32)
        ht = [sb(f"ht{i}", [128, TW], F32) for i in range(2)]
        tmax = sb("tmax", [128, 1], F32)
        rmax = sb("rmax", [128, 1], F32)
        rrow = sb("rrow", [1, 128], F32)
        hm_sb = sb("hm_sb", [1, 1], F32)
        s_ldc = sem("s_ldc")
        s_row = sem("s_row")
        s_rw = sem("s_rw")
        s_xt = [sem(f"s_xt{i}") for i in range(2)]
        s_xq = sem("s_xq")
        s_mm = sem("s_mm")
        s_ev = sem("s_ev")
        s_hx = sem("s_hx")
        s_st = sem("s_st")
        s_wq = sem("s_wq")
        s_wt = sem("s_wt")
        s_cw = sem("s_cw")
        s_tk = sem("s_tk")

        def tick(e, ins):
            s_tk.n += 1
            ins.then_inc(s_tk.h, 1)
            e.wait_ge(s_tk.h, s_tk.n)

        def init_loads(e):
            s_ldc.dma(e.dma_start(out=mrow[:], in_=mx8))
            s_ldc.dma(e.dma_start(out=b1q_sb[:], in_=b1_r))
        S("sync", init_loads)
        LDC = s_ldc.n

        def gmaxc(e, LDC=LDC):
            e.wait_ge(s_ldc.h, LDC)
            tick(e, e.tensor_reduce(out=grow[:, 0:1], in_=mrow[:], axis=mybir.AxisListType.X, op=ALU.max))
            tick(e, e.tensor_scalar(out=grow[:, 0:1], in0=grow[:, 0:1], scalar1=EPS, scalar2=1.0 / 255.0, op0=ALU.max, op1=ALU.mult))
            ins = e.reciprocal(out=grow[:, 1:2], in_=grow[:, 0:1])
            s_row.inc(ins)
        S("vector", gmaxc)
        GR = s_row.n

        def groww(e, GR=GR):
            e.wait_ge(s_row.h, GR)
            s_cw.dma(e.dma_start(out=rowscr, in_=grow[:]))
        S("gpsimd", groww)
        GW = s_cw.n

        def growr(e, GW=GW):
            e.wait_ge(s_cw.h, GW)
            s_cw.dma(e.dma_start(out=scol[:], in_=rowscr[0:1, 0:1].partition_broadcast(128)))
            s_cw.dma(e.dma_start(out=rcol[:], in_=rowscr[0:1, 1:2].partition_broadcast(128)))
        S("sync", growr)
        SCOL = s_cw.n

        wctx = ExitStack()
        wn = [wctx.enter_context(nc.sbuf_tensor(pfx + f"wn{i}", [128, E], F32)) for i in range(2)]
        wq = [wctx.enter_context(nc.sbuf_tensor(pfx + f"wq{i}", [128, E], BF16)) for i in range(2)]
        wqi = wctx.enter_context(nc.sbuf_tensor(pfx + "wqi", [128, E], I32))
        swc = wctx.enter_context(nc.sbuf_tensor(pfx + "swc", [128, 32], F32))
        swi = wctx.enter_context(nc.sbuf_tensor(pfx + "swi", [128, 1], F32))
        s_wn = [sem(f"s_wn{i}") for i in range(2)]

        def wloads(e):
            for fc in range(32):
                if fc >= 2:
                    e.wait_ge(s_wq.h, fc - 1)
                s_wn[fc % 2].dma(e.dma_start(out=wn[fc % 2][:], in_=w13[:, fc, :]))
        S("sync", wloads)

        for fc in range(32):
            def wquant(e, fc=fc):
                e.wait_ge(s_wn[fc % 2].h, (fc // 2 + 1) * 16)
                if fc >= 2:
                    e.wait_ge(s_wt.h, (fc - 1) * NCH * 16)
                tick(e, e.tensor_reduce(out=swc[:, fc:fc + 1], in_=wn[fc % 2][:], axis=mybir.AxisListType.X, op=ALU.max, apply_absolute_value=True))
                tick(e, e.tensor_scalar(out=swc[:, fc:fc + 1], in0=swc[:, fc:fc + 1], scalar1=EPS, scalar2=1.0 / 127.0, op0=ALU.max, op1=ALU.mult))
                tick(e, e.reciprocal(out=swi[:], in_=swc[:, fc:fc + 1]))
                e.tensor_scalar(out=wn[fc % 2][:], in0=wn[fc % 2][:], scalar1=swi[:], scalar2=None, op0=ALU.mult)
                e.tensor_copy(out=wqi[:], in_=wn[fc % 2][:])
                e.tensor_copy(out=wn[fc % 2][:], in_=wqi[:])
                ins = e.tensor_scalar(out=wq[fc % 2][:], in0=wn[fc % 2][:], scalar1=swc[:, fc:fc + 1], scalar2=None, op0=ALU.mult)
                s_wq.inc(ins)
            S("vector", wquant)
            WQ = s_wq.n

            def wtr(e, fc=fc, WQ=WQ):
                e.wait_ge(s_wq.h, WQ)
                for a in range(NCH):
                    s_wt.dma(e.dma_start(out=w1t_sb[:, a, fc * 128:(fc + 1) * 128], in_=wq[fc % 2][:, a * 128:(a + 1) * 128], transpose=True))
            S("scalar", wtr)

        WT_ALL = s_wt.n
        WQ_ALL = s_wq.n
        wctx2 = ExitStack()
        b1t = wctx2.enter_context(nc.sbuf_tensor(pfx + "b1t", [128, 32], F32))
        b1i = wctx2.enter_context(nc.sbuf_tensor(pfx + "b1i", [128, 32], I32))
        swinv = wctx2.enter_context(nc.sbuf_tensor(pfx + "swinv", [128, 32], F32))

        def b1qc(e, WQ_ALL=WQ_ALL, SCOL=SCOL):
            e.wait_ge(s_wq.h, WQ_ALL)
            e.wait_ge(s_cw.h, SCOL)
            tick(e, e.reciprocal(out=swinv[:], in_=swc[:]))
            tick(e, e.tensor_tensor(out=b1t[:], in0=b1q_sb[:], in1=swinv[:], op=ALU.mult))
            tick(e, e.tensor_scalar(out=b1t[:], in0=b1t[:], scalar1=rcol[:], scalar2=None, op0=ALU.mult))
            tick(e, e.tensor_copy(out=b1i[:], in_=b1t[:]))
            tick(e, e.tensor_copy(out=b1t[:], in_=b1i[:]))
            tick(e, e.tensor_tensor(out=b1t[:], in0=b1t[:], in1=swc[:], op=ALU.mult))
            ins = e.tensor_scalar(out=b1q_sb[:], in0=b1t[:], scalar1=scol[:], scalar2=None, op0=ALU.mult)
            s_hx.inc(ins)
        S("vector", b1qc)
        B1Q = s_hx.n

        with nc.psum_tensor(pfx + "ps_a", [128, 512], F32) as ps_a, nc.psum_tensor(pfx + "ps_b", [128, 512], F32) as ps_b:
            ps_pp = [ps_a, ps_b]
            for j in range(NT):
                jb = j % 2

                def ld(e, j=j, jb=jb, WT_ALL=WT_ALL):
                    if j == 0:
                        e.wait_ge(s_wt.h, WT_ALL)
                        e.wait_ge(s_hx.h, B1Q)
                    if j >= 2:
                        e.wait_ge(s_xq.h, (j - 1) * 3)
                    s_xt[jb].dma(e.dma_start(out=xtt[jb][:], in_=xn3[:, :, j * TW:(j + 1) * TW]))
                S("sync", ld)
                XT_J = s_xt[jb].n

                def xquant(e, j=j, jb=jb, XT_J=XT_J, SCOL=SCOL):
                    e.wait_ge(s_xt[jb].h, XT_J)
                    if j == 0:
                        e.wait_ge(s_cw.h, SCOL)
                        e.wait_ge(s_hx.h, B1Q)
                    if j >= 2:
                        e.wait_ge(s_mm.h, (j - 1) * 32)
                    e.tensor_scalar(out=xtt[jb][:], in0=xtt[jb][:], scalar1=rcol[:], scalar2=0.0, op0=ALU.mult, op1=ALU.max)
                    e.tensor_copy(out=xqi[:], in_=xtt[jb][:])
                    ins = e.tensor_copy(out=xq[jb][:], in_=xqi[:])
                    s_xq.inc(ins, 3)
                S("vector", xquant)
                XQ_J = s_xq.n

                for fc in range(32):
                    fb = fc % 2

                    def mmf(e, jb=jb, fc=fc, fb=fb, XQ_J=XQ_J, EV=s_ev.n):
                        e.wait_ge(s_xq.h, XQ_J)
                        if EV >= 2:
                            e.wait_ge(s_ev.h, EV - 1)
                        for a in range(NCH):
                            ins = e.matmul(ps_pp[fb][:], w1t_sb[:, a, fc * 128:(fc + 1) * 128], xq[jb][:, a, :], start=(a == 0), stop=(a == NCH - 1))
                        s_mm.inc(ins)
                    S("tensor", mmf)
                    MM_F = s_mm.n

                    def evf(e, j=j, fb=fb, fc=fc, MM_F=MM_F, HX=s_hx.n, K=(j * 32 + fc)):
                        e.wait_ge(s_mm.h, MM_F)
                        if HX >= B1Q + 2:
                            e.wait_ge(s_hx.h, HX - 1)
                        if K >= 2:
                            e.wait_ge(s_st.h, (K - 1) * 16)
                        ins = e.activation(out=ht[fb][:], in_=ps_pp[fb][:], func=AF.Relu, bias=b1q_sb[:, fc:fc + 1], scale=scol[:])
                        s_ev.inc(ins)
                    S("scalar", evf)
                    EV_F = s_ev.n

                    def hmx(e, j=j, fc=fc, fb=fb, EV_F=EV_F):
                        e.wait_ge(s_ev.h, EV_F)
                        e.tensor_reduce(out=tmax[:], in_=ht[fb][:], axis=mybir.AxisListType.X, op=ALU.max)
                        if j == 0 and fc == 0:
                            ins = e.tensor_copy(out=rmax[:], in_=tmax[:])
                        else:
                            ins = e.tensor_tensor(out=rmax[:], in0=rmax[:], in1=tmax[:], op=ALU.max)
                        s_hx.inc(ins)
                    S("vector", hmx)
                    HX_F = s_hx.n

                    def hst(e, j=j, fc=fc, fb=fb, HX_F=HX_F):
                        e.wait_ge(s_hx.h, HX_F)
                        s_st.dma(e.dma_start(out=ho3[:, fc, j * TW:(j + 1) * TW], in_=ht[fb][:]))
                    S("gpsimd", hst)

        LAST_HX = s_hx.n

        def colw(e, LAST_HX=LAST_HX):
            e.wait_ge(s_hx.h, LAST_HX)
            s_cw.dma(e.dma_start(out=colscr, in_=rmax[:]))
        S("gpsimd", colw)
        CW2 = s_cw.n

        def colr(e, CW2=CW2):
            e.wait_ge(s_cw.h, CW2)
            s_cw.dma(e.dma_start(out=rrow[:], in_=colscr.rearrange("p o -> o p")))
        S("sync", colr)
        CR2 = s_cw.n

        def finmax(e, CR2=CR2):
            e.wait_ge(s_cw.h, CR2)
            ins = e.tensor_reduce(out=hm_sb[:], in_=rrow[:], axis=mybir.AxisListType.X, op=ALU.max)
            s_hx.inc(ins)
        S("vector", finmax)
        FM = s_hx.n

        def hmw(e, FM=FM):
            e.wait_ge(s_hx.h, FM)
            s_cw.dma(e.dma_start(out=hm_out, in_=hm_sb[:]))
        S("gpsimd", hmw)

        def fin(e):
            e.wait_ge(s_st.h, s_st.n)
            e.wait_ge(s_cw.h, s_cw.n)
        S("sync", fin)
        wctx2.close()
        wctx.close()
    return phase_terminals()


def qw1_inputs(xn4_fm, mx8, inp):
    return {
        "xn4": xn4_fm,
        "mx8": mx8.reshape(1, 8),
        "w1": inp["L1_ffn_w1"].copy(),
        "b1_r": np.ascontiguousarray(inp["L1_ffn_b1"].reshape(32, 128).T),
    }



def emit_qw2(nc, block, semctx, pfx, io, gate):
    h_in = io["h"]
    hm8 = io["hm8"]
    w2 = io["w2"]
    b2_r = io["b2_r"]
    x3 = io["x3"]
    x4_out = io["x4"]
    rowscr = nc.dram_tensor(pfx + "rowscr", [1, 2], F32, kind="Internal").ap()

    h3 = h_in.rearrange("(c p) t -> p c t", p=128)
    w23 = w2.rearrange("(c p) o -> p c o", p=128)
    x33 = x3.rearrange("(c p) t -> p c t", p=128)
    xo3 = x4_out.rearrange("(c p) t -> p c t", p=128)
    TW = 512
    NT = TOK // TW
    EPS = 1e-8

    with ExitStack() as ctx:
        emit_gate(block, gate)
        env = Env(nc, ctx, block, semctx, pfx)
        sb, sem, S = env.sb, env.sem, env.S
        w2t_sb = sb("w2t_sb", [128, 32, E], BF16)
        b2q_sb = sb("b2q_sb", [128, NCH], F32)
        scol = sb("scol", [128, 1], F32)
        rcol = sb("rcol", [128, 1], F32)
        mrow = sb("mrow", [1, 8], F32)
        grow = sb("grow", [1, 2], F32)
        s_ldc = sem("s_ldc")
        s_row = sem("s_row")
        s_ht = [sem(f"s_ht{i}") for i in range(2)]
        s_x3l = [sem(f"s_x3l{i}") for i in range(2)]
        s_hq = sem("s_hq")
        s_mm = sem("s_mm")
        s_ev = sem("s_ev")
        s_x4 = sem("s_x4")
        s_st = sem("s_st")
        s_wq = sem("s_wq")
        s_wt = sem("s_wt")
        s_cw = sem("s_cw")
        s_tk = sem("s_tk")

        def tick(e, ins):
            s_tk.n += 1
            ins.then_inc(s_tk.h, 1)
            e.wait_ge(s_tk.h, s_tk.n)

        def init_loads(e):
            s_ldc.dma(e.dma_start(out=mrow[:], in_=hm8))
            s_ldc.dma(e.dma_start(out=b2q_sb[:], in_=b2_r))
        S("sync", init_loads)
        LDC = s_ldc.n

        def gmaxc(e, LDC=LDC):
            e.wait_ge(s_ldc.h, LDC)
            tick(e, e.tensor_reduce(out=grow[:, 0:1], in_=mrow[:], axis=mybir.AxisListType.X, op=ALU.max))
            tick(e, e.tensor_scalar(out=grow[:, 0:1], in0=grow[:, 0:1], scalar1=EPS, scalar2=1.0 / 255.0, op0=ALU.max, op1=ALU.mult))
            ins = e.reciprocal(out=grow[:, 1:2], in_=grow[:, 0:1])
            s_row.inc(ins)
        S("vector", gmaxc)
        GR = s_row.n

        def groww(e, GR=GR):
            e.wait_ge(s_row.h, GR)
            s_cw.dma(e.dma_start(out=rowscr, in_=grow[:]))
        S("gpsimd", groww)
        GW = s_cw.n

        def growr(e, GW=GW):
            e.wait_ge(s_cw.h, GW)
            s_cw.dma(e.dma_start(out=scol[:], in_=rowscr[0:1, 0:1].partition_broadcast(128)))
            s_cw.dma(e.dma_start(out=rcol[:], in_=rowscr[0:1, 1:2].partition_broadcast(128)))
        S("sync", growr)
        SCOL = s_cw.n

        wctx = ExitStack()
        wn = [wctx.enter_context(nc.sbuf_tensor(pfx + f"wn{i}", [128, 4096], F32)) for i in range(2)]
        wq = [wctx.enter_context(nc.sbuf_tensor(pfx + f"wq{i}", [128, 4096], BF16)) for i in range(2)]
        wqi = wctx.enter_context(nc.sbuf_tensor(pfx + "wqi", [128, 4096], I32))
        swc = wctx.enter_context(nc.sbuf_tensor(pfx + "swc", [128, NCH], F32))
        swi = wctx.enter_context(nc.sbuf_tensor(pfx + "swi", [128, 1], F32))
        swinv = wctx.enter_context(nc.sbuf_tensor(pfx + "swinv", [128, NCH], F32))
        b2t = wctx.enter_context(nc.sbuf_tensor(pfx + "b2t", [128, NCH], F32))
        b2i = wctx.enter_context(nc.sbuf_tensor(pfx + "b2i", [128, NCH], I32))
        s_wn = [sem(f"s_wn{i}") for i in range(2)]

        def wloads(e):
            for dc in range(NCH):
                if dc >= 2:
                    e.wait_ge(s_wq.h, dc - 1)
                s_wn[dc % 2].dma(e.dma_start(out=wn[dc % 2][:], in_=w23[:, dc, :]))
        S("sync", wloads)

        for dc in range(NCH):
            def wquant(e, dc=dc):
                e.wait_ge(s_wn[dc % 2].h, (dc // 2 + 1) * 16)
                if dc >= 2:
                    e.wait_ge(s_wt.h, (dc - 1) * 32 * 16)
                tick(e, e.tensor_reduce(out=swc[:, dc:dc + 1], in_=wn[dc % 2][:], axis=mybir.AxisListType.X, op=ALU.max, apply_absolute_value=True))
                tick(e, e.tensor_scalar(out=swc[:, dc:dc + 1], in0=swc[:, dc:dc + 1], scalar1=EPS, scalar2=1.0 / 127.0, op0=ALU.max, op1=ALU.mult))
                tick(e, e.reciprocal(out=swi[:], in_=swc[:, dc:dc + 1]))
                e.tensor_scalar(out=wn[dc % 2][:], in0=wn[dc % 2][:], scalar1=swi[:], scalar2=None, op0=ALU.mult)
                e.tensor_copy(out=wqi[:], in_=wn[dc % 2][:])
                e.tensor_copy(out=wn[dc % 2][:], in_=wqi[:])
                ins = e.tensor_scalar(out=wq[dc % 2][:], in0=wn[dc % 2][:], scalar1=swc[:, dc:dc + 1], scalar2=None, op0=ALU.mult)
                s_wq.inc(ins)
            S("vector", wquant)
            WQ = s_wq.n

            def wtr(e, dc=dc, WQ=WQ):
                e.wait_ge(s_wq.h, WQ)
                for a in range(32):
                    s_wt.dma(e.dma_start(out=w2t_sb[:, a, dc * 128:(dc + 1) * 128], in_=wq[dc % 2][:, a * 128:(a + 1) * 128], transpose=True))
            S("scalar", wtr)

        WT_ALL = s_wt.n
        WQ_ALL = s_wq.n

        def b2qc(e, WQ_ALL=WQ_ALL, SCOL=SCOL):
            e.wait_ge(s_wq.h, WQ_ALL)
            e.wait_ge(s_cw.h, SCOL)
            tick(e, e.reciprocal(out=swinv[:], in_=swc[:]))
            tick(e, e.tensor_tensor(out=b2t[:], in0=b2q_sb[:], in1=swinv[:], op=ALU.mult))
            tick(e, e.tensor_scalar(out=b2t[:], in0=b2t[:], scalar1=rcol[:], scalar2=None, op0=ALU.mult))
            tick(e, e.tensor_copy(out=b2i[:], in_=b2t[:]))
            tick(e, e.tensor_copy(out=b2t[:], in_=b2i[:]))
            tick(e, e.tensor_tensor(out=b2t[:], in0=b2t[:], in1=swc[:], op=ALU.mult))
            ins = e.tensor_scalar(out=b2q_sb[:], in0=b2t[:], scalar1=scol[:], scalar2=None, op0=ALU.mult)
            s_hq.inc(ins)
        S("vector", b2qc)
        B2Q = s_hq.n
        wctx.close()

        hq0 = sb("hq0", [128, 32, TW], BF16)
        hq = [hq0, hq0]
        hf = [sb(f"hf{i}", [128, NCH, TW], F32) for i in range(2)]
        hqi = sb("hqi", [128, NCH, TW], I32)
        x3t = [sb(f"x3t{i}", [128, NCH, TW], F32) for i in range(2)]
        ot = sb("ot", [128, NCH, TW], F32)

        with nc.psum_tensor(pfx + "ps_a", [128, 512], F32) as ps_a, nc.psum_tensor(pfx + "ps_b", [128, 512], F32) as ps_b:
            ps_pp = [ps_a, ps_b]
            pp_ev = []
            pp_i = [0]

            def group(mm_emit, ev_emit, pre_mm=None):
                b = pp_i[0] % 2

                def mm(e, b=b, prev=(pp_ev[-2] if len(pp_ev) >= 2 else None)):
                    if pre_mm:
                        pre_mm(e)
                    if prev is not None:
                        e.wait_ge(s_ev.h, prev)
                    ins = mm_emit(e, ps_pp[b])
                    s_mm.inc(ins)
                S("tensor", mm)
                MM = s_mm.n

                def ev(e, b=b, MM=MM):
                    e.wait_ge(s_mm.h, MM)
                    ins = ev_emit(e, ps_pp[b])
                    s_ev.inc(ins)
                S("scalar", ev)
                pp_ev.append(s_ev.n)
                pp_i[0] += 1
                return s_ev.n

            hq_done = {}
            for j in range(NT):
                for fs in range(4):
                    fb = fs % 2

                    def hld(e, j=j, fs=fs, fb=fb, WT_ALL=WT_ALL, B2Q=B2Q):
                        if j == 0 and fs == 0:
                            e.wait_ge(s_wt.h, WT_ALL)
                            e.wait_ge(s_hq.h, B2Q)
                        k = j * 4 + fs
                        if k >= 2:
                            e.wait_ge(s_hq.h, B2Q + (k - 1) * 2)
                        s_ht[fb].dma(e.dma_start(out=hf[fb][:], in_=h3[:, fs * 8:(fs + 1) * 8, j * TW:(j + 1) * TW]))
                    S("sync", hld)
                    HT_K = s_ht[fb].n

                    def hquant(e, j=j, fs=fs, fb=fb, HT_K=HT_K):
                        e.wait_ge(s_ht[fb].h, HT_K)
                        if fs == 0 and j >= 1:
                            e.wait_ge(s_mm.h, j * 8)
                        e.tensor_scalar(out=hf[fb][:], in0=hf[fb][:], scalar1=rcol[:], scalar2=None, op0=ALU.mult)
                        e.tensor_copy(out=hqi[:], in_=hf[fb][:])
                        ins = e.tensor_copy(out=hq0[:, fs * 8:(fs + 1) * 8, :], in_=hqi[:])
                        s_hq.inc(ins, 2)
                    S("vector", hquant)
                HQ_J = s_hq.n
                hq_done[j] = HQ_J
                jb = j % 2

                def x3ld(e, j=j, jb=jb):
                    if j >= 2:
                        e.wait_ge(s_x4.h, j - 1)
                        e.wait_ge(s_st.h, (j - 1) * 16)
                    s_x3l[jb].dma(e.dma_start(out=x3t[jb][:], in_=x33[:, :, j * TW:(j + 1) * TW]))
                S("sync", x3ld)
                X3_J = s_x3l[jb].n

                for dc in range(NCH):
                    def omm(e, ps, dc=dc, HQ_J=HQ_J):
                        e.wait_ge(s_hq.h, HQ_J)
                        for a in range(32):
                            ins = e.matmul(ps[:], w2t_sb[:, a, dc * 128:(dc + 1) * 128], hq0[:, a, :], start=(a == 0), stop=(a == 31))
                        return ins

                    def oev(e, ps, j=j, dc=dc):
                        if dc == 0 and j >= 1:
                            e.wait_ge(s_x4.h, j)
                        return e.activation(out=ot[:, dc, :], in_=ps[:], func=AF.Identity, bias=b2q_sb[:, dc:dc + 1], scale=scol[:])

                    group(omm, oev)
                OEV_J = s_ev.n

                def resadd(e, jb=jb, OEV_J=OEV_J, X3_J=X3_J):
                    e.wait_ge(s_ev.h, OEV_J)
                    e.wait_ge(s_x3l[jb].h, X3_J)
                    if s_st.n:
                        e.wait_ge(s_st.h, s_st.n)
                    ins = e.tensor_tensor(out=x3t[jb][:], in0=x3t[jb][:], in1=ot[:], op=ALU.add)
                    s_x4.inc(ins)
                S("vector", resadd)
                X4_J = s_x4.n

                def store(e, j=j, jb=jb, X4_J=X4_J):
                    e.wait_ge(s_x4.h, X4_J)
                    s_st.dma(e.dma_start(out=xo3[:, :, j * TW:(j + 1) * TW], in_=x3t[jb][:]))
                S("gpsimd", store)

            def fin(e):
                e.wait_ge(s_st.h, s_st.n)
            S("sync", fin)
    return phase_terminals()


def qw2_inputs(h_fm, hm8, x3_fm, inp):
    return {
        "h": h_fm,
        "hm8": hm8.reshape(1, 8),
        "w2": inp["L1_ffn_w2"].copy(),
        "b2_r": np.ascontiguousarray(inp["L1_ffn_b2"].reshape(NCH, 128).T),
        "x3": x3_fm,
    }



import ml_dtypes as _mld


def _bits(a):
    if a.dtype == _mld.bfloat16:
        return a.view(np.uint16)
    return np.ascontiguousarray(a)


def _bf(a):
    if a.dtype == np.uint16:
        return a.view(_mld.bfloat16)
    return a


_CORES = list(range(8))




def _mk_base():
    nc = bass.Bass("TRN2", target_bir_lowering=False, debug=False)
    stk = ExitStack()
    block = stk.enter_context(nc.Block())
    return nc, stk, block


def build_L1():
    nc, stk, block = _mk_base()
    EI = lambda n, s, d: nc.dram_tensor(n, s, d, kind="ExternalInput").ap()
    EO = lambda n, s, d: nc.dram_tensor(n, s, d, kind="ExternalOutput").ap()
    IN = lambda n, s, d: nc.dram_tensor(n, s, d, kind="Internal").ap()
    io = {}
    io["xh"] = EI("xh", [E, NC_LOC * W], F32)
    for n, s in [("wqT", [E, E]), ("wkT", [E, E]), ("wvT", [E, E])]:
        io[n] = EI(n, s, F32)
    io["bq_r"] = EI("bq_r", [128, NCH], F32)
    io["bk_r"] = EI("bk_r", [128, NCH], F32)
    io["bv_row"] = EI("bv_row", [1, E], F32)
    io["ones_col"] = EI("ones_col", [128, 1], BF16)
    io["ones_row"] = EI("ones_row", [1, 128], BF16)
    io["ones_c16"] = EI("ones_c16", [128, 16], BF16)
    io["ones64f"] = EI("ones64f", [1, 64], F32)
    io["ek"] = EI("ek", [1, 2], F32)
    io["mlo"] = EI("mlo", [2, 128, W], F32)
    io["mhi"] = EI("mhi", [2, 128, W], F32)
    for n, s in [("u1", [E, 256]), ("v1", [256, 4096]), ("u2", [4096, 256]), ("v2", [256, E])]:
        io[n] = EI(n, s, F32)
    io["b1_r"] = EI("b1_r", [128, 32], F32)
    io["b2_r"] = EI("b2_r", [128, NCH], F32)
    io["win"] = EI("win", [E, 2048], F32)
    io["bin_r"] = EI("bin_r", [128, 16], F32)
    io["x1"] = IN("x1_i", [E, TOK], F32)
    io["x2"] = EO("x2", [E, TOK], F32)
    io["xg"] = EO("xg", [2048, TOK], F32)
    with stk:
        g = emit_l0attn(nc, block, stk, "a_", io, [])
        g = emit_l0ffn(nc, block, stk, "b_", io, g)
        g = emit_l1pre(nc, block, stk, "c_", io, g)
    return nc


def build_L2():
    nc, stk, block = _mk_base()
    EI = lambda n, s, d: nc.dram_tensor(n, s, d, kind="ExternalInput").ap()
    EO = lambda n, s, d: nc.dram_tensor(n, s, d, kind="ExternalOutput").ap()
    IN = lambda n, s, d: nc.dram_tensor(n, s, d, kind="Internal").ap()
    io = {}
    io["xs"] = EI("xs", [E, 8192], F32)
    io["cw"] = EI("cw", [128, NCH, 4], F32)
    io["cb"] = EI("cb", [128, NCH], F32)
    io["zer"] = EI("zer", [128, 4], F32)
    io["wx"] = EI("wx", [E, 32], F32)
    io["bx"] = EI("bx", [32, 1], F32)
    io["alane"] = EI("alane", [128, 64], F32)
    io["dcol"] = EI("dcol", [128, 4], F32)
    io["sel"] = EI("sel", [16, 128, 128], BF16)
    io["zcol"] = EI("zcol", [128, 1], F32)
    io["xcb"] = IN("xcb_i", [E, 8192], BF16)
    io["bc"] = IN("bc_i", [32, 8192], BF16)
    io["y"] = EO("y", [512, 8192], F32)
    io["ng"] = NCH
    io["ncols"] = 8192
    with stk:
        g = emit_conv(nc, block, stk, "d1_", io, [])
        g = emit_bc(nc, block, stk, "d2_", io, g)
        g = emit_scan(nc, block, stk, "d3_", io, g)
    return nc


def build_L3():
    nc, stk, block = _mk_base()
    EI = lambda n, s, d: nc.dram_tensor(n, s, d, kind="ExternalInput").ap()
    EO = lambda n, s, d: nc.dram_tensor(n, s, d, kind="ExternalOutput").ap()
    io = {}
    io["y"] = EI("y", [E, TOK], F32)
    io["gate"] = EI("gate", [E, TOK], F32)
    io["x2"] = EI("x2", [E, TOK], F32)
    io["wo"] = EI("wo", [E, E], F32)
    io["bo_r"] = EI("bo_r", [128, NCH], F32)
    io["ones_col"] = EI("ones_col", [128, 1], BF16)
    io["x3"] = EO("x3", [E, TOK], F32)
    io["xn4"] = EO("xn4", [E, TOK], F32)
    io["mx"] = EO("mx", [1, 1], F32)
    with stk:
        g = emit_l1post(nc, block, stk, "e_", io, [])
        g = emit_ln4(nc, block, stk, "f_", io, g)
    return nc


def build_L4():
    nc, stk, block = _mk_base()
    EI = lambda n, s, d: nc.dram_tensor(n, s, d, kind="ExternalInput").ap()
    EO = lambda n, s, d: nc.dram_tensor(n, s, d, kind="ExternalOutput").ap()
    io = {}
    io["xn4"] = EI("xn4", [E, TOK], F32)
    io["mx8"] = EI("mx8", [1, 8], F32)
    io["w1"] = EI("w1", [4096, E], F32)
    io["b1_r"] = EI("b1_r", [128, 32], F32)
    io["h"] = EO("h", [4096, TOK], F32)
    io["hm"] = EO("hm", [1, 1], F32)
    with stk:
        emit_qw1(nc, block, stk, "g_", io, [])
    return nc


def build_L5():
    nc, stk, block = _mk_base()
    EI = lambda n, s, d: nc.dram_tensor(n, s, d, kind="ExternalInput").ap()
    EO = lambda n, s, d: nc.dram_tensor(n, s, d, kind="ExternalOutput").ap()
    io = {}
    io["h"] = EI("h", [4096, TOK], F32)
    io["hm8"] = EI("hm8", [1, 8], F32)
    io["w2"] = EI("w2", [E, 4096], F32)
    io["b2_r"] = EI("b2_r", [128, NCH], F32)
    io["x3"] = EI("x3", [E, TOK], F32)
    io["x4"] = EO("x4", [E, TOK], F32)
    with stk:
        emit_qw2(nc, block, stk, "h_", io, [])
    return nc


def kernel(**inputs):
    inp = {k: np.asarray(v) for k, v in inputs.items()}
    x0 = inp["L0_attn_hidden_states"]
    xfm = [np.ascontiguousarray(x0[b].T) for b in range(4)]
    pairs = [(b, th) for b in range(4) for th in range(2)]

    nc = build_L1()
    in1 = []
    for (b, th) in pairs:
        m = l0attn_inputs(xfm[b], inp, th)
        m2 = l0ffn_inputs(np.zeros(1, np.float32), inp)
        m2.pop("x1")
        m.update(m2)
        m3 = l1pre_inputs(np.zeros(1, np.float32), inp)
        m3.pop("x2")
        m.update(m3)
        in1.append(m)
    res = run_bass_kernel_spmd(nc, in1, core_ids=_CORES)
    x2 = [res.results[c]["x2"] for c in range(8)]
    xg = [res.results[c]["xg"] for c in range(8)]

    nc = build_L2()
    in2 = []
    for (b, th) in pairs:
        full_ssm = np.concatenate([xg[2 * b][0:E, :], xg[2 * b + 1][0:E, :]], axis=1)
        permidx = np.concatenate([np.arange(th * 512, (th + 1) * 512), np.arange((1 - th) * 512, (2 - th) * 512)])
        m = {
            "xs": np.ascontiguousarray(full_ssm[permidx]),
            "cw": np.ascontiguousarray(inp["L1_attn_conv_w"][permidx].reshape(NCH, 128, 4).transpose(1, 0, 2)),
            "cb": np.ascontiguousarray(inp["L1_attn_conv_b"][permidx].reshape(NCH, 128).T),
            "zer": np.zeros((128, 4), np.float32),
            "wx": np.ascontiguousarray(inp["L1_attn_W_x"][permidx]),
            "bx": inp["L1_attn_b_x"].reshape(32, 1).copy(),
        }
        m4 = scan_inputs(np.zeros(1, np.float32), np.zeros(1, np.float32), inp, th)
        for k in ("xcb", "bc"):
            m4.pop(k)
        m.update(m4)
        in2.append(m)
    res = run_bass_kernel_spmd(nc, in2, core_ids=_CORES)
    yv = [res.results[c]["y"] for c in range(8)]

    nc = build_L3()
    in3 = []
    for (b, th) in pairs:
        y_full = np.concatenate([yv[2 * b], yv[2 * b + 1]], axis=0)
        m = l1post_inputs(np.ascontiguousarray(y_full[:, th * 4096:(th + 1) * 4096]),
                          np.ascontiguousarray(xg[2 * b + th][E:2 * E, :]),
                          x2[2 * b + th], inp)
        m5 = ln4_inputs(np.zeros(1, np.float32))
        m5.pop("x3")
        m.update(m5)
        in3.append(m)
    res = run_bass_kernel_spmd(nc, in3, core_ids=_CORES)
    x3 = [res.results[c]["x3"] for c in range(8)]
    xn4 = [res.results[c]["xn4"] for c in range(8)]
    mx8 = np.array([float(res.results[c]["mx"][0, 0]) for c in range(8)], np.float32)

    nc = build_L4()
    res = run_bass_kernel_spmd(nc, [qw1_inputs(xn4[c], mx8, inp) for c in range(8)], core_ids=_CORES)
    hh = [res.results[c]["h"] for c in range(8)]
    hm8 = np.array([float(res.results[c]["hm"][0, 0]) for c in range(8)], np.float32)

    nc = build_L5()
    res = run_bass_kernel_spmd(nc, [qw2_inputs(hh[c], hm8, x3[c], inp) for c in range(8)], core_ids=_CORES)
    x4 = [res.results[c]["x4"] for c in range(8)]

    out = np.empty((4, 8192, 1024), np.float32)
    for b in range(4):
        for th in range(2):
            out[b, th * 4096:(th + 1) * 4096, :] = x4[2 * b + th].T
    return out


# version 03 (final)
# speedup vs baseline: 1.0531x; 1.0531x over previous
from concourse.bass_utils import run_bass_kernel_spmd

import numpy as np
import concourse.bass as bass
import concourse.mybir as mybir
from contextlib import ExitStack

F32 = mybir.dt.float32
BF16 = mybir.dt.bfloat16
I32 = mybir.dt.int32
AF = mybir.ActivationFunctionType
ALU = mybir.AluOpType

E = 1024
NCH = 8
W = 256
H = 16
DH = 64
TOK = 4096
NC_OWN = 16
NC_LOC = 18
NEG = -1.0e30


class Sem:
    def __init__(self, ctx, nc, name):
        self.h = ctx.enter_context(nc.semaphore(name))
        self.n = 0

    def inc(self, ins, k=1):
        ins.then_inc(self.h, k)
        self.n += k
        return self.n

    def dma(self, ins):
        ins.then_inc(self.h, 16)
        self.n += 16
        return self.n


def emit_l0attn(nc, block, semctx, pfx, io, gate):
    xh = io["xh"]
    wqT = io["wqT"]
    wkT = io["wkT"]
    wvT = io["wvT"]
    bq_r = io["bq_r"]
    bk_r = io["bk_r"]
    bv_row = io["bv_row"]
    ones_col = io["ones_col"]
    ones_row = io["ones_row"]
    ones_c16 = io["ones_c16"]
    ones64f = io["ones64f"]
    ek = io["ek"]
    mlo = io["mlo"]
    mhi = io["mhi"]
    x1_out = io["x1"]
    lnrows = nc.dram_tensor(pfx + "lnrows", [NC_LOC, 2, W], F32, kind="Internal").ap()

    xh3 = xh.rearrange("(c p) t -> p c t", p=128)
    x1o3 = x1_out.rearrange("(c p) t -> p c t", p=128)

    with ExitStack() as ctx:
        emit_gate(block, gate)

        def sb(name, shape, dt):
            return ctx.enter_context(nc.sbuf_tensor(pfx + name, shape, dt))

        def S(eng, fn):
            getattr(block, eng)(fn)

        def sem(name):
            s = Sem(semctx, nc, pfx + name)
            PHASE_SEMS.append(s)
            return s

        wq_sb = sb("wq_sb", [128, NCH, E], BF16)
        wk_sb = sb("wk_sb", [128, NCH, E], BF16)
        wv_sb = sb("wv_sb", [128, NCH, E], BF16)
        wtmp = [sb(f"wtmp{i}", [128, E], F32) for i in range(2)]
        bqs_sb = sb("bqs_sb", [128, NCH], F32)
        bk_sb = sb("bk_sb", [128, NCH], F32)
        bvb_sb = sb("bvb_sb", [1, E], BF16)
        bvf_sb = sb("bvf_sb", [1, E], F32)
        onec_sb = sb("onec_sb", [128, 1], BF16)
        oner_sb = sb("oner_sb", [1, 128], BF16)
        onec16_sb = sb("onec16_sb", [128, 16], BF16)
        ones64_sb = sb("ones64_sb", [1, 64], F32)
        e0_sb = sb("e0_sb", [128, 1], F32)
        e1_sb = sb("e1_sb", [128, 1], F32)
        mlo_sb = sb("mlo_sb", [128, 2, W], F32)
        mhi_sb = sb("mhi_sb", [128, 2, W], F32)
        k_ring = sb("k_ring", [128, 4, NCH, W], BF16)
        v_ring = sb("v_ring", [128, 4, 2, H, 65], BF16)
        st_row = [sb(f"st_row{i}", [1, W], F32) for i in range(6)]
        mr_row = sb("mr_row", [1, W], F32)
        xt = [sb(f"xt{i}", [128, NCH, W], F32) for i in range(2)]
        xres = [sb(f"xres{i}", [128, NCH, W], F32) for i in range(2)]
        xsq = sb("xsq", [128, NCH, W], BF16)
        xbq = sb("xbq", [128, NCH, W], BF16)
        rstd_rep = [sb(f"rstd_rep{i}", [128, W], F32) for i in range(2)]
        mr_rep = [sb(f"mr_rep{i}", [128, W], F32) for i in range(2)]
        tnorm = sb("tnorm", [128, NCH, W], F32)
        xn = [sb(f"xn{i}", [128, NCH, W], BF16) for i in range(2)]
        qt = [sb(f"qt{i}", [128, NCH, W], BF16) for i in range(2)]
        probs = [sb(f"probs{i}", [128, 3 * W], BF16) for i in range(2)]
        rz = [sb(f"rz{i}", [1, W], F32) for i in range(2)]
        zrep = [sb(f"zrep{i}", [128, W], F32) for i in range(2)]
        t64 = [sb(f"t64{i}", [128, W], F32) for i in range(2)]
        x1t = [sb(f"x1t{i}", [128, NCH, W], F32) for i in range(2)]

        s_ldc = sem("s_ldc")
        s_wl = [sem(f"s_wl{i}") for i in range(2)]
        s_xt = [sem(f"s_xt{i}") for i in range(2)]
        s_lr = [sem(f"s_lr{i}") for i in range(2)]
        s_xr = [sem(f"s_xr{i}") for i in range(2)]
        s_rw = sem("s_rw")
        s_st1 = sem("s_st1")
        s_w = sem("s_w")
        s_cst = sem("s_cst")
        s_csv = sem("s_csv")
        s_st = sem("s_st")
        s_row = sem("s_row")
        s_norm = sem("s_norm")
        s_pkv = sem("s_pkv")
        s_ev = sem("s_ev")
        s_sc = sem("s_sc")
        s_msk = sem("s_msk")
        s_exp = sem("s_exp")
        s_av = sem("s_av")
        s_rz = sem("s_rz")
        s_t64 = sem("s_t64")
        s_x1 = sem("s_x1")
        s_tk = sem("s_tk")

        def init_loads(e):
            for dst, src in [(bk_sb[:], bk_r), (bqs_sb[:], bq_r), (bvf_sb[:], bv_row),
                             (onec_sb[:], ones_col), (oner_sb[:], ones_row),
                             (onec16_sb[:], ones_c16), (ones64_sb[:], ones64f)]:
                s_ldc.dma(e.dma_start(out=dst, in_=src))
            s_ldc.dma(e.dma_start(out=e0_sb[:], in_=ek[0:1, 0:1].partition_broadcast(128)))
            s_ldc.dma(e.dma_start(out=e1_sb[:], in_=ek[0:1, 1:2].partition_broadcast(128)))
            for si in range(2):
                s_ldc.dma(e.dma_start(out=mlo_sb[:, si, :], in_=mlo[si]))
                s_ldc.dma(e.dma_start(out=mhi_sb[:, si, :], in_=mhi[si]))
        S("sync", init_loads)
        LDC = s_ldc.n

        w3 = [(wqT, wq_sb), (wkT, wk_sb), (wvT, wv_sb)]

        def wload(e):
            for wi, (wsrc, _) in enumerate(w3):
                ws = wsrc.rearrange("(c p) o -> p c o", p=128)
                for c in range(NCH):
                    idx = wi * NCH + c
                    if idx >= 2:
                        e.wait_ge(s_w.h, idx - 1)
                    s_wl[idx % 2].dma(e.dma_start(out=wtmp[idx % 2][:], in_=ws[:, c, :]))
        S("scalar", wload)

        def wcast(e):
            for wi, (_, wdst) in enumerate(w3):
                for c in range(NCH):
                    idx = wi * NCH + c
                    e.wait_ge(s_wl[idx % 2].h, (idx // 2 + 1) * 16)
                    ins = e.tensor_copy(out=wdst[:, c, :], in_=wtmp[idx % 2][:])
                    s_w.inc(ins)
        S("vector", wcast)

        def bias_prep(e):
            e.wait_ge(s_ldc.h, LDC)
            e.tensor_copy(out=bvb_sb[:], in_=bvf_sb[:])
            ins = None
            for sl in range(4):
                for hf in range(2):
                    ins = e.tensor_copy(out=v_ring[:, sl, hf, :, 64], in_=onec16_sb[:])
            s_w.inc(ins)
        S("vector", bias_prep)
        W_DONE = s_w.n

        with nc.psum_tensor(pfx + "ps_s1", [1, W], F32) as ps_s1, nc.psum_tensor(pfx + "ps_s2", [1, W], F32) as ps_s2:
            for j in range(NC_LOC):
                jb = j % 2

                def ld(e, j=j, jb=jb):
                    if j >= 2:
                        e.wait_ge(s_st.h, (j - 1) * 2)
                    s_xt[jb].dma(e.dma_start(out=xt[jb][:], in_=xh3[:, :, j * W:(j + 1) * W]))
                S("sync", ld)
                XT_J = s_xt[jb].n

                def casts(e, j=j, jb=jb, XT_J=XT_J):
                    e.wait_ge(s_xt[jb].h, XT_J)
                    if j >= 1:
                        e.wait_ge(s_st.h, j * 2)
                    ins = e.tensor_tensor(out=xsq[:], in0=xt[jb][:], in1=xt[jb][:], op=ALU.mult)
                    s_cst.inc(ins)
                S("vector", casts)

                def castb(e, j=j, jb=jb, XT_J=XT_J):
                    e.wait_ge(s_xt[jb].h, XT_J)
                    if j >= 1:
                        e.wait_ge(s_st.h, j * 2)
                    if j == 0:
                        e.wait_ge(s_w.h, W_DONE)
                    ins = e.tensor_copy(out=xbq[:], in_=xt[jb][:])
                    s_csv.inc(ins)
                S("vector", castb)

                def stats(e, j=j, CST=s_cst.n, CSV=s_csv.n):
                    e.wait_ge(s_cst.h, CST)
                    e.wait_ge(s_csv.h, CSV)
                    if j >= 1:
                        e.wait_ge(s_row.h, (j - 1) * 3 + 1)
                    for c in range(NCH):
                        e.matmul(ps_s1[:], onec_sb[:], xbq[:, c, :], start=(c == 0), stop=(c == NCH - 1))
                    for c in range(NCH):
                        ins = e.matmul(ps_s2[:], onec_sb[:], xsq[:, c, :], start=(c == 0), stop=(c == NCH - 1))
                    s_st.inc(ins, 2)
                S("tensor", stats)

                def rowmath(e, ST=s_st.n, RALL=s_row.n, base=s_tk.n):
                    e.wait_ge(s_st.h, ST)
                    if RALL:
                        e.wait_ge(s_row.h, RALL)
                    k = base
                    ins = e.tensor_scalar(out=st_row[0][:], in0=ps_s1[:], scalar1=1.0 / E, scalar2=None, op0=ALU.mult)
                    ins.then_inc(s_tk.h, 1); k += 1
                    ins = e.tensor_scalar(out=st_row[1][:], in0=ps_s2[:], scalar1=1.0 / E, scalar2=None, op0=ALU.mult)
                    ins.then_inc(s_tk.h, 1); k += 1
                    e.wait_ge(s_tk.h, k)
                    ins = e.tensor_tensor(out=st_row[2][:], in0=st_row[0][:], in1=st_row[0][:], op=ALU.mult)
                    ins.then_inc(s_tk.h, 1); k += 1
                    e.wait_ge(s_tk.h, k)
                    ins = e.tensor_tensor(out=st_row[3][:], in0=st_row[1][:], in1=st_row[2][:], op=ALU.subtract)
                    ins.then_inc(s_tk.h, 1); k += 1
                    e.wait_ge(s_tk.h, k)
                    ins = e.tensor_scalar(out=st_row[3][:], in0=st_row[3][:], scalar1=1e-5, scalar2=None, op0=ALU.add)
                    ins.then_inc(s_tk.h, 1); k += 1
                    e.wait_ge(s_tk.h, k)
                    ins = e.reciprocal(out=st_row[4][:], in_=st_row[3][:])
                    s_row.inc(ins)
                    s_tk.n = k
                S("vector", rowmath)

                def rsqrt(e, RM=s_row.n, RW=s_rw.n):
                    e.wait_ge(s_row.h, RM)
                    if RW:
                        e.wait_ge(s_rw.h, RW)
                    ins = e.activation(out=st_row[5][:], in_=st_row[4][:], func=AF.Sqrt)
                    s_row.inc(ins)
                S("scalar", rsqrt)

                def mrmul(e, RS=s_row.n):
                    e.wait_ge(s_row.h, RS)
                    if s_rw.n > 0:
                        e.wait_ge(s_rw.h, s_rw.n)
                    ins = e.tensor_tensor(out=mr_row[:], in0=st_row[0][:], in1=st_row[5][:], op=ALU.mult)
                    s_row.inc(ins)
                S("vector", mrmul)

                def rowwrite(e, j=j, MR=s_row.n):
                    e.wait_ge(s_row.h, MR)
                    s_rw.dma(e.dma_start(out=lnrows[j, 0:1, :], in_=mr_row[:]))
                    s_rw.dma(e.dma_start(out=lnrows[j, 1:2, :], in_=st_row[5][:]))
                S("gpsimd", rowwrite)

        A1_RW = s_rw.n
        A1_ST = s_st.n

        with nc.psum_tensor(pfx + "ps_sc0", [128, 1024], F32) as ps_sc0, \
             nc.psum_tensor(pfx + "ps_sc1", [128, 1024], F32) as ps_sc1, \
             nc.psum_tensor(pfx + "ps_p0", [128, 512], F32) as ps_p0, \
             nc.psum_tensor(pfx + "ps_p1", [128, 512], F32) as ps_p1, \
             nc.psum_tensor(pfx + "ps_o0", [128, 512], F32) as ps_o0, \
             nc.psum_tensor(pfx + "ps_o1", [128, 512], F32) as ps_o1:
            ps_sc = [ps_sc0, ps_sc1]
            ps_p = [ps_p0, ps_p1]
            ps_o = [ps_o0, ps_o1]
            pp_ev = []
            pp_i = [0]
            norm_done = {}
            exp_at = [0, 0]
            av_at = [0, 0]
            t64_at = [0, 0]
            ev_at_zrep = [0, 0]

            def qkv_group(mm_emit, ev_emit, pre_mm=None):
                b = pp_i[0] % 2

                def mm(e, b=b, prev=(pp_ev[-2] if len(pp_ev) >= 2 else None)):
                    if pre_mm:
                        pre_mm(e)
                    if prev is not None:
                        e.wait_ge(s_ev.h, prev)
                    ins = mm_emit(e, ps_p[b])
                    s_pkv.inc(ins)
                S("tensor", mm)
                PKV = s_pkv.n

                def ev(e, b=b, PKV=PKV):
                    e.wait_ge(s_pkv.h, PKV)
                    ins = ev_emit(e, ps_p[b])
                    s_ev.inc(ins)
                S("vector", ev)
                pp_ev.append(s_ev.n)
                pp_i[0] += 1
                return s_ev.n

            for j in range(NC_LOC):
                jb = j % 2
                sl = j % 4
                c_att = j - 2

                def ld(e, j=j, jb=jb, c_att=c_att):
                    if j < 2:
                        e.wait_ge(s_st.h, A1_ST)
                    else:
                        e.wait_ge(s_norm.h, norm_done[j - 2])
                    s_xt[jb].dma(e.dma_start(out=xt[jb][:], in_=xh3[:, :, j * W:(j + 1) * W]))
                    e.wait_ge(s_rw.h, A1_RW)
                    s_lr[jb].dma(e.dma_start(out=rstd_rep[jb][:], in_=lnrows[j, 1:2, :].partition_broadcast(128)))
                    s_lr[jb].dma(e.dma_start(out=mr_rep[jb][:], in_=lnrows[j, 0:1, :].partition_broadcast(128)))
                    if 0 <= c_att < NC_OWN:
                        if c_att >= 2:
                            e.wait_ge(s_st1.h, (c_att - 1) * 16)
                        s_xr[c_att % 2].dma(e.dma_start(out=xres[c_att % 2][:], in_=xh3[:, :, (c_att + 1) * W:(c_att + 2) * W]))
                S("sync", ld)
                XT_J = s_xt[jb].n
                LR_J = s_lr[jb].n
                XR_C = s_xr[c_att % 2].n if 0 <= c_att < NC_OWN else 0

                def norm(e, jb=jb, XT_J=XT_J, LR_J=LR_J):
                    e.wait_ge(s_xt[jb].h, XT_J)
                    e.wait_ge(s_lr[jb].h, LR_J)
                    e.wait_ge(s_pkv.h, s_pkv.n)
                    e.tensor_tensor(out=tnorm[:], in0=xt[jb][:], in1=rstd_rep[jb][:].unsqueeze(1).broadcast_to([128, NCH, W]), op=ALU.mult)
                    ins = e.tensor_tensor(out=xn[jb][:], in0=tnorm[:], in1=mr_rep[jb][:].unsqueeze(1).broadcast_to([128, NCH, W]), op=ALU.subtract)
                    s_norm.inc(ins, 2)
                S("vector", norm)
                NORM_J = s_norm.n
                norm_done[j] = s_norm.n

                def ring_guard(e, j=j):
                    cprev = j - 4
                    if 0 <= cprev < NC_OWN:
                        e.wait_ge(s_av.h, (cprev + 1) * H * 6)

                for dc in range(NCH):
                    def kmm(e, ps, jb=jb, dc=dc):
                        for a in range(NCH):
                            ins = e.matmul(ps[:, 0:W], wk_sb[:, a, dc * 128:(dc + 1) * 128], xn[jb][:, a, :], start=(a == 0), stop=(a == NCH - 1))
                        return ins

                    def kev(e, ps, sl=sl, dc=dc, j=j):
                        ring_guard(e, j)
                        return e.tensor_scalar(out=k_ring[:, sl, dc, :], in0=ps[:, 0:W], scalar1=bk_sb[:, dc:dc + 1], scalar2=None, op0=ALU.add)

                    qkv_group(kmm, kev, pre_mm=(lambda e, NORM_J=NORM_J: e.wait_ge(s_norm.h, NORM_J)))

                for hf in range(2):
                    for es in range(2):
                        def vmm(e, ps, jb=jb, hf=hf, es=es):
                            for a in range(NCH):
                                e.matmul(ps[:], xn[jb][:, a, hf * 128:(hf + 1) * 128], wv_sb[:, a, es * 512:(es + 1) * 512], start=(a == 0), stop=False)
                            return e.matmul(ps[:], oner_sb[:], bvb_sb[:, es * 512:(es + 1) * 512], start=False, stop=True)

                        def vev(e, ps, sl=sl, hf=hf, es=es, j=j):
                            ring_guard(e, j)
                            dst = v_ring[:, sl, hf, es * 8:(es + 1) * 8, 0:64]
                            return e.tensor_copy(out=dst, in_=ps[:].rearrange("p (h o) -> p h o", o=64))

                        qkv_group(vmm, vev, pre_mm=(lambda e, NORM_J=NORM_J: e.wait_ge(s_norm.h, NORM_J)))

                EDGE_N = None
                if j == 0 or j == NC_LOC - 1:
                    ecol = e0_sb if j == 0 else e1_sb
                    EV_J = s_ev.n

                    def edge(e, sl=sl, ecol=ecol, EV_J=EV_J):
                        e.wait_ge(s_ev.h, EV_J)
                        e.tensor_scalar(out=k_ring[:, sl, :, :], in0=k_ring[:, sl, :, :], scalar1=ecol[:], scalar2=None, op0=ALU.mult)
                        ins = e.tensor_scalar(out=v_ring[:, sl, :, :, 0:64], in0=v_ring[:, sl, :, :, 0:64], scalar1=ecol[:], scalar2=None, op0=ALU.mult)
                        s_norm.inc(ins, 2)
                    S("vector", edge)
                    EDGE_N = s_norm.n

                cq = j - 1
                if 0 <= cq < NC_OWN:
                    for dc in range(NCH):
                        def qmm(e, ps, jq=jb, dc=dc):
                            for a in range(NCH):
                                ins = e.matmul(ps[:, 0:W], wq_sb[:, a, dc * 128:(dc + 1) * 128], xn[jq][:, a, :], start=(a == 0), stop=(a == NCH - 1))
                            return ins

                        def qev(e, ps, cq=cq, dc=dc):
                            if cq >= 2:
                                e.wait_ge(s_sc.h, (cq - 1) * H * 6)
                            return e.tensor_scalar(out=qt[cq % 2][:, dc, :], in0=ps[:, 0:W], scalar1=bqs_sb[:, dc:dc + 1], scalar2=0.125, op0=ALU.add, op1=ALU.mult)

                        qkv_group(qmm, qev)
                Q_EV = s_ev.n

                if 0 <= c_att < NC_OWN:
                    c = c_att
                    cb = c % 2
                    for h in range(H):
                        hb = h % 2
                        hc = h // 2
                        hr = 64 * (h % 2)
                        for g in range(2):
                            def smm(e, c=c, cb=cb, h=h, hc=hc, hr=hr, g=g, Q_EV=Q_EV, EDGE_N=EDGE_N, EXPG=exp_at[g]):
                                e.wait_ge(s_ev.h, Q_EV)
                                if EDGE_N is not None:
                                    e.wait_ge(s_norm.h, EDGE_N)
                                if EXPG:
                                    e.wait_ge(s_exp.h, EXPG)
                                for si in range(3):
                                    s = 3 * g + si
                                    slj = (c + s // 2) % 4
                                    half = s % 2
                                    ins = e.matmul(
                                        ps_sc[g][:, si * W:(si + 1) * W],
                                        k_ring[hr:hr + 64, slj, hc, half * 128:(half + 1) * 128],
                                        qt[cb][hr:hr + 64, hc, :],
                                        start=True, stop=True)
                                s_sc.inc(ins, 3)
                            S("tensor", smm)
                            SC_H = s_sc.n

                            need_mask = (c == 0 and g == 0) or (c == NC_OWN - 1 and g == 1)
                            if need_mask:
                                def mask(e, c=c, g=g, SC_H=SC_H):
                                    e.wait_ge(s_sc.h, SC_H)
                                    if c == 0:
                                        for si in range(2):
                                            ins = e.tensor_tensor(out=ps_sc[0][:, si * W:(si + 1) * W], in0=ps_sc[0][:, si * W:(si + 1) * W], in1=mlo_sb[:, si, :], op=ALU.add)
                                    else:
                                        for si in range(2):
                                            ins = e.tensor_tensor(out=ps_sc[1][:, (1 + si) * W:(2 + si) * W], in0=ps_sc[1][:, (1 + si) * W:(2 + si) * W], in1=mhi_sb[:, si, :], op=ALU.add)
                                    s_msk.inc(ins, 2)
                                S("vector", mask)
                            MSK_H = s_msk.n

                            def expf(e, g=g, SC_H=SC_H, MSK_H=MSK_H, need_mask=need_mask, AVG=av_at[g]):
                                e.wait_ge(s_sc.h, SC_H)
                                if need_mask:
                                    e.wait_ge(s_msk.h, MSK_H)
                                if AVG:
                                    e.wait_ge(s_av.h, AVG)
                                ins = e.activation(out=probs[g][:], in_=ps_sc[g][:, 0:3 * W], func=AF.Exp)
                                s_exp.inc(ins)
                            S("scalar", expf)
                            exp_at[g] = s_exp.n

                            def avmm(e, c=c, h=h, hb=hb, g=g, EXP_G=s_exp.n, T64B=t64_at[hb]):
                                e.wait_ge(s_exp.h, EXP_G)
                                if g == 0 and T64B:
                                    e.wait_ge(s_t64.h, T64B)
                                for si in range(3):
                                    s = 3 * g + si
                                    slj = (c + s // 2) % 4
                                    half = s % 2
                                    ins = e.matmul(
                                        ps_o[hb][0:65, 0:W],
                                        v_ring[:, slj, half, h, :],
                                        probs[g][:, si * W:(si + 1) * W],
                                        start=(s == 0), stop=(s == 5))
                                s_av.inc(ins, 3)
                            S("tensor", avmm)
                            av_at[g] = s_av.n
                        AV_H = s_av.n

                        def recip(e, hb=hb, AV_H=AV_H, EVZ=ev_at_zrep[hb]):
                            e.wait_ge(s_av.h, AV_H)
                            if EVZ:
                                e.wait_ge(s_ev.h, EVZ)
                            ins = e.reciprocal(out=rz[hb][:], in_=ps_o[hb][64:65, 0:W])
                            s_rz.inc(ins)
                        S("vector", recip)
                        RZ_H = s_rz.n

                        def zmm(e, ps, hb=hb, RZ_H=RZ_H):
                            e.wait_ge(s_rz.h, RZ_H)
                            return e.matmul(ps[0:64, 0:W], ones64_sb[:], rz[hb][:], start=True, stop=True)

                        def zev(e, ps, hb=hb, hr=hr, T64B=t64_at[hb]):
                            if T64B:
                                e.wait_ge(s_t64.h, T64B)
                            return e.tensor_copy(out=zrep[hb][hr:hr + 64, :], in_=ps[0:64, 0:W])

                        ev_at_zrep[hb] = qkv_group(zmm, zev)
                        ZEV_H = s_ev.n

                        def t64mul(e, hb=hb, hr=hr, ZEV_H=ZEV_H):
                            e.wait_ge(s_ev.h, ZEV_H)
                            ins = e.tensor_tensor(out=t64[hb][hr:hr + 64, :], in0=ps_o[hb][0:64, 0:W], in1=zrep[hb][hr:hr + 64, :], op=ALU.mult)
                            s_t64.inc(ins)
                        S("vector", t64mul)
                        t64_at[hb] = s_t64.n
                        T64_H = s_t64.n

                        def xadd(e, c=c, hb=hb, hc=hc, hr=hr, T64_H=T64_H, XR_C=XR_C):
                            e.wait_ge(s_t64.h, T64_H)
                            e.wait_ge(s_xr[c % 2].h, XR_C)
                            ins = e.tensor_tensor(out=x1t[c % 2][hr:hr + 64, hc, :], in0=xres[c % 2][hr:hr + 64, hc, :], in1=t64[hb][hr:hr + 64, :], op=ALU.add)
                            s_x1.inc(ins)
                        S("gpsimd", xadd)

                    X1_C = s_x1.n

                    def store(e, c=c, X1_C=X1_C):
                        e.wait_ge(s_x1.h, X1_C)
                        s_st1.dma(e.dma_start(out=x1o3[:, :, c * W:(c + 1) * W], in_=x1t[c % 2][:]))
                    S("gpsimd", store)

            def fin(e):
                e.wait_ge(s_st1.h, s_st1.n)
            S("sync", fin)

    return phase_terminals()



def l0attn_inputs(xfull_fm_b, inp, th):
    import ml_dtypes
    x = xfull_fm_b
    cols = np.zeros((E, NC_LOC * W), np.float32)
    lo = th * TOK - W
    for j in range(NC_LOC):
        g0 = lo + j * W
        if 0 <= g0 and g0 + W <= 8192:
            cols[:, j * W:(j + 1) * W] = x[:, g0:g0 + W]
    m = {
        "xh": cols,
        "wqT": np.ascontiguousarray(inp["L0_attn_Wq"].T),
        "wkT": np.ascontiguousarray(inp["L0_attn_Wk"].T),
        "wvT": np.ascontiguousarray(inp["L0_attn_Wv"].T),
        "bq_r": np.ascontiguousarray(inp["L0_attn_bq"].reshape(NCH, 128).T),
        "bk_r": np.ascontiguousarray(inp["L0_attn_bk"].reshape(NCH, 128).T),
        "bv_row": inp["L0_attn_bv"].reshape(1, E).copy(),
        "ones_col": np.ones((128, 1), ml_dtypes.bfloat16).view(np.uint16),
        "ones_row": np.ones((1, 128), ml_dtypes.bfloat16).view(np.uint16),
        "ones_c16": np.ones((128, 16), ml_dtypes.bfloat16).view(np.uint16),
        "ones64f": np.ones((1, 64), np.float32),
        "ek": np.array([[0.0 if th == 0 else 1.0, 1.0 if th == 0 else 0.0]], np.float32),
    }
    mlo = np.zeros((2, 128, W), np.float32)
    mhi = np.zeros((2, 128, W), np.float32)
    rr = np.arange(128)[:, None]
    ii = np.arange(W)[None, :]
    if th == 0:
        mlo[0] = np.where(rr + ii < 256, NEG, 0.0)
        mlo[1] = np.where(rr + ii < 128, NEG, 0.0)
    if th == 1:
        mhi[0] = np.where(rr + ii >= 255, NEG, 0.0)
        mhi[1] = np.where(rr + ii >= 127, NEG, 0.0)
    m["mlo"] = mlo
    m["mhi"] = mhi
    return m



PHASE_SEMS = []


class Env:
    def __init__(self, nc, ctx, block, semctx=None, pfx=""):
        self.nc, self.ctx, self.block = nc, ctx, block
        self.semctx = semctx if semctx is not None else ctx
        self.pfx = pfx

    def sb(self, name, shape, dt):
        return self.ctx.enter_context(self.nc.sbuf_tensor(self.pfx + name, shape, dt))

    def sem(self, name):
        s = Sem(self.semctx, self.nc, self.pfx + name)
        PHASE_SEMS.append(s)
        return s

    def S(self, eng, fn):
        getattr(self.block, eng)(fn)


def emit_gate(block, gate):
    if not gate:
        return
    for _eng in ("sync", "scalar", "vector", "tensor", "gpsimd"):
        def _g(e, gate=gate):
            for (s, v) in gate:
                e.wait_ge(s, v)
        getattr(block, _eng)(_g)


def phase_terminals():
    terms = [(s.h, s.n) for s in PHASE_SEMS if s.n > 0]
    PHASE_SEMS.clear()
    return terms


def emit_ln_stats(env, src3, ntiles, TW, lnrows, nch=NCH, pre_wait=None):
    nc, S, sem = env.nc, env.S, env.sem
    lctx = ExitStack()
    def sb(name, shape, dt):
        return lctx.enter_context(nc.sbuf_tensor(env.pfx + name, shape, dt))
    _lnxt = sb("lnxt", [128, nch, TW], F32)
    xt = [_lnxt, _lnxt]
    xsq = sb("lnxsq", [128, nch, TW], BF16)
    xbq = sb("lnxbq", [128, nch, TW], BF16)
    onec = env.onec_sb
    _rows = [sb(f"lnrow{i}", [1, TW], F32) for i in range(4)]
    st_row = [_rows[0], _rows[1], _rows[2], _rows[3], _rows[3], _rows[2]]
    mr_row = sb("lnmr", [1, TW], F32)
    s_xt = [sem(f"lns_xt{i}") for i in range(2)]
    s_cst = sem("lns_cst")
    s_csv = sem("lns_csv")
    s_st = sem("lns_st")
    s_row = sem("lns_row")
    s_rw = sem("lns_rw")
    s_tk = sem("lns_tk")
    with nc.psum_tensor(env.pfx + "lnps1", [1, TW], F32) as ps1, nc.psum_tensor(env.pfx + "lnps2", [1, TW], F32) as ps2:
        for j in range(ntiles):
            jb = j % 2

            def ld(e, j=j, jb=jb):
                if j == 0 and pre_wait is not None:
                    pre_wait(e)
                if j >= 1:
                    e.wait_ge(s_st.h, j * 2)
                s_xt[jb].dma(e.dma_start(out=xt[jb][:], in_=src3[:, :, j * TW:(j + 1) * TW]))
            S("sync", ld)
            XT_J = s_xt[jb].n

            def casts(e, j=j, jb=jb, XT_J=XT_J):
                e.wait_ge(s_xt[jb].h, XT_J)
                if j >= 1:
                    e.wait_ge(s_st.h, j * 2)
                ins = e.tensor_tensor(out=xsq[:], in0=xt[jb][:], in1=xt[jb][:], op=ALU.mult)
                s_cst.inc(ins)
            S("vector", casts)

            def castb(e, j=j, jb=jb, XT_J=XT_J):
                e.wait_ge(s_xt[jb].h, XT_J)
                if j >= 1:
                    e.wait_ge(s_st.h, j * 2)
                if j == 0 and pre_wait is not None:
                    pre_wait(e)
                ins = e.tensor_copy(out=xbq[:], in_=xt[jb][:])
                s_csv.inc(ins)
            S("vector", castb)

            def stats(e, j=j, CST=s_cst.n, CSV=s_csv.n):
                e.wait_ge(s_cst.h, CST)
                e.wait_ge(s_csv.h, CSV)
                if j >= 1:
                    e.wait_ge(s_row.h, (j - 1) * 3 + 1)
                for c in range(nch):
                    e.matmul(ps1[:], onec[:], xbq[:, c, :], start=(c == 0), stop=(c == nch - 1))
                for c in range(nch):
                    ins = e.matmul(ps2[:], onec[:], xsq[:, c, :], start=(c == 0), stop=(c == nch - 1))
                s_st.inc(ins, 2)
            S("tensor", stats)

            def rowmath(e, ST=s_st.n, RALL=s_row.n, RW=s_rw.n, base=s_tk.n):
                e.wait_ge(s_st.h, ST)
                if RALL:
                    e.wait_ge(s_row.h, RALL)
                if RW:
                    e.wait_ge(s_rw.h, RW)
                k = base
                ins = e.tensor_scalar(out=st_row[0][:], in0=ps1[:], scalar1=1.0 / (128 * nch), scalar2=None, op0=ALU.mult)
                ins.then_inc(s_tk.h, 1); k += 1
                ins = e.tensor_scalar(out=st_row[1][:], in0=ps2[:], scalar1=1.0 / (128 * nch), scalar2=None, op0=ALU.mult)
                ins.then_inc(s_tk.h, 1); k += 1
                e.wait_ge(s_tk.h, k)
                ins = e.tensor_tensor(out=st_row[2][:], in0=st_row[0][:], in1=st_row[0][:], op=ALU.mult)
                ins.then_inc(s_tk.h, 1); k += 1
                e.wait_ge(s_tk.h, k)
                ins = e.tensor_tensor(out=st_row[3][:], in0=st_row[1][:], in1=st_row[2][:], op=ALU.subtract)
                ins.then_inc(s_tk.h, 1); k += 1
                e.wait_ge(s_tk.h, k)
                ins = e.tensor_scalar(out=st_row[3][:], in0=st_row[3][:], scalar1=1e-5, scalar2=None, op0=ALU.add)
                ins.then_inc(s_tk.h, 1); k += 1
                e.wait_ge(s_tk.h, k)
                ins = e.reciprocal(out=st_row[4][:], in_=st_row[3][:])
                s_row.inc(ins)
                s_tk.n = k
            S("vector", rowmath)

            def rsqrt(e, RM=s_row.n, RW=s_rw.n):
                e.wait_ge(s_row.h, RM)
                if RW:
                    e.wait_ge(s_rw.h, RW)
                ins = e.activation(out=st_row[5][:], in_=st_row[4][:], func=AF.Sqrt)
                s_row.inc(ins)
            S("scalar", rsqrt)

            def mrmul(e, RS=s_row.n, RW=s_rw.n):
                e.wait_ge(s_row.h, RS)
                if RW:
                    e.wait_ge(s_rw.h, RW)
                ins = e.tensor_tensor(out=mr_row[:], in0=st_row[0][:], in1=st_row[5][:], op=ALU.mult)
                s_row.inc(ins)
            S("vector", mrmul)

            def rowwrite(e, j=j, MR=s_row.n):
                e.wait_ge(s_row.h, MR)
                s_rw.dma(e.dma_start(out=lnrows[j, 0:1, :], in_=mr_row[:]))
                s_rw.dma(e.dma_start(out=lnrows[j, 1:2, :], in_=st_row[5][:]))
            S("gpsimd", rowwrite)
    lctx.close()
    return s_rw, s_st



def emit_l0ffn(nc, block, semctx, pfx, io, gate):
    x1 = io["x1"]
    u1 = io["u1"]
    v1 = io["v1"]
    u2 = io["u2"]
    v2 = io["v2"]
    b1_r = io["b1_r"]
    b2_r = io["b2_r"]
    ones_col = io["ones_col"]
    x2_out = io["x2"]
    lnrows = nc.dram_tensor(pfx + "lnrows", [8, 2, 512], F32, kind="Internal").ap()

    x13 = x1.rearrange("(c p) t -> p c t", p=128)
    x2o3 = x2_out.rearrange("(c p) t -> p c t", p=128)
    TW = 512
    NT = TOK // TW

    with ExitStack() as ctx:
        emit_gate(block, gate)
        env = Env(nc, ctx, block, semctx, pfx)
        sb, sem, S = env.sb, env.sem, env.S

        u1_sb = sb("u1_sb", [128, NCH, 256], BF16)
        v1_sb = sb("v1_sb", [128, 2, 4096], BF16)
        u2_sb = sb("u2_sb", [128, 32, 256], BF16)
        v2_sb = sb("v2_sb", [128, 2, E], BF16)
        b1_sb = sb("b1_sb", [128, 32], F32)
        b2_sb = sb("b2_sb", [128, NCH], F32)
        env.onec_sb = sb("onec_sb", [128, 1], BF16)
        xn = [sb(f"xn{i}", [128, NCH, TW], BF16) for i in range(2)]
        xt = [sb(f"xt{i}", [128, NCH, TW], F32) for i in range(2)]
        rstd_rep = [sb(f"rstd_rep{i}", [128, TW], F32) for i in range(2)]
        mr_rep = [sb(f"mr_rep{i}", [128, TW], F32) for i in range(2)]
        tnorm = sb("tnorm", [128, NCH, TW], F32)
        _pt = sb("pt", [128, 2, TW], BF16)
        pt = [_pt, _pt]
        hbt0 = sb("hbt0", [128, 32, TW], BF16)
        hbt = [hbt0, hbt0]
        _q2t = sb("q2t", [128, 2, TW], BF16)
        q2t = [_q2t, _q2t]
        _x2t = sb("x2t", [128, NCH, TW], F32)
        x2t = [_x2t, _x2t]

        s_ldc = sem("s_ldc")
        s_wl = [sem(f"s_wl{i}") for i in range(2)]
        s_w = sem("s_w")
        s_xt2 = [sem(f"s_xt2{i}") for i in range(2)]
        s_lr = [sem(f"s_lr{i}") for i in range(2)]
        s_norm = sem("s_norm")
        s_mm = sem("s_mm")
        s_ev = sem("s_ev")
        s_x2 = sem("s_x2")
        s_st2 = sem("s_st2")

        def init_loads(e):
            for dst, src in [(b1_sb[:], b1_r), (b2_sb[:], b2_r), (env.onec_sb[:], ones_col)]:
                s_ldc.dma(e.dma_start(out=dst, in_=src))
        S("sync", init_loads)
        LDC = s_ldc.n

        wspecs = [(u1, u1_sb, 8, 256), (v1, v1_sb, 2, 4096), (u2, u2_sb, 32, 256), (v2, v2_sb, 2, E)]
        wctx = ExitStack()
        wtmp = [wctx.enter_context(nc.sbuf_tensor(pfx + f"wtmp{i}", [128, 4096], F32)) for i in range(2)]

        def wload(e):
            idx = 0
            for wsrc, wdst, nchk, cols in wspecs:
                ws = wsrc.rearrange("(c p) o -> p c o", p=128)
                for c in range(nchk):
                    if idx >= 2:
                        e.wait_ge(s_w.h, idx - 1)
                    s_wl[idx % 2].dma(e.dma_start(out=wtmp[idx % 2][:, 0:cols], in_=ws[:, c, :]))
                    idx += 1
        S("scalar", wload)

        def wcast(e):
            idx = 0
            for wsrc, wdst, nchk, cols in wspecs:
                for c in range(nchk):
                    e.wait_ge(s_wl[idx % 2].h, (idx // 2 + 1) * 16)
                    ins = e.tensor_copy(out=wdst[:, c, :], in_=wtmp[idx % 2][:, 0:cols])
                    s_w.inc(ins)
                    idx += 1
        S("vector", wcast)
        W_DONE = s_w.n
        wctx.close()

        s_rw, s_stA = emit_ln_stats(env, x13, NT, TW, lnrows,
                                    pre_wait=lambda e: e.wait_ge(s_w.h, W_DONE))
        A_RW = s_rw.n
        A_ST = s_stA.n
        norm_done = {}

        with nc.psum_tensor(pfx + "ps_a", [128, 512], F32) as ps_a, \
             nc.psum_tensor(pfx + "ps_b", [128, 512], F32) as ps_b, \
             nc.psum_tensor(pfx + "ps_q0", [128, 512], F32) as ps_q0, \
             nc.psum_tensor(pfx + "ps_q1", [128, 512], F32) as ps_q1:
            ps_pp = [ps_a, ps_b]
            ps_q2 = [ps_q0, ps_q1]
            pp_ev = []
            pp_i = [0]
            q2ev_done = [0]

            def group(mm_emit, ev_emit, pre_mm=None, ev_eng=("vector",)):
                b = pp_i[0] % 2

                def mm(e, b=b, prev=(pp_ev[-2] if len(pp_ev) >= 2 else None)):
                    if pre_mm:
                        pre_mm(e)
                    if prev is not None:
                        e.wait_ge(s_ev.h, prev)
                    ins = mm_emit(e, ps_pp[b])
                    s_mm.inc(ins)
                S("tensor", mm)
                MM = s_mm.n

                def ev(e, b=b, MM=MM, eng=None):
                    e.wait_ge(s_mm.h, MM)
                    ins = ev_emit(e, ps_pp[b])
                    s_ev.inc(ins)
                S(ev_eng[0] if ev_eng else "vector", ev)
                pp_ev.append(s_ev.n)
                pp_i[0] += 1
                return s_ev.n

            for j in range(NT):
                jb = j % 2

                def ld(e, j=j, jb=jb):
                    if j < 2:
                        e.wait_ge(s_stA.h, A_ST)
                    else:
                        e.wait_ge(s_x2.h, j - 1)
                    s_xt2[jb].dma(e.dma_start(out=xt[jb][:], in_=x13[:, :, j * TW:(j + 1) * TW]))
                    e.wait_ge(s_rw.h, A_RW)
                    s_lr[jb].dma(e.dma_start(out=rstd_rep[jb][:], in_=lnrows[j, 1:2, :].partition_broadcast(128)))
                    s_lr[jb].dma(e.dma_start(out=mr_rep[jb][:], in_=lnrows[j, 0:1, :].partition_broadcast(128)))
                S("sync", ld)
                XT_J = s_xt2[jb].n
                LR_J = s_lr[jb].n

                def norm(e, j=j, jb=jb, XT_J=XT_J, LR_J=LR_J):
                    e.wait_ge(s_xt2[jb].h, XT_J)
                    e.wait_ge(s_lr[jb].h, LR_J)
                    e.tensor_tensor(out=tnorm[:], in0=xt[jb][:], in1=rstd_rep[jb][:].unsqueeze(1).broadcast_to([128, NCH, TW]), op=ALU.mult)
                    ins = e.tensor_tensor(out=xn[jb][:], in0=tnorm[:], in1=mr_rep[jb][:].unsqueeze(1).broadcast_to([128, NCH, TW]), op=ALU.subtract)
                    s_norm.inc(ins, 2)
                S("vector", norm)
                norm_done[j] = s_norm.n
                NORM_J = s_norm.n

                for rc in range(2):
                    def pmm(e, ps, jb=jb, rc=rc):
                        for a in range(NCH):
                            ins = e.matmul(ps[:], u1_sb[:, a, rc * 128:(rc + 1) * 128], xn[jb][:, a, :], start=(a == 0), stop=(a == NCH - 1))
                        return ins

                    def pev(e, ps, jb=jb, rc=rc):
                        return e.tensor_copy(out=pt[jb][:, rc, :], in_=ps[:])

                    group(pmm, pev, pre_mm=(lambda e, NORM_J=NORM_J: e.wait_ge(s_norm.h, NORM_J)))
                PEV = s_ev.n

                for fc in range(32):
                    def hmm(e, ps, jb=jb, fc=fc, PEV=PEV):
                        e.wait_ge(s_ev.h, PEV)
                        for rc in range(2):
                            ins = e.matmul(ps[:], v1_sb[:, rc, fc * 128:(fc + 1) * 128], pt[jb][:, rc, :], start=(rc == 0), stop=(rc == 1))
                        return ins

                    def hev(e, ps, jb=jb, fc=fc):
                        return e.activation(out=hbt[jb][:, fc, :], in_=ps[:], func=AF.Gelu_apprx_tanh, bias=b1_sb[:, fc:fc + 1], scale=1.0)

                    HEV = group(hmm, hev, ev_eng=("scalar",))

                    def q2mm(e, jb=jb, fc=fc, HEV=HEV, Q2P=q2ev_done[0]):
                        e.wait_ge(s_ev.h, HEV)
                        if fc == 0 and Q2P:
                            e.wait_ge(s_ev.h, Q2P)
                        for rc in range(2):
                            ins = e.matmul(ps_q2[rc][:], u2_sb[:, fc, rc * 128:(rc + 1) * 128], hbt[jb][:, fc, :], start=(fc == 0), stop=(fc == 31))
                        s_mm.inc(ins)
                    S("tensor", q2mm)

                Q2MM = s_mm.n

                def q2ev(e, jb=jb, Q2MM=Q2MM):
                    e.wait_ge(s_mm.h, Q2MM)
                    for rc in range(2):
                        ins = e.tensor_copy(out=q2t[jb][:, rc, :], in_=ps_q2[rc][:])
                    s_ev.inc(ins, 1)
                S("vector", q2ev)
                Q2EV = s_ev.n
                q2ev_done[0] = Q2EV
                pp_ev.append(s_ev.n)

                for dc in range(NCH):
                    def omm(e, ps, jb=jb, dc=dc, Q2EV=Q2EV):
                        e.wait_ge(s_ev.h, Q2EV)
                        for rc in range(2):
                            ins = e.matmul(ps[:], v2_sb[:, rc, dc * 128:(dc + 1) * 128], q2t[jb][:, rc, :], start=(rc == 0), stop=(rc == 1))
                        return ins

                    def oev(e, ps, j=j, jb=jb, dc=dc):
                        if dc == 0 and j >= 1:
                            e.wait_ge(s_st2.h, j * 16)
                        return e.tensor_scalar(out=x2t[jb][:, dc, :], in0=ps[:], scalar1=b2_sb[:, dc:dc + 1], scalar2=None, op0=ALU.add)

                    group(omm, oev)
                OEV_J = s_ev.n

                def resadd(e, jb=jb, OEV_J=OEV_J):
                    e.wait_ge(s_ev.h, OEV_J)
                    ins = e.tensor_tensor(out=x2t[jb][:], in0=x2t[jb][:], in1=xt[jb][:], op=ALU.add)
                    s_x2.inc(ins)
                S("vector", resadd)
                X2_J = s_x2.n

                def store(e, j=j, jb=jb, X2_J=X2_J):
                    e.wait_ge(s_x2.h, X2_J)
                    s_st2.dma(e.dma_start(out=x2o3[:, :, j * TW:(j + 1) * TW], in_=x2t[jb][:]))
                S("gpsimd", store)

            def fin(e):
                e.wait_ge(s_st2.h, s_st2.n)
            S("sync", fin)

    return phase_terminals()


def l0ffn_inputs(x1_fm, inp):
    import ml_dtypes
    return {
        "x1": x1_fm,
        "u1": inp["L0_ffn_U1"].copy(),
        "v1": inp["L0_ffn_V1"].copy(),
        "u2": inp["L0_ffn_U2"].copy(),
        "v2": inp["L0_ffn_V2"].copy(),
        "b1_r": np.ascontiguousarray(inp["L0_ffn_b1"].reshape(32, 128).T),
        "b2_r": np.ascontiguousarray(inp["L0_ffn_b2"].reshape(NCH, 128).T),
        "ones_col": np.ones((128, 1), ml_dtypes.bfloat16).view(np.uint16),
    }



def emit_l1pre(nc, block, semctx, pfx, io, gate):
    x2 = io["x2"]
    win = io["win"]
    bin_r = io["bin_r"]
    ones_col = io["ones_col"]
    xg_out = io["xg"]
    lnrows = nc.dram_tensor(pfx + "lnrows", [8, 2, 512], F32, kind="Internal").ap()

    x23 = x2.rearrange("(c p) t -> p c t", p=128)
    xgo3 = xg_out.rearrange("(c p) t -> p c t", p=128)
    TW = 512
    NT = TOK // TW

    with ExitStack() as ctx:
        emit_gate(block, gate)
        env = Env(nc, ctx, block, semctx, pfx)
        sb, sem, S = env.sb, env.sem, env.S

        win_sb = sb("win_sb", [128, NCH, 2048], BF16)
        bin_sb = sb("bin_sb", [128, 16], F32)
        env.onec_sb = sb("onec_sb", [128, 1], BF16)
        xn = [sb(f"xn{i}", [128, NCH, TW], BF16) for i in range(2)]
        xt = [sb(f"xt{i}", [128, NCH, TW], F32) for i in range(2)]
        rstd_rep = [sb(f"rstd_rep{i}", [128, TW], F32) for i in range(2)]
        mr_rep = [sb(f"mr_rep{i}", [128, TW], F32) for i in range(2)]
        tnorm = sb("tnorm", [128, NCH, TW], F32)
        _xgt = sb("xgt", [128, 16, TW], F32)
        xgt = [_xgt, _xgt]

        s_ldc = sem("s_ldc")
        s_wl = [sem(f"s_wl{i}") for i in range(2)]
        s_w = sem("s_w")
        s_xt2 = [sem(f"s_xt2{i}") for i in range(2)]
        s_lr = [sem(f"s_lr{i}") for i in range(2)]
        s_norm = sem("s_norm")
        s_mm = sem("s_mm")
        s_ev = sem("s_ev")
        s_st2 = sem("s_st2")

        def init_loads(e):
            s_ldc.dma(e.dma_start(out=bin_sb[:], in_=bin_r))
            s_ldc.dma(e.dma_start(out=env.onec_sb[:], in_=ones_col))
        S("sync", init_loads)

        wctx = ExitStack()
        wtmp = [wctx.enter_context(nc.sbuf_tensor(pfx + f"wtmp{i}", [128, 2048], F32)) for i in range(2)]

        def wload(e):
            ws = win.rearrange("(c p) o -> p c o", p=128)
            for c in range(NCH):
                if c >= 2:
                    e.wait_ge(s_w.h, c - 1)
                s_wl[c % 2].dma(e.dma_start(out=wtmp[c % 2][:], in_=ws[:, c, :]))
        S("scalar", wload)

        def wcast(e):
            for c in range(NCH):
                e.wait_ge(s_wl[c % 2].h, (c // 2 + 1) * 16)
                ins = e.tensor_copy(out=win_sb[:, c, :], in_=wtmp[c % 2][:])
                s_w.inc(ins)
        S("vector", wcast)
        W_DONE = s_w.n
        wctx.close()

        s_rw, s_stA = emit_ln_stats(env, x23, NT, TW, lnrows,
                                    pre_wait=lambda e: e.wait_ge(s_w.h, W_DONE))
        A_RW = s_rw.n
        A_ST = s_stA.n
        norm_done = {}

        with nc.psum_tensor(pfx + "ps_a", [128, 512], F32) as ps_a, \
             nc.psum_tensor(pfx + "ps_b", [128, 512], F32) as ps_b:
            ps_pp = [ps_a, ps_b]
            pp_ev = []
            pp_i = [0]

            def group(mm_emit, ev_emit, pre_mm=None, ev_eng=("vector",)):
                b = pp_i[0] % 2

                def mm(e, b=b, prev=(pp_ev[-2] if len(pp_ev) >= 2 else None)):
                    if pre_mm:
                        pre_mm(e)
                    if prev is not None:
                        e.wait_ge(s_ev.h, prev)
                    ins = mm_emit(e, ps_pp[b])
                    s_mm.inc(ins)
                S("tensor", mm)
                MM = s_mm.n

                def ev(e, b=b, MM=MM, eng=None):
                    e.wait_ge(s_mm.h, MM)
                    ins = ev_emit(e, ps_pp[b])
                    s_ev.inc(ins)
                S(ev_eng[0] if ev_eng else "vector", ev)
                pp_ev.append(s_ev.n)
                pp_i[0] += 1
                return s_ev.n

            for j in range(NT):
                jb = j % 2

                def ld(e, j=j, jb=jb):
                    if j < 2:
                        e.wait_ge(s_stA.h, A_ST)
                    else:
                        e.wait_ge(s_norm.h, norm_done[j - 2])
                    s_xt2[jb].dma(e.dma_start(out=xt[jb][:], in_=x23[:, :, j * TW:(j + 1) * TW]))
                    e.wait_ge(s_rw.h, A_RW)
                    s_lr[jb].dma(e.dma_start(out=rstd_rep[jb][:], in_=lnrows[j, 1:2, :].partition_broadcast(128)))
                    s_lr[jb].dma(e.dma_start(out=mr_rep[jb][:], in_=lnrows[j, 0:1, :].partition_broadcast(128)))
                S("sync", ld)
                XT_J = s_xt2[jb].n
                LR_J = s_lr[jb].n

                def norm(e, jb=jb, XT_J=XT_J, LR_J=LR_J):
                    e.wait_ge(s_xt2[jb].h, XT_J)
                    e.wait_ge(s_lr[jb].h, LR_J)
                    e.wait_ge(s_mm.h, s_mm.n)
                    e.tensor_tensor(out=tnorm[:], in0=xt[jb][:], in1=rstd_rep[jb][:].unsqueeze(1).broadcast_to([128, NCH, TW]), op=ALU.mult)
                    ins = e.tensor_tensor(out=xn[jb][:], in0=tnorm[:], in1=mr_rep[jb][:].unsqueeze(1).broadcast_to([128, NCH, TW]), op=ALU.subtract)
                    s_norm.inc(ins, 2)
                S("vector", norm)
                norm_done[j] = s_norm.n
                NORM_J = s_norm.n

                for oc in range(16):
                    def omm(e, ps, jb=jb, oc=oc):
                        for a in range(NCH):
                            ins = e.matmul(ps[:], win_sb[:, a, oc * 128:(oc + 1) * 128], xn[jb][:, a, :], start=(a == 0), stop=(a == NCH - 1))
                        return ins

                    def oev(e, ps, j=j, jb=jb, oc=oc):
                        if oc == 0 and j >= 1:
                            e.wait_ge(s_st2.h, j * 16)
                        return e.tensor_scalar(out=xgt[jb][:, oc, :], in0=ps[:], scalar1=bin_sb[:, oc:oc + 1], scalar2=None, op0=ALU.add)

                    group(omm, oev, pre_mm=(lambda e, NORM_J=NORM_J: e.wait_ge(s_norm.h, NORM_J)))
                OEV_J = s_ev.n

                def store(e, j=j, jb=jb, OEV_J=OEV_J):
                    e.wait_ge(s_ev.h, OEV_J)
                    s_st2.dma(e.dma_start(out=xgo3[:, :, j * TW:(j + 1) * TW], in_=xgt[jb][:]))
                S("gpsimd", store)

            def fin(e):
                e.wait_ge(s_st2.h, s_st2.n)
            S("sync", fin)
    return phase_terminals()


def l1pre_inputs(x2_fm, inp):
    import ml_dtypes
    return {
        "x2": x2_fm,
        "win": inp["L1_attn_W_in"].copy(),
        "bin_r": np.ascontiguousarray(inp["L1_attn_b_in"].reshape(16, 128).T),
        "ones_col": np.ones((128, 1), ml_dtypes.bfloat16).view(np.uint16),
    }



def emit_conv(nc, block, semctx, pfx, io, gate):
    S8 = 8192
    xs = io["xs"]
    cw = io["cw"]
    NG = io.get("ng", 4)
    cb = io["cb"]
    zer = io["zer"]
    xcb_out = io["xcb"]
    xs3 = xs.rearrange("(g p) t -> p g t", p=128)
    xo3 = xcb_out.rearrange("(g p) t -> p g t", p=128)

    with ExitStack() as ctx:
        emit_gate(block, gate)
        env = Env(nc, ctx, block, semctx, pfx)
        sb, sem, S = env.sb, env.sem, env.S
        cw_sb = sb("cw_sb", [128, NG, 4], F32)
        cb_sb = sb("cb_sb", [128, NG], F32)
        xpad = [sb(f"xpad{i}", [128, 4 + S8], F32) for i in range(2)]
        acc = [sb(f"acc{i}", [128, S8], F32) for i in range(2)]
        xcb_sb = [sb(f"xcb{i}", [128, S8], BF16) for i in range(2)]
        s_ldc = sem("s_ldc")
        s_xp = [sem(f"s_xp{i}") for i in range(2)]
        s_cv = sem("s_cv")
        s_si = sem("s_si")
        s_st = sem("s_st")

        def init_loads(e):
            s_ldc.dma(e.dma_start(out=cw_sb[:], in_=cw))
            s_ldc.dma(e.dma_start(out=cb_sb[:], in_=cb))
            s_ldc.dma(e.dma_start(out=xpad[0][:, 0:4], in_=zer))
            s_ldc.dma(e.dma_start(out=xpad[1][:, 0:4], in_=zer))
        S("sync", init_loads)
        LDC = s_ldc.n

        for g in range(NG):
            gb = g % 2

            def ld(e, g=g, gb=gb):
                if g < 2:
                    e.wait_ge(s_ldc.h, LDC)
                else:
                    e.wait_ge(s_cv.h, (g - 1) * 4)
                s_xp[gb].dma(e.dma_start(out=xpad[gb][:, 4:], in_=xs3[:, g, :]))
            S("sync", ld)
            XP_G = s_xp[gb].n

            def conv(e, g=g, gb=gb, XP_G=XP_G):
                e.wait_ge(s_xp[gb].h, XP_G)
                if g >= 2:
                    e.wait_ge(s_si.h, g - 1)
                e.tensor_scalar(out=acc[gb][:], in0=xpad[gb][:, 4:4 + S8], scalar1=cw_sb[:, g, 3:4], scalar2=None, op0=ALU.mult)
                for k in range(3):
                    ins = e.scalar_tensor_tensor(out=acc[gb][:], in0=xpad[gb][:, 1 + k:1 + k + S8], scalar=cw_sb[:, g, k:k + 1], in1=acc[gb][:], op0=ALU.mult, op1=ALU.add)
                s_cv.inc(ins, 4)
            S("vector", conv)
            CV_G = s_cv.n

            def silu(e, g=g, gb=gb, CV_G=CV_G):
                e.wait_ge(s_cv.h, CV_G)
                if g >= 2:
                    e.wait_ge(s_st.h, (g - 1) * 16)
                ins = e.activation(out=xcb_sb[gb][:], in_=acc[gb][:], func=AF.Silu, bias=cb_sb[:, g:g + 1], scale=1.0)
                s_si.inc(ins)
            S("scalar", silu)
            SI_G = s_si.n

            def store(e, g=g, gb=gb, SI_G=SI_G):
                e.wait_ge(s_si.h, SI_G)
                s_st.dma(e.dma_start(out=xo3[:, g, :], in_=xcb_sb[gb][:]))
            S("gpsimd", store)

        def fin(e):
            e.wait_ge(s_st.h, s_st.n)
        S("sync", fin)
    return phase_terminals()


def conv_inputs(xs_half, inp, dh):
    cwh = inp["L1_attn_conv_w"][dh * 512:(dh + 1) * 512]
    cbh = inp["L1_attn_conv_b"][dh * 512:(dh + 1) * 512]
    return {
        "xs": xs_half,
        "cw": np.ascontiguousarray(cwh.reshape(4, 128, 4).transpose(1, 0, 2)),
        "cb": np.ascontiguousarray(cbh.reshape(4, 128).T),
        "zer": np.zeros((128, 4), np.float32),
    }



def emit_bc(nc, block, semctx, pfx, io, gate):
    xcb = io["xcb"]
    wx = io["wx"]
    bx = io["bx"]
    bc_out = io["bc"]
    NCOLS = io.get("ncols", TOK)
    xc3 = xcb.rearrange("(c p) t -> p c t", p=128)
    TW = 512
    NT = NCOLS // TW

    with ExitStack() as ctx:
        emit_gate(block, gate)
        env = Env(nc, ctx, block, semctx, pfx)
        sb, sem, S = env.sb, env.sem, env.S
        wx_sb = sb("wx_sb", [128, NCH, 32], BF16)
        wxf = sb("wxf", [128, NCH, 32], F32)
        bx_sb = sb("bx_sb", [32, 1], F32)
        xct = [sb(f"xct{i}", [128, NCH, TW], BF16) for i in range(2)]
        bct = [sb(f"bct{i}", [32, TW], BF16) for i in range(2)]
        s_ldc = sem("s_ldc")
        s_xc = [sem(f"s_xc{i}") for i in range(2)]
        s_w = sem("s_w")
        s_mm = sem("s_mm")
        s_ev = sem("s_ev")
        s_st = sem("s_st")

        def init_loads(e):
            s_ldc.dma(e.dma_start(out=wxf[:], in_=wx.rearrange("(c p) o -> p c o", p=128)))
            s_ldc.dma(e.dma_start(out=bx_sb[:], in_=bx))
        S("sync", init_loads)
        LDC = s_ldc.n

        def wcast(e):
            e.wait_ge(s_ldc.h, LDC)
            ins = e.tensor_copy(out=wx_sb[:], in_=wxf[:])
            s_w.inc(ins)
        S("vector", wcast)
        W_DONE = s_w.n

        with nc.psum_tensor(pfx + "ps_a", [128, 512], F32) as ps_a, nc.psum_tensor(pfx + "ps_b", [128, 512], F32) as ps_b:
            ps_pp = [ps_a, ps_b]
            for j in range(NT):
                jb = j % 2

                def ld(e, j=j, jb=jb):
                    if j >= 2:
                        e.wait_ge(s_mm.h, j - 1)
                    s_xc[jb].dma(e.dma_start(out=xct[jb][:], in_=xc3[:, :, j * TW:(j + 1) * TW]))
                S("sync", ld)
                XC_J = s_xc[jb].n

                def mm(e, j=j, jb=jb, XC_J=XC_J):
                    e.wait_ge(s_xc[jb].h, XC_J)
                    if j == 0:
                        e.wait_ge(s_w.h, W_DONE)
                    if j >= 2:
                        e.wait_ge(s_ev.h, j - 1)
                    for a in range(NCH):
                        ins = e.matmul(ps_pp[jb][0:32, :], wx_sb[:, a, :], xct[jb][:, a, :], start=(a == 0), stop=(a == NCH - 1))
                    s_mm.inc(ins)
                S("tensor", mm)
                MM_J = s_mm.n

                def ev(e, jb=jb, MM_J=MM_J, j=j):
                    e.wait_ge(s_mm.h, MM_J)
                    if j >= 2:
                        e.wait_ge(s_st.h, (j - 1) * 16)
                    ins = e.tensor_scalar(out=bct[jb][:], in0=ps_pp[jb][0:32, :], scalar1=bx_sb[:], scalar2=None, op0=ALU.add)
                    s_ev.inc(ins)
                S("vector", ev)
                EV_J = s_ev.n

                def store(e, j=j, jb=jb, EV_J=EV_J):
                    e.wait_ge(s_ev.h, EV_J)
                    s_st.dma(e.dma_start(out=bc_out[:, j * TW:(j + 1) * TW], in_=bct[jb][:]))
                S("gpsimd", store)

            def fin(e):
                e.wait_ge(s_st.h, s_st.n)
            S("sync", fin)
    return phase_terminals()


def bc_inputs(xcb_full, inp):
    return {
        "xcb": xcb_full,
        "wx": inp["L1_attn_W_x"].copy(),
        "bx": inp["L1_attn_b_x"].reshape(32, 1).copy(),
    }



def emit_scan(nc, block, semctx, pfx, io, gate):
    S8 = 8192
    TC = 4096
    NTC = S8 // TC
    xcb = io["xcb"]
    bc = io["bc"]
    alane = io["alane"]
    dcol = io["dcol"]
    sel = io["sel"]
    zcol = io["zcol"]
    y_out = io["y"]
    y3 = y_out.rearrange("(g p) t -> p g t", p=128)

    with ExitStack() as ctx:
        emit_gate(block, gate)
        env = Env(nc, ctx, block, semctx, pfx)
        sb, sem, S = env.sb, env.sem, env.S
        a_sb = sb("a_sb", [128, 64], F32)
        at_sb = sb("at_sb", [128, 64], F32)
        d_sb = sb("d_sb", [128, 4], F32)
        sel_sb = sb("sel_sb", [128, 16, 128], BF16)
        z_sb = sb("z_sb", [128, 1], F32)
        hstv = [sb(f"hstv{i}", [128, 1], F32) for i in range(64)]
        bexp = sb("bexp", [128, TC], BF16)
        cexp = sb("cexp", [128, TC], BF16)
        xce = [sb(f"xce{i}", [128, TC], BF16) for i in range(2)]
        u_sb = sb("u_sb", [128, TC], BF16)
        h_sb2 = [sb(f"h_sb{i}", [128, TC], BF16) for i in range(2)]
        tmpb = [sb(f"tmpb{i}", [128, TC], BF16) for i in range(16)]
        xcd = [sb(f"xcd{i}", [128, 512], BF16) for i in range(2)]
        yt = [sb(f"yt{i}", [128, 512], F32) for i in range(2)]

        s_ldc = sem("s_ldc")
        s_bc = sem("s_bc")
        s_xc = [sem(f"s_xc{i}") for i in range(2)]
        s_xcd = [sem(f"s_xcd{i}") for i in range(2)]
        s_a = sem("s_a")
        s_tmp = sem("s_tmp")
        s_scn = sem("s_scn")
        s_hst = sem("s_hst")
        s_ymm = sem("s_ymm")
        s_yev = sem("s_yev")
        s_st = sem("s_st")

        def init_loads(e):
            s_ldc.dma(e.dma_start(out=at_sb[:], in_=alane))
            s_ldc.dma(e.dma_start(out=d_sb[:], in_=dcol))
            s_ldc.dma(e.dma_start(out=z_sb[:], in_=zcol))
            for gg in range(16):
                s_ldc.dma(e.dma_start(out=sel_sb[:, gg, :], in_=sel[gg]))
        S("sync", init_loads)
        LDC = s_ldc.n

        def aprep(e):
            e.wait_ge(s_ldc.h, LDC)
            e.activation(out=a_sb[:], in_=at_sb[:], func=AF.Exp)
            ins = e.activation(out=a_sb[:], in_=a_sb[:], func=AF.Exp, scale=-1.0)
            s_a.inc(ins, 2)
        S("scalar", aprep)
        A_DONE = s_a.n

        tmp_done_prev_db = [0]
        ps_yctx = ExitStack()
        ps_y = [ps_yctx.enter_context(nc.psum_tensor(f"ps_y{i}", [128, 512], F32)) for i in range(2)]
        for tc in range(NTC):
            TMP_ALL = s_tmp.n

            def bcl(e, tc=tc, TMP_ALL=TMP_ALL):
                if tc > 0:
                    e.wait_ge(s_tmp.h, TMP_ALL)
                for n in range(16):
                    s_bc.dma(e.dma_start(out=bexp[n * 8:(n + 1) * 8, :], in_=bc[n:n + 1, tc * TC:(tc + 1) * TC].partition_broadcast(8)))
                for n in range(16):
                    s_bc.dma(e.dma_start(out=cexp[n * 8:(n + 1) * 8, :], in_=bc[16 + n:17 + n, tc * TC:(tc + 1) * TC].partition_broadcast(8)))
            S("sync", bcl)
            BC_TC = s_bc.n

            for db in range(4):
                for gg in range(16):
                    g = db * 16 + gg
                    gb = g % 2

                    def xcl(e, g=g, gb=gb, tc=tc, T2=s_tmp.n):
                        if T2 >= 2:
                            e.wait_ge(s_tmp.h, T2 - 1)
                        for n in range(16):
                            s_xc[gb].dma(e.dma_start(out=xce[gb][n * 8:(n + 1) * 8, :], in_=xcb[g * 8:(g + 1) * 8, tc * TC:(tc + 1) * TC]))
                    S("sync", xcl)
                    XC_G = s_xc[gb].n

                    def chain(e, g=g, gb=gb, gg=gg, tc=tc, XC_G=XC_G, BC_TC=BC_TC, YP=tmp_done_prev_db[0]):
                        e.wait_ge(s_xc[gb].h, XC_G)
                        e.wait_ge(s_bc.h, BC_TC)
                        if g == 0 and tc == 0:
                            e.wait_ge(s_a.h, A_DONE)
                        if YP:
                            e.wait_ge(s_ymm.h, YP)
                        need = 0
                        if g >= 2:
                            need = max(need, tc * 64 + g - 1)
                        elif tc > 0:
                            need = max(need, (tc - 1) * 64 + 64)
                        if tc > 0:
                            need = max(need, (tc - 1) * 64 + g + 1)
                        if need:
                            e.wait_ge(s_hst.h, need)
                        e.tensor_tensor(out=u_sb[:], in0=xce[gb][:], in1=bexp[:], op=ALU.mult)
                        init_ap = z_sb[:] if tc == 0 else hstv[g][:]
                        sc = e.tensor_tensor_scan(out=h_sb2[gb][:], data0=a_sb[:, g:g + 1].broadcast_to([128, TC]), data1=u_sb[:], initial=init_ap, op0=ALU.mult, op1=ALU.add)
                        s_scn.inc(sc)
                        ins = e.tensor_tensor(out=tmpb[gg][:], in0=h_sb2[gb][:], in1=cexp[:], op=ALU.mult)
                        s_tmp.inc(ins)
                    S("vector", chain)

                    def hcap(e, g=g, gb=gb, SCN=s_scn.n):
                        e.wait_ge(s_scn.h, SCN)
                        ins = e.copy(out=hstv[g][:], in_=h_sb2[gb][:, TC - 1:TC])
                        s_hst.inc(ins)
                    S("scalar", hcap)

                TMP_DB = s_tmp.n
                for ts8 in range(TC // 512):
                    ypb = ts8 % 2

                    def xcdl(e, db=db, tc=tc, ts8=ts8, ypb=ypb, YEV=s_yev.n):
                        if YEV >= 2:
                            e.wait_ge(s_yev.h, YEV - 1)
                        s_xcd[ypb].dma(e.dma_start(out=xcd[ypb][:], in_=xcb[db * 128:(db + 1) * 128, tc * TC + ts8 * 512: tc * TC + (ts8 + 1) * 512]))
                    S("sync", xcdl)
                    XCD = s_xcd[ypb].n

                    def ymm(e, db=db, ts8=ts8, ypb=ypb, TMP_DB=TMP_DB, YEV=s_yev.n):
                        e.wait_ge(s_tmp.h, TMP_DB)
                        if YEV >= 2:
                            e.wait_ge(s_yev.h, YEV - 1)
                        for gg in range(16):
                            ins = e.matmul(ps_y[ypb][:], sel_sb[:, gg, :], tmpb[gg][:, ts8 * 512:(ts8 + 1) * 512], start=(gg == 0), stop=(gg == 15))
                        s_ymm.inc(ins)
                    S("tensor", ymm)
                    YMM = s_ymm.n

                    def yev(e, db=db, ypb=ypb, YMM=YMM, XCD=XCD, ST=s_st.n):
                        e.wait_ge(s_ymm.h, YMM)
                        e.wait_ge(s_xcd[ypb].h, XCD)
                        if ST >= 32:
                            e.wait_ge(s_st.h, ST - 16)
                        ins = e.scalar_tensor_tensor(out=yt[ypb][:], in0=xcd[ypb][:], scalar=d_sb[:, db:db + 1], in1=ps_y[ypb][:], op0=ALU.mult, op1=ALU.add)
                        s_yev.inc(ins)
                    S("vector", yev)
                    YEV2 = s_yev.n

                    def yst(e, db=db, tc=tc, ts8=ts8, ypb=ypb, YEV2=YEV2):
                        e.wait_ge(s_yev.h, YEV2)
                        s_st.dma(e.dma_start(out=y3[:, db, tc * TC + ts8 * 512: tc * TC + (ts8 + 1) * 512], in_=yt[ypb][:]))
                    S("gpsimd", yst)
                tmp_done_prev_db[0] = s_ymm.n

        def fin(e):
            e.wait_ge(s_st.h, s_st.n)
        S("sync", fin)
        ps_yctx.close()
    return phase_terminals()


def scan_inputs(xcb_half, bc_full, inp, dh):
    import ml_dtypes
    al = inp["L1_attn_A_log"][dh * 512:(dh + 1) * 512]
    alane = np.zeros((128, 64), np.float32)
    for g in range(64):
        for oct in range(8):
            for n in range(16):
                alane[n * 8 + oct, g] = al[g * 8 + oct, n]
    dp = inp["L1_attn_D_param"][dh * 512:(dh + 1) * 512]
    selm = np.zeros((16, 128, 128), np.float32)
    for gg in range(16):
        for n in range(16):
            for oct in range(8):
                selm[gg, n * 8 + oct, gg * 8 + oct] = 1.0
    return {
        "xcb": xcb_half,
        "bc": bc_full,
        "alane": alane,
        "dcol": np.ascontiguousarray(dp.reshape(4, 128).T),
        "sel": selm.astype(ml_dtypes.bfloat16).view(np.uint16),
        "zcol": np.zeros((128, 1), np.float32),
    }



def emit_l1post(nc, block, semctx, pfx, io, gate):
    y_in = io["y"]
    gate_t = io["gate"]
    x2 = io["x2"]
    wo = io["wo"]
    bo_r = io["bo_r"]
    x3_out = io["x3"]
    y3 = y_in.rearrange("(c p) t -> p c t", p=128)
    g3 = gate_t.rearrange("(c p) t -> p c t", p=128)
    x23 = x2.rearrange("(c p) t -> p c t", p=128)
    xo3 = x3_out.rearrange("(c p) t -> p c t", p=128)
    TW = 512
    NT = TOK // TW

    with ExitStack() as ctx:
        emit_gate(block, gate)
        env = Env(nc, ctx, block, semctx, pfx)
        sb, sem, S = env.sb, env.sem, env.S
        wo_sb = sb("wo_sb", [128, NCH, E], BF16)
        bo_sb = sb("bo_sb", [128, NCH], F32)
        yt = [sb(f"yt{i}", [128, NCH, TW], F32) for i in range(2)]
        gt = [sb(f"gt{i}", [128, NCH, TW], F32) for i in range(2)]
        x2t = [sb(f"x2t{i}", [128, NCH, TW], F32) for i in range(2)]
        ysg = [sb(f"ysg{i}", [128, NCH, TW], BF16) for i in range(2)]
        ot = sb("ot", [128, NCH, TW], F32)
        s_ldc = sem("s_ldc")
        s_wl = [sem(f"s_wl{i}") for i in range(2)]
        s_w = sem("s_w")
        s_ld = [sem(f"s_ld{i}") for i in range(2)]
        s_sg = sem("s_sg")
        s_ys = sem("s_ys")
        s_mm = sem("s_mm")
        s_ev = sem("s_ev")
        s_x3 = sem("s_x3")
        s_st = sem("s_st")

        def init_loads(e):
            s_ldc.dma(e.dma_start(out=bo_sb[:], in_=bo_r))
        S("sync", init_loads)

        wctx = ExitStack()
        wtmp = [wctx.enter_context(nc.sbuf_tensor(pfx + f"wtmp{i}", [128, E], F32)) for i in range(2)]

        def wload(e):
            ws = wo.rearrange("(c p) o -> p c o", p=128)
            for c in range(NCH):
                if c >= 2:
                    e.wait_ge(s_w.h, c - 1)
                s_wl[c % 2].dma(e.dma_start(out=wtmp[c % 2][:], in_=ws[:, c, :]))
        S("scalar", wload)

        def wcast(e):
            for c in range(NCH):
                e.wait_ge(s_wl[c % 2].h, (c // 2 + 1) * 16)
                ins = e.tensor_copy(out=wo_sb[:, c, :], in_=wtmp[c % 2][:])
                s_w.inc(ins)
        S("vector", wcast)
        W_DONE = s_w.n
        wctx.close()

        with nc.psum_tensor(pfx + "ps_a", [128, 512], F32) as ps_a, nc.psum_tensor(pfx + "ps_b", [128, 512], F32) as ps_b:
            ps_pp = [ps_a, ps_b]
            pp_ev = []
            pp_i = [0]

            def group(mm_emit, ev_emit, pre_mm=None, ev_eng=("vector",)):
                b = pp_i[0] % 2

                def mm(e, b=b, prev=(pp_ev[-2] if len(pp_ev) >= 2 else None)):
                    if pre_mm:
                        pre_mm(e)
                    if prev is not None:
                        e.wait_ge(s_ev.h, prev)
                    ins = mm_emit(e, ps_pp[b])
                    s_mm.inc(ins)
                S("tensor", mm)
                MM = s_mm.n

                def ev(e, b=b, MM=MM, eng=None):
                    e.wait_ge(s_mm.h, MM)
                    ins = ev_emit(e, ps_pp[b])
                    s_ev.inc(ins)
                S(ev_eng[0] if ev_eng else "vector", ev)
                pp_ev.append(s_ev.n)
                pp_i[0] += 1
                return s_ev.n

            for j in range(NT):
                jb = j % 2

                def ld(e, j=j, jb=jb):
                    if j == 0:
                        e.wait_ge(s_w.h, W_DONE)
                    if j >= 2:
                        e.wait_ge(s_ys.h, j - 1)
                        e.wait_ge(s_x3.h, j - 1)
                        e.wait_ge(s_st.h, (j - 1) * 16)
                    s_ld[jb].dma(e.dma_start(out=yt[jb][:], in_=y3[:, :, j * TW:(j + 1) * TW]))
                    s_ld[jb].dma(e.dma_start(out=gt[jb][:], in_=g3[:, :, j * TW:(j + 1) * TW]))
                    s_ld[jb].dma(e.dma_start(out=x2t[jb][:], in_=x23[:, :, j * TW:(j + 1) * TW]))
                S("sync", ld)
                LD_J = s_ld[jb].n

                def sg(e, jb=jb, LD_J=LD_J, j=j):
                    e.wait_ge(s_ld[jb].h, LD_J)
                    if j >= 2:
                        e.wait_ge(s_ys.h, j - 1)
                    ins = e.activation(out=gt[jb][:], in_=gt[jb][:], func=AF.Silu)
                    s_sg.inc(ins)
                S("scalar", sg)
                SG_J = s_sg.n

                def ysb(e, jb=jb, SG_J=SG_J, MM_ALL=s_mm.n):
                    e.wait_ge(s_sg.h, SG_J)
                    e.wait_ge(s_mm.h, MM_ALL)
                    ins = e.tensor_tensor(out=ysg[jb][:], in0=yt[jb][:], in1=gt[jb][:], op=ALU.mult)
                    s_ys.inc(ins)
                S("vector", ysb)
                YS_J = s_ys.n

                for dc in range(NCH):
                    def omm(e, ps, jb=jb, dc=dc, YS_J=YS_J):
                        e.wait_ge(s_ys.h, YS_J)
                        for a in range(NCH):
                            ins = e.matmul(ps[:], wo_sb[:, a, dc * 128:(dc + 1) * 128], ysg[jb][:, a, :], start=(a == 0), stop=(a == NCH - 1))
                        return ins

                    def oev(e, ps, j=j, dc=dc):
                        if dc == 0 and j >= 1:
                            e.wait_ge(s_x3.h, j)
                        return e.tensor_scalar(out=ot[:, dc, :], in0=ps[:], scalar1=bo_sb[:, dc:dc + 1], scalar2=None, op0=ALU.add)

                    group(omm, oev)
                OEV_J = s_ev.n

                def resadd(e, jb=jb, OEV_J=OEV_J):
                    e.wait_ge(s_ev.h, OEV_J)
                    if s_st.n:
                        e.wait_ge(s_st.h, s_st.n)
                    ins = e.tensor_tensor(out=x2t[jb][:], in0=x2t[jb][:], in1=ot[:], op=ALU.add)
                    s_x3.inc(ins)
                S("vector", resadd)
                X3_J = s_x3.n

                def store(e, j=j, jb=jb, X3_J=X3_J):
                    e.wait_ge(s_x3.h, X3_J)
                    s_st.dma(e.dma_start(out=xo3[:, :, j * TW:(j + 1) * TW], in_=x2t[jb][:]))
                S("gpsimd", store)

            def fin(e):
                e.wait_ge(s_st.h, s_st.n)
            S("sync", fin)
    return phase_terminals()


def l1post_inputs(y_fm, gate_fm, x2_fm, inp):
    return {
        "y": y_fm, "gate": gate_fm, "x2": x2_fm,
        "wo": inp["L1_attn_W_out"].copy(),
        "bo_r": np.ascontiguousarray(inp["L1_attn_b_out"].reshape(NCH, 128).T),
    }



def emit_ln4(nc, block, semctx, pfx, io, gate):
    x3 = io["x3"]
    ones_col = io["ones_col"]
    xn4_out = io["xn4"]
    mx_out = io["mx"]
    lnrows = nc.dram_tensor(pfx + "lnrows", [8, 2, 512], F32, kind="Internal").ap()
    colscr = nc.dram_tensor(pfx + "colscr", [128, 1], F32, kind="Internal").ap()

    x33 = x3.rearrange("(c p) t -> p c t", p=128)
    xo3 = xn4_out.rearrange("(c p) t -> p c t", p=128)
    TW = 512
    NT = TOK // TW

    with ExitStack() as ctx:
        emit_gate(block, gate)
        env = Env(nc, ctx, block, semctx, pfx)
        sb, sem, S = env.sb, env.sem, env.S
        env.onec_sb = sb("onec_sb", [128, 1], BF16)
        xt = [sb(f"xt{i}", [128, NCH, TW], F32) for i in range(2)]
        rstd_rep = [sb(f"rstd_rep{i}", [128, TW], F32) for i in range(2)]
        mr_rep = [sb(f"mr_rep{i}", [128, TW], F32) for i in range(2)]
        xnt = [sb(f"xnt{i}", [128, NCH, TW], F32) for i in range(2)]
        tmax = sb("tmax", [128, 1], F32)
        rmax = sb("rmax", [128, 1], F32)
        rrow = sb("rrow", [1, 128], F32)
        mx_sb = sb("mx_sb", [1, 1], F32)
        s_ldc = sem("s_ldc")
        s_xt2 = [sem(f"s_xt2{i}") for i in range(2)]
        s_lr = [sem(f"s_lr{i}") for i in range(2)]
        s_norm = sem("s_norm")
        s_mx = sem("s_mx")
        s_st2 = sem("s_st2")
        s_cw = sem("s_cw")
        s_tk = sem("s_tk")

        def init_loads(e):
            s_ldc.dma(e.dma_start(out=env.onec_sb[:], in_=ones_col))
        S("sync", init_loads)
        LDC = s_ldc.n

        s_rw, s_stA = emit_ln_stats(env, x33, NT, TW, lnrows,
                                    pre_wait=lambda e: e.wait_ge(s_ldc.h, LDC))
        A_RW = s_rw.n
        A_ST = s_stA.n
        norm_done = {}

        for j in range(NT):
            jb = j % 2

            def ld(e, j=j, jb=jb):
                if j < 2:
                    e.wait_ge(s_stA.h, A_ST)
                else:
                    e.wait_ge(s_norm.h, norm_done[j - 2])
                s_xt2[jb].dma(e.dma_start(out=xt[jb][:], in_=x33[:, :, j * TW:(j + 1) * TW]))
                e.wait_ge(s_rw.h, A_RW)
                s_lr[jb].dma(e.dma_start(out=rstd_rep[jb][:], in_=lnrows[j, 1:2, :].partition_broadcast(128)))
                s_lr[jb].dma(e.dma_start(out=mr_rep[jb][:], in_=lnrows[j, 0:1, :].partition_broadcast(128)))
            S("sync", ld)
            XT_J = s_xt2[jb].n
            LR_J = s_lr[jb].n

            def norm(e, j=j, jb=jb, XT_J=XT_J, LR_J=LR_J):
                e.wait_ge(s_xt2[jb].h, XT_J)
                e.wait_ge(s_lr[jb].h, LR_J)
                if j >= 2:
                    e.wait_ge(s_st2.h, (j - 1) * 16)
                e.tensor_tensor(out=xt[jb][:], in0=xt[jb][:], in1=rstd_rep[jb][:].unsqueeze(1).broadcast_to([128, NCH, TW]), op=ALU.mult)
                e.tensor_tensor(out=xnt[jb][:], in0=xt[jb][:], in1=mr_rep[jb][:].unsqueeze(1).broadcast_to([128, NCH, TW]), op=ALU.subtract)
                ins = e.tensor_reduce(out=tmax[:], in_=xnt[jb][:].rearrange('p c t -> p (c t)'), axis=mybir.AxisListType.X, op=ALU.max)
                s_tk.inc(ins)
                e.wait_ge(s_tk.h, s_tk.n)
                if j == 0:
                    ins = e.tensor_copy(out=rmax[:], in_=tmax[:])
                else:
                    ins = e.tensor_tensor(out=rmax[:], in0=rmax[:], in1=tmax[:], op=ALU.max)
                s_norm.inc(ins, 2)
            S("vector", norm)
            norm_done[j] = s_norm.n
            NORM_J = s_norm.n

            def store(e, j=j, jb=jb, NORM_J=NORM_J):
                e.wait_ge(s_norm.h, NORM_J)
                s_st2.dma(e.dma_start(out=xo3[:, :, j * TW:(j + 1) * TW], in_=xnt[jb][:]))
            S("gpsimd", store)

        LAST_NORM = s_norm.n

        def colw(e, LAST_NORM=LAST_NORM):
            e.wait_ge(s_norm.h, LAST_NORM)
            s_cw.dma(e.dma_start(out=colscr, in_=rmax[:]))
        S("gpsimd", colw)
        CW = s_cw.n

        def colr(e, CW=CW):
            e.wait_ge(s_cw.h, CW)
            s_cw.dma(e.dma_start(out=rrow[:], in_=colscr.rearrange("p o -> o p")))
        S("sync", colr)
        CR = s_cw.n

        def finmax(e, CR=CR):
            e.wait_ge(s_cw.h, CR)
            ins = e.tensor_reduce(out=mx_sb[:], in_=rrow[:], axis=mybir.AxisListType.X, op=ALU.max)
            s_mx.inc(ins)
        S("vector", finmax)
        MX = s_mx.n

        def mxw(e, MX=MX):
            e.wait_ge(s_mx.h, MX)
            s_cw.dma(e.dma_start(out=mx_out, in_=mx_sb[:]))
        S("gpsimd", mxw)

        def fin(e):
            e.wait_ge(s_st2.h, s_st2.n)
            e.wait_ge(s_cw.h, s_cw.n)
        S("sync", fin)
    return phase_terminals()


def ln4_inputs(x3_fm):
    import ml_dtypes
    return {"x3": x3_fm, "ones_col": np.ones((128, 1), ml_dtypes.bfloat16).view(np.uint16)}



def emit_qw1(nc, block, semctx, pfx, io, gate):
    xn4 = io["xn4"]
    mx8 = io["mx8"]
    w1 = io["w1"]
    b1_r = io["b1_r"]
    h_out = io["h"]
    hm_out = io["hm"]
    rowscr = nc.dram_tensor(pfx + "rowscr", [1, 2], F32, kind="Internal").ap()
    colscr = nc.dram_tensor(pfx + "colscr", [128, 1], F32, kind="Internal").ap()

    xn3 = xn4.rearrange("(c p) t -> p c t", p=128)
    w13 = w1.rearrange("(c p) o -> p c o", p=128)
    ho3 = h_out.rearrange("(c p) t -> p c t", p=128)
    TW = 512
    NT = TOK // TW
    EPS = 1e-8

    with ExitStack() as ctx:
        emit_gate(block, gate)
        env = Env(nc, ctx, block, semctx, pfx)
        sb, sem, S = env.sb, env.sem, env.S
        w1t_sb = sb("w1t_sb", [128, NCH, 4096], BF16)
        b1q_sb = sb("b1q_sb", [128, 32], F32)
        scol = sb("scol", [128, 1], F32)
        rcol = sb("rcol", [128, 1], F32)
        mrow = sb("mrow", [1, 8], F32)
        grow = sb("grow", [1, 2], F32)
        xq = [sb(f"xq{i}", [128, NCH, TW], BF16) for i in range(2)]
        xtt = [sb(f"xtt{i}", [128, NCH, TW], F32) for i in range(2)]
        xqi = sb("xqi", [128, NCH, TW], I32)
        ht = [sb(f"ht{i}", [128, TW], F32) for i in range(2)]
        tmax = sb("tmax", [128, 1], F32)
        rmax = sb("rmax", [128, 1], F32)
        rrow = sb("rrow", [1, 128], F32)
        hm_sb = sb("hm_sb", [1, 1], F32)
        s_ldc = sem("s_ldc")
        s_row = sem("s_row")
        s_rw = sem("s_rw")
        s_xt = [sem(f"s_xt{i}") for i in range(2)]
        s_xq = sem("s_xq")
        s_mm = sem("s_mm")
        s_ev = sem("s_ev")
        s_hx = sem("s_hx")
        s_st = sem("s_st")
        s_wq = sem("s_wq")
        s_wt = sem("s_wt")
        s_cw = sem("s_cw")
        s_tk = sem("s_tk")

        def tick(e, ins):
            s_tk.n += 1
            ins.then_inc(s_tk.h, 1)
            e.wait_ge(s_tk.h, s_tk.n)

        def init_loads(e):
            s_ldc.dma(e.dma_start(out=mrow[:], in_=mx8))
            s_ldc.dma(e.dma_start(out=b1q_sb[:], in_=b1_r))
        S("sync", init_loads)
        LDC = s_ldc.n

        def gmaxc(e, LDC=LDC):
            e.wait_ge(s_ldc.h, LDC)
            tick(e, e.tensor_reduce(out=grow[:, 0:1], in_=mrow[:], axis=mybir.AxisListType.X, op=ALU.max))
            tick(e, e.tensor_scalar(out=grow[:, 0:1], in0=grow[:, 0:1], scalar1=EPS, scalar2=1.0 / 255.0, op0=ALU.max, op1=ALU.mult))
            ins = e.reciprocal(out=grow[:, 1:2], in_=grow[:, 0:1])
            s_row.inc(ins)
        S("vector", gmaxc)
        GR = s_row.n

        def groww(e, GR=GR):
            e.wait_ge(s_row.h, GR)
            s_cw.dma(e.dma_start(out=rowscr, in_=grow[:]))
        S("gpsimd", groww)
        GW = s_cw.n

        def growr(e, GW=GW):
            e.wait_ge(s_cw.h, GW)
            s_cw.dma(e.dma_start(out=scol[:], in_=rowscr[0:1, 0:1].partition_broadcast(128)))
            s_cw.dma(e.dma_start(out=rcol[:], in_=rowscr[0:1, 1:2].partition_broadcast(128)))
        S("sync", growr)
        SCOL = s_cw.n

        wctx = ExitStack()
        wn = [wctx.enter_context(nc.sbuf_tensor(pfx + f"wn{i}", [128, E], F32)) for i in range(2)]
        wq = [wctx.enter_context(nc.sbuf_tensor(pfx + f"wq{i}", [128, E], BF16)) for i in range(2)]
        wqi = wctx.enter_context(nc.sbuf_tensor(pfx + "wqi", [128, E], I32))
        swc = wctx.enter_context(nc.sbuf_tensor(pfx + "swc", [128, 32], F32))
        swi = wctx.enter_context(nc.sbuf_tensor(pfx + "swi", [128, 1], F32))
        s_wn = [sem(f"s_wn{i}") for i in range(2)]

        def wloads(e):
            for fc in range(32):
                if fc >= 2:
                    e.wait_ge(s_wq.h, fc - 1)
                s_wn[fc % 2].dma(e.dma_start(out=wn[fc % 2][:], in_=w13[:, fc, :]))
        S("sync", wloads)

        for fc in range(32):
            def wquant(e, fc=fc):
                e.wait_ge(s_wn[fc % 2].h, (fc // 2 + 1) * 16)
                if fc >= 2:
                    e.wait_ge(s_wt.h, (fc - 1) * NCH * 16)
                tick(e, e.tensor_reduce(out=swc[:, fc:fc + 1], in_=wn[fc % 2][:], axis=mybir.AxisListType.X, op=ALU.max, apply_absolute_value=True))
                tick(e, e.tensor_scalar(out=swc[:, fc:fc + 1], in0=swc[:, fc:fc + 1], scalar1=EPS, scalar2=1.0 / 127.0, op0=ALU.max, op1=ALU.mult))
                tick(e, e.reciprocal(out=swi[:], in_=swc[:, fc:fc + 1]))
                e.tensor_scalar(out=wn[fc % 2][:], in0=wn[fc % 2][:], scalar1=swi[:], scalar2=None, op0=ALU.mult)
                e.tensor_copy(out=wqi[:], in_=wn[fc % 2][:])
                e.tensor_copy(out=wn[fc % 2][:], in_=wqi[:])
                ins = e.tensor_scalar(out=wq[fc % 2][:], in0=wn[fc % 2][:], scalar1=swc[:, fc:fc + 1], scalar2=None, op0=ALU.mult)
                s_wq.inc(ins)
            S("vector", wquant)
            WQ = s_wq.n

            def wtr(e, fc=fc, WQ=WQ):
                e.wait_ge(s_wq.h, WQ)
                for a in range(NCH):
                    s_wt.dma(e.dma_start(out=w1t_sb[:, a, fc * 128:(fc + 1) * 128], in_=wq[fc % 2][:, a * 128:(a + 1) * 128], transpose=True))
            S("scalar", wtr)

        WT_ALL = s_wt.n
        WQ_ALL = s_wq.n
        wctx2 = ExitStack()
        b1t = wctx2.enter_context(nc.sbuf_tensor(pfx + "b1t", [128, 32], F32))
        b1i = wctx2.enter_context(nc.sbuf_tensor(pfx + "b1i", [128, 32], I32))
        swinv = wctx2.enter_context(nc.sbuf_tensor(pfx + "swinv", [128, 32], F32))

        def b1qc(e, WQ_ALL=WQ_ALL, SCOL=SCOL):
            e.wait_ge(s_wq.h, WQ_ALL)
            e.wait_ge(s_cw.h, SCOL)
            tick(e, e.reciprocal(out=swinv[:], in_=swc[:]))
            tick(e, e.tensor_tensor(out=b1t[:], in0=b1q_sb[:], in1=swinv[:], op=ALU.mult))
            tick(e, e.tensor_scalar(out=b1t[:], in0=b1t[:], scalar1=rcol[:], scalar2=None, op0=ALU.mult))
            tick(e, e.tensor_copy(out=b1i[:], in_=b1t[:]))
            tick(e, e.tensor_copy(out=b1t[:], in_=b1i[:]))
            tick(e, e.tensor_tensor(out=b1t[:], in0=b1t[:], in1=swc[:], op=ALU.mult))
            ins = e.tensor_scalar(out=b1q_sb[:], in0=b1t[:], scalar1=scol[:], scalar2=None, op0=ALU.mult)
            s_hx.inc(ins)
        S("vector", b1qc)
        B1Q = s_hx.n

        with nc.psum_tensor(pfx + "ps_a", [128, 512], F32) as ps_a, nc.psum_tensor(pfx + "ps_b", [128, 512], F32) as ps_b:
            ps_pp = [ps_a, ps_b]
            for j in range(NT):
                jb = j % 2

                def ld(e, j=j, jb=jb, WT_ALL=WT_ALL):
                    if j == 0:
                        e.wait_ge(s_wt.h, WT_ALL)
                        e.wait_ge(s_hx.h, B1Q)
                    if j >= 2:
                        e.wait_ge(s_xq.h, (j - 1) * 3)
                    s_xt[jb].dma(e.dma_start(out=xtt[jb][:], in_=xn3[:, :, j * TW:(j + 1) * TW]))
                S("sync", ld)
                XT_J = s_xt[jb].n

                def xquant(e, j=j, jb=jb, XT_J=XT_J, SCOL=SCOL):
                    e.wait_ge(s_xt[jb].h, XT_J)
                    if j == 0:
                        e.wait_ge(s_cw.h, SCOL)
                        e.wait_ge(s_hx.h, B1Q)
                    if j >= 2:
                        e.wait_ge(s_mm.h, (j - 1) * 32)
                    e.tensor_scalar(out=xtt[jb][:], in0=xtt[jb][:], scalar1=rcol[:], scalar2=0.0, op0=ALU.mult, op1=ALU.max)
                    e.tensor_copy(out=xqi[:], in_=xtt[jb][:])
                    ins = e.tensor_copy(out=xq[jb][:], in_=xqi[:])
                    s_xq.inc(ins, 3)
                S("vector", xquant)
                XQ_J = s_xq.n

                for fc in range(32):
                    fb = fc % 2

                    def mmf(e, jb=jb, fc=fc, fb=fb, XQ_J=XQ_J, EV=s_ev.n):
                        e.wait_ge(s_xq.h, XQ_J)
                        if EV >= 2:
                            e.wait_ge(s_ev.h, EV - 1)
                        for a in range(NCH):
                            ins = e.matmul(ps_pp[fb][:], w1t_sb[:, a, fc * 128:(fc + 1) * 128], xq[jb][:, a, :], start=(a == 0), stop=(a == NCH - 1))
                        s_mm.inc(ins)
                    S("tensor", mmf)
                    MM_F = s_mm.n

                    def evf(e, j=j, fb=fb, fc=fc, MM_F=MM_F, HX=s_hx.n, K=(j * 32 + fc)):
                        e.wait_ge(s_mm.h, MM_F)
                        if HX >= B1Q + 2:
                            e.wait_ge(s_hx.h, HX - 1)
                        if K >= 2:
                            e.wait_ge(s_st.h, (K - 1) * 16)
                        ins = e.activation(out=ht[fb][:], in_=ps_pp[fb][:], func=AF.Relu, bias=b1q_sb[:, fc:fc + 1], scale=scol[:])
                        s_ev.inc(ins)
                    S("scalar", evf)
                    EV_F = s_ev.n

                    def hmx(e, j=j, fc=fc, fb=fb, EV_F=EV_F):
                        e.wait_ge(s_ev.h, EV_F)
                        e.tensor_reduce(out=tmax[:], in_=ht[fb][:], axis=mybir.AxisListType.X, op=ALU.max)
                        if j == 0 and fc == 0:
                            ins = e.tensor_copy(out=rmax[:], in_=tmax[:])
                        else:
                            ins = e.tensor_tensor(out=rmax[:], in0=rmax[:], in1=tmax[:], op=ALU.max)
                        s_hx.inc(ins)
                    S("vector", hmx)
                    HX_F = s_hx.n

                    def hst(e, j=j, fc=fc, fb=fb, HX_F=HX_F):
                        e.wait_ge(s_hx.h, HX_F)
                        s_st.dma(e.dma_start(out=ho3[:, fc, j * TW:(j + 1) * TW], in_=ht[fb][:]))
                    S("gpsimd", hst)

        LAST_HX = s_hx.n

        def colw(e, LAST_HX=LAST_HX):
            e.wait_ge(s_hx.h, LAST_HX)
            s_cw.dma(e.dma_start(out=colscr, in_=rmax[:]))
        S("gpsimd", colw)
        CW2 = s_cw.n

        def colr(e, CW2=CW2):
            e.wait_ge(s_cw.h, CW2)
            s_cw.dma(e.dma_start(out=rrow[:], in_=colscr.rearrange("p o -> o p")))
        S("sync", colr)
        CR2 = s_cw.n

        def finmax(e, CR2=CR2):
            e.wait_ge(s_cw.h, CR2)
            ins = e.tensor_reduce(out=hm_sb[:], in_=rrow[:], axis=mybir.AxisListType.X, op=ALU.max)
            s_hx.inc(ins)
        S("vector", finmax)
        FM = s_hx.n

        def hmw(e, FM=FM):
            e.wait_ge(s_hx.h, FM)
            s_cw.dma(e.dma_start(out=hm_out, in_=hm_sb[:]))
        S("gpsimd", hmw)

        def fin(e):
            e.wait_ge(s_st.h, s_st.n)
            e.wait_ge(s_cw.h, s_cw.n)
        S("sync", fin)
        wctx2.close()
        wctx.close()
    return phase_terminals()


def qw1_inputs(xn4_fm, mx8, inp):
    return {
        "xn4": xn4_fm,
        "mx8": mx8.reshape(1, 8),
        "w1": inp["L1_ffn_w1"].copy(),
        "b1_r": np.ascontiguousarray(inp["L1_ffn_b1"].reshape(32, 128).T),
    }



def emit_qw2(nc, block, semctx, pfx, io, gate):
    h_in = io["h"]
    hm8 = io["hm8"]
    w2 = io["w2"]
    b2_r = io["b2_r"]
    x3 = io["x3"]
    x4_out = io["x4"]
    rowscr = nc.dram_tensor(pfx + "rowscr", [1, 2], F32, kind="Internal").ap()

    h3 = h_in.rearrange("(c p) t -> p c t", p=128)
    w23 = w2.rearrange("(c p) o -> p c o", p=128)
    x33 = x3.rearrange("(c p) t -> p c t", p=128)
    xo3 = x4_out.rearrange("(c p) t -> p c t", p=128)
    TW = 512
    NT = TOK // TW
    EPS = 1e-8

    with ExitStack() as ctx:
        emit_gate(block, gate)
        env = Env(nc, ctx, block, semctx, pfx)
        sb, sem, S = env.sb, env.sem, env.S
        w2t_sb = sb("w2t_sb", [128, 32, E], BF16)
        b2q_sb = sb("b2q_sb", [128, NCH], F32)
        scol = sb("scol", [128, 1], F32)
        rcol = sb("rcol", [128, 1], F32)
        mrow = sb("mrow", [1, 8], F32)
        grow = sb("grow", [1, 2], F32)
        s_ldc = sem("s_ldc")
        s_row = sem("s_row")
        s_ht = [sem(f"s_ht{i}") for i in range(2)]
        s_x3l = [sem(f"s_x3l{i}") for i in range(2)]
        s_hq = sem("s_hq")
        s_mm = sem("s_mm")
        s_ev = sem("s_ev")
        s_x4 = sem("s_x4")
        s_st = sem("s_st")
        s_wq = sem("s_wq")
        s_wt = sem("s_wt")
        s_cw = sem("s_cw")
        s_tk = sem("s_tk")

        def tick(e, ins):
            s_tk.n += 1
            ins.then_inc(s_tk.h, 1)
            e.wait_ge(s_tk.h, s_tk.n)

        def init_loads(e):
            s_ldc.dma(e.dma_start(out=mrow[:], in_=hm8))
            s_ldc.dma(e.dma_start(out=b2q_sb[:], in_=b2_r))
        S("sync", init_loads)
        LDC = s_ldc.n

        def gmaxc(e, LDC=LDC):
            e.wait_ge(s_ldc.h, LDC)
            tick(e, e.tensor_reduce(out=grow[:, 0:1], in_=mrow[:], axis=mybir.AxisListType.X, op=ALU.max))
            tick(e, e.tensor_scalar(out=grow[:, 0:1], in0=grow[:, 0:1], scalar1=EPS, scalar2=1.0 / 255.0, op0=ALU.max, op1=ALU.mult))
            ins = e.reciprocal(out=grow[:, 1:2], in_=grow[:, 0:1])
            s_row.inc(ins)
        S("vector", gmaxc)
        GR = s_row.n

        def groww(e, GR=GR):
            e.wait_ge(s_row.h, GR)
            s_cw.dma(e.dma_start(out=rowscr, in_=grow[:]))
        S("gpsimd", groww)
        GW = s_cw.n

        def growr(e, GW=GW):
            e.wait_ge(s_cw.h, GW)
            s_cw.dma(e.dma_start(out=scol[:], in_=rowscr[0:1, 0:1].partition_broadcast(128)))
            s_cw.dma(e.dma_start(out=rcol[:], in_=rowscr[0:1, 1:2].partition_broadcast(128)))
        S("sync", growr)
        SCOL = s_cw.n

        wctx = ExitStack()
        wn = [wctx.enter_context(nc.sbuf_tensor(pfx + f"wn{i}", [128, 4096], F32)) for i in range(2)]
        wq = [wctx.enter_context(nc.sbuf_tensor(pfx + f"wq{i}", [128, 4096], BF16)) for i in range(2)]
        wqi = wctx.enter_context(nc.sbuf_tensor(pfx + "wqi", [128, 4096], I32))
        swc = wctx.enter_context(nc.sbuf_tensor(pfx + "swc", [128, NCH], F32))
        swi = wctx.enter_context(nc.sbuf_tensor(pfx + "swi", [128, 1], F32))
        swinv = wctx.enter_context(nc.sbuf_tensor(pfx + "swinv", [128, NCH], F32))
        b2t = wctx.enter_context(nc.sbuf_tensor(pfx + "b2t", [128, NCH], F32))
        b2i = wctx.enter_context(nc.sbuf_tensor(pfx + "b2i", [128, NCH], I32))
        s_wn = [sem(f"s_wn{i}") for i in range(2)]

        def wloads(e):
            for dc in range(NCH):
                if dc >= 2:
                    e.wait_ge(s_wq.h, dc - 1)
                s_wn[dc % 2].dma(e.dma_start(out=wn[dc % 2][:], in_=w23[:, dc, :]))
        S("sync", wloads)

        for dc in range(NCH):
            def wquant(e, dc=dc):
                e.wait_ge(s_wn[dc % 2].h, (dc // 2 + 1) * 16)
                if dc >= 2:
                    e.wait_ge(s_wt.h, (dc - 1) * 32 * 16)
                tick(e, e.tensor_reduce(out=swc[:, dc:dc + 1], in_=wn[dc % 2][:], axis=mybir.AxisListType.X, op=ALU.max, apply_absolute_value=True))
                tick(e, e.tensor_scalar(out=swc[:, dc:dc + 1], in0=swc[:, dc:dc + 1], scalar1=EPS, scalar2=1.0 / 127.0, op0=ALU.max, op1=ALU.mult))
                tick(e, e.reciprocal(out=swi[:], in_=swc[:, dc:dc + 1]))
                e.tensor_scalar(out=wn[dc % 2][:], in0=wn[dc % 2][:], scalar1=swi[:], scalar2=None, op0=ALU.mult)
                e.tensor_copy(out=wqi[:], in_=wn[dc % 2][:])
                e.tensor_copy(out=wn[dc % 2][:], in_=wqi[:])
                ins = e.tensor_scalar(out=wq[dc % 2][:], in0=wn[dc % 2][:], scalar1=swc[:, dc:dc + 1], scalar2=None, op0=ALU.mult)
                s_wq.inc(ins)
            S("vector", wquant)
            WQ = s_wq.n

            def wtr(e, dc=dc, WQ=WQ):
                e.wait_ge(s_wq.h, WQ)
                for a in range(32):
                    s_wt.dma(e.dma_start(out=w2t_sb[:, a, dc * 128:(dc + 1) * 128], in_=wq[dc % 2][:, a * 128:(a + 1) * 128], transpose=True))
            S("scalar", wtr)

        WT_ALL = s_wt.n
        WQ_ALL = s_wq.n

        def b2qc(e, WQ_ALL=WQ_ALL, SCOL=SCOL):
            e.wait_ge(s_wq.h, WQ_ALL)
            e.wait_ge(s_cw.h, SCOL)
            tick(e, e.reciprocal(out=swinv[:], in_=swc[:]))
            tick(e, e.tensor_tensor(out=b2t[:], in0=b2q_sb[:], in1=swinv[:], op=ALU.mult))
            tick(e, e.tensor_scalar(out=b2t[:], in0=b2t[:], scalar1=rcol[:], scalar2=None, op0=ALU.mult))
            tick(e, e.tensor_copy(out=b2i[:], in_=b2t[:]))
            tick(e, e.tensor_copy(out=b2t[:], in_=b2i[:]))
            tick(e, e.tensor_tensor(out=b2t[:], in0=b2t[:], in1=swc[:], op=ALU.mult))
            ins = e.tensor_scalar(out=b2q_sb[:], in0=b2t[:], scalar1=scol[:], scalar2=None, op0=ALU.mult)
            s_hq.inc(ins)
        S("vector", b2qc)
        B2Q = s_hq.n
        wctx.close()

        hq0 = sb("hq0", [128, 32, TW], BF16)
        hq = [hq0, hq0]
        hf = [sb(f"hf{i}", [128, NCH, TW], F32) for i in range(2)]
        hqi = sb("hqi", [128, NCH, TW], I32)
        x3t = [sb(f"x3t{i}", [128, NCH, TW], F32) for i in range(2)]
        ot = sb("ot", [128, NCH, TW], F32)

        with nc.psum_tensor(pfx + "ps_a", [128, 512], F32) as ps_a, nc.psum_tensor(pfx + "ps_b", [128, 512], F32) as ps_b:
            ps_pp = [ps_a, ps_b]
            pp_ev = []
            pp_i = [0]

            def group(mm_emit, ev_emit, pre_mm=None, ev_eng=("vector",)):
                b = pp_i[0] % 2

                def mm(e, b=b, prev=(pp_ev[-2] if len(pp_ev) >= 2 else None)):
                    if pre_mm:
                        pre_mm(e)
                    if prev is not None:
                        e.wait_ge(s_ev.h, prev)
                    ins = mm_emit(e, ps_pp[b])
                    s_mm.inc(ins)
                S("tensor", mm)
                MM = s_mm.n

                def ev(e, b=b, MM=MM, eng=None):
                    e.wait_ge(s_mm.h, MM)
                    ins = ev_emit(e, ps_pp[b])
                    s_ev.inc(ins)
                S(ev_eng[0] if ev_eng else "vector", ev)
                pp_ev.append(s_ev.n)
                pp_i[0] += 1
                return s_ev.n

            hq_done = {}
            for j in range(NT):
                for fs in range(4):
                    fb = fs % 2

                    def hld(e, j=j, fs=fs, fb=fb, WT_ALL=WT_ALL, B2Q=B2Q):
                        if j == 0 and fs == 0:
                            e.wait_ge(s_wt.h, WT_ALL)
                            e.wait_ge(s_hq.h, B2Q)
                        k = j * 4 + fs
                        if k >= 2:
                            e.wait_ge(s_hq.h, B2Q + (k - 1) * 2)
                        s_ht[fb].dma(e.dma_start(out=hf[fb][:], in_=h3[:, fs * 8:(fs + 1) * 8, j * TW:(j + 1) * TW]))
                    S("sync", hld)
                    HT_K = s_ht[fb].n

                    def hquant(e, j=j, fs=fs, fb=fb, HT_K=HT_K):
                        e.wait_ge(s_ht[fb].h, HT_K)
                        if fs == 0 and j >= 1:
                            e.wait_ge(s_mm.h, j * 8)
                        e.tensor_scalar(out=hf[fb][:], in0=hf[fb][:], scalar1=rcol[:], scalar2=None, op0=ALU.mult)
                        e.tensor_copy(out=hqi[:], in_=hf[fb][:])
                        ins = e.tensor_copy(out=hq0[:, fs * 8:(fs + 1) * 8, :], in_=hqi[:])
                        s_hq.inc(ins, 2)
                    S("vector", hquant)
                HQ_J = s_hq.n
                hq_done[j] = HQ_J
                jb = j % 2

                def x3ld(e, j=j, jb=jb):
                    if j >= 2:
                        e.wait_ge(s_x4.h, j - 1)
                        e.wait_ge(s_st.h, (j - 1) * 16)
                    s_x3l[jb].dma(e.dma_start(out=x3t[jb][:], in_=x33[:, :, j * TW:(j + 1) * TW]))
                S("sync", x3ld)
                X3_J = s_x3l[jb].n

                for dc in range(NCH):
                    def omm(e, ps, dc=dc, HQ_J=HQ_J):
                        e.wait_ge(s_hq.h, HQ_J)
                        for a in range(32):
                            ins = e.matmul(ps[:], w2t_sb[:, a, dc * 128:(dc + 1) * 128], hq0[:, a, :], start=(a == 0), stop=(a == 31))
                        return ins

                    def oev(e, ps, j=j, dc=dc):
                        if dc == 0 and j >= 1:
                            e.wait_ge(s_x4.h, j)
                        return e.activation(out=ot[:, dc, :], in_=ps[:], func=AF.Identity, bias=b2q_sb[:, dc:dc + 1], scale=scol[:])

                    group(omm, oev, ev_eng=("scalar",))
                OEV_J = s_ev.n

                def resadd(e, jb=jb, OEV_J=OEV_J, X3_J=X3_J):
                    e.wait_ge(s_ev.h, OEV_J)
                    e.wait_ge(s_x3l[jb].h, X3_J)
                    if s_st.n:
                        e.wait_ge(s_st.h, s_st.n)
                    ins = e.tensor_tensor(out=x3t[jb][:], in0=x3t[jb][:], in1=ot[:], op=ALU.add)
                    s_x4.inc(ins)
                S("vector", resadd)
                X4_J = s_x4.n

                def store(e, j=j, jb=jb, X4_J=X4_J):
                    e.wait_ge(s_x4.h, X4_J)
                    s_st.dma(e.dma_start(out=xo3[:, :, j * TW:(j + 1) * TW], in_=x3t[jb][:]))
                S("gpsimd", store)

            def fin(e):
                e.wait_ge(s_st.h, s_st.n)
            S("sync", fin)
    return phase_terminals()


def qw2_inputs(h_fm, hm8, x3_fm, inp):
    return {
        "h": h_fm,
        "hm8": hm8.reshape(1, 8),
        "w2": inp["L1_ffn_w2"].copy(),
        "b2_r": np.ascontiguousarray(inp["L1_ffn_b2"].reshape(NCH, 128).T),
        "x3": x3_fm,
    }



import ml_dtypes as _mld


def _bits(a):
    if a.dtype == _mld.bfloat16:
        return a.view(np.uint16)
    return np.ascontiguousarray(a)


def _bf(a):
    if a.dtype == np.uint16:
        return a.view(_mld.bfloat16)
    return a


_CORES = list(range(8))




def _mk_base():
    nc = bass.Bass("TRN2", target_bir_lowering=False, debug=False)
    stk = ExitStack()
    block = stk.enter_context(nc.Block())
    return nc, stk, block


def build_L1():
    nc, stk, block = _mk_base()
    EI = lambda n, s, d: nc.dram_tensor(n, s, d, kind="ExternalInput").ap()
    EO = lambda n, s, d: nc.dram_tensor(n, s, d, kind="ExternalOutput").ap()
    IN = lambda n, s, d: nc.dram_tensor(n, s, d, kind="Internal").ap()
    io = {}
    io["xh"] = EI("xh", [E, NC_LOC * W], F32)
    for n, s in [("wqT", [E, E]), ("wkT", [E, E]), ("wvT", [E, E])]:
        io[n] = EI(n, s, F32)
    io["bq_r"] = EI("bq_r", [128, NCH], F32)
    io["bk_r"] = EI("bk_r", [128, NCH], F32)
    io["bv_row"] = EI("bv_row", [1, E], F32)
    io["ones_col"] = EI("ones_col", [128, 1], BF16)
    io["ones_row"] = EI("ones_row", [1, 128], BF16)
    io["ones_c16"] = EI("ones_c16", [128, 16], BF16)
    io["ones64f"] = EI("ones64f", [1, 64], F32)
    io["ek"] = EI("ek", [1, 2], F32)
    io["mlo"] = EI("mlo", [2, 128, W], F32)
    io["mhi"] = EI("mhi", [2, 128, W], F32)
    for n, s in [("u1", [E, 256]), ("v1", [256, 4096]), ("u2", [4096, 256]), ("v2", [256, E])]:
        io[n] = EI(n, s, F32)
    io["b1_r"] = EI("b1_r", [128, 32], F32)
    io["b2_r"] = EI("b2_r", [128, NCH], F32)
    io["win"] = EI("win", [E, 2048], F32)
    io["bin_r"] = EI("bin_r", [128, 16], F32)
    io["x1"] = IN("x1_i", [E, TOK], F32)
    io["x2"] = EO("x2", [E, TOK], F32)
    io["xg"] = EO("xg", [2048, TOK], F32)
    with stk:
        g = emit_l0attn(nc, block, stk, "a_", io, [])
        g = emit_l0ffn(nc, block, stk, "b_", io, g)
        g = emit_l1pre(nc, block, stk, "c_", io, g)
    return nc


def build_L2():
    nc, stk, block = _mk_base()
    EI = lambda n, s, d: nc.dram_tensor(n, s, d, kind="ExternalInput").ap()
    EO = lambda n, s, d: nc.dram_tensor(n, s, d, kind="ExternalOutput").ap()
    IN = lambda n, s, d: nc.dram_tensor(n, s, d, kind="Internal").ap()
    io = {}
    io["xs"] = EI("xs", [E, 8192], F32)
    io["cw"] = EI("cw", [128, NCH, 4], F32)
    io["cb"] = EI("cb", [128, NCH], F32)
    io["zer"] = EI("zer", [128, 4], F32)
    io["wx"] = EI("wx", [E, 32], F32)
    io["bx"] = EI("bx", [32, 1], F32)
    io["alane"] = EI("alane", [128, 64], F32)
    io["dcol"] = EI("dcol", [128, 4], F32)
    io["sel"] = EI("sel", [16, 128, 128], BF16)
    io["zcol"] = EI("zcol", [128, 1], F32)
    io["xcb"] = IN("xcb_i", [E, 8192], BF16)
    io["bc"] = IN("bc_i", [32, 8192], BF16)
    io["y"] = EO("y", [512, 8192], F32)
    io["ng"] = NCH
    io["ncols"] = 8192
    with stk:
        g = emit_conv(nc, block, stk, "d1_", io, [])
        g = emit_bc(nc, block, stk, "d2_", io, g)
        g = emit_scan(nc, block, stk, "d3_", io, g)
    return nc


def build_L3():
    nc, stk, block = _mk_base()
    EI = lambda n, s, d: nc.dram_tensor(n, s, d, kind="ExternalInput").ap()
    EO = lambda n, s, d: nc.dram_tensor(n, s, d, kind="ExternalOutput").ap()
    io = {}
    io["y"] = EI("y", [E, TOK], F32)
    io["gate"] = EI("gate", [E, TOK], F32)
    io["x2"] = EI("x2", [E, TOK], F32)
    io["wo"] = EI("wo", [E, E], F32)
    io["bo_r"] = EI("bo_r", [128, NCH], F32)
    io["ones_col"] = EI("ones_col", [128, 1], BF16)
    io["x3"] = EO("x3", [E, TOK], F32)
    io["xn4"] = EO("xn4", [E, TOK], F32)
    io["mx"] = EO("mx", [1, 1], F32)
    with stk:
        g = emit_l1post(nc, block, stk, "e_", io, [])
        g = emit_ln4(nc, block, stk, "f_", io, g)
    return nc


def build_L4():
    nc, stk, block = _mk_base()
    EI = lambda n, s, d: nc.dram_tensor(n, s, d, kind="ExternalInput").ap()
    EO = lambda n, s, d: nc.dram_tensor(n, s, d, kind="ExternalOutput").ap()
    io = {}
    io["xn4"] = EI("xn4", [E, TOK], F32)
    io["mx8"] = EI("mx8", [1, 8], F32)
    io["w1"] = EI("w1", [4096, E], F32)
    io["b1_r"] = EI("b1_r", [128, 32], F32)
    io["h"] = EO("h", [4096, TOK], F32)
    io["hm"] = EO("hm", [1, 1], F32)
    with stk:
        emit_qw1(nc, block, stk, "g_", io, [])
    return nc


def build_L5():
    nc, stk, block = _mk_base()
    EI = lambda n, s, d: nc.dram_tensor(n, s, d, kind="ExternalInput").ap()
    EO = lambda n, s, d: nc.dram_tensor(n, s, d, kind="ExternalOutput").ap()
    io = {}
    io["h"] = EI("h", [4096, TOK], F32)
    io["hm8"] = EI("hm8", [1, 8], F32)
    io["w2"] = EI("w2", [E, 4096], F32)
    io["b2_r"] = EI("b2_r", [128, NCH], F32)
    io["x3"] = EI("x3", [E, TOK], F32)
    io["x4"] = EO("x4", [E, TOK], F32)
    with stk:
        emit_qw2(nc, block, stk, "h_", io, [])
    return nc


def kernel(**inputs):
    inp = {k: np.asarray(v) for k, v in inputs.items()}
    x0 = inp["L0_attn_hidden_states"]
    xfm = [np.ascontiguousarray(x0[b].T) for b in range(4)]
    pairs = [(b, th) for b in range(4) for th in range(2)]

    nc = build_L1()
    in1 = []
    for (b, th) in pairs:
        m = l0attn_inputs(xfm[b], inp, th)
        m2 = l0ffn_inputs(np.zeros(1, np.float32), inp)
        m2.pop("x1")
        m.update(m2)
        m3 = l1pre_inputs(np.zeros(1, np.float32), inp)
        m3.pop("x2")
        m.update(m3)
        in1.append(m)
    res = run_bass_kernel_spmd(nc, in1, core_ids=_CORES)
    x2 = [res.results[c]["x2"] for c in range(8)]
    xg = [res.results[c]["xg"] for c in range(8)]

    nc = build_L2()
    in2 = []
    for (b, th) in pairs:
        full_ssm = np.concatenate([xg[2 * b][0:E, :], xg[2 * b + 1][0:E, :]], axis=1)
        permidx = np.concatenate([np.arange(th * 512, (th + 1) * 512), np.arange((1 - th) * 512, (2 - th) * 512)])
        m = {
            "xs": np.ascontiguousarray(full_ssm[permidx]),
            "cw": np.ascontiguousarray(inp["L1_attn_conv_w"][permidx].reshape(NCH, 128, 4).transpose(1, 0, 2)),
            "cb": np.ascontiguousarray(inp["L1_attn_conv_b"][permidx].reshape(NCH, 128).T),
            "zer": np.zeros((128, 4), np.float32),
            "wx": np.ascontiguousarray(inp["L1_attn_W_x"][permidx]),
            "bx": inp["L1_attn_b_x"].reshape(32, 1).copy(),
        }
        m4 = scan_inputs(np.zeros(1, np.float32), np.zeros(1, np.float32), inp, th)
        for k in ("xcb", "bc"):
            m4.pop(k)
        m.update(m4)
        in2.append(m)
    res = run_bass_kernel_spmd(nc, in2, core_ids=_CORES)
    yv = [res.results[c]["y"] for c in range(8)]

    nc = build_L3()
    in3 = []
    for (b, th) in pairs:
        y_full = np.concatenate([yv[2 * b], yv[2 * b + 1]], axis=0)
        m = l1post_inputs(np.ascontiguousarray(y_full[:, th * 4096:(th + 1) * 4096]),
                          np.ascontiguousarray(xg[2 * b + th][E:2 * E, :]),
                          x2[2 * b + th], inp)
        m5 = ln4_inputs(np.zeros(1, np.float32))
        m5.pop("x3")
        m.update(m5)
        in3.append(m)
    res = run_bass_kernel_spmd(nc, in3, core_ids=_CORES)
    x3 = [res.results[c]["x3"] for c in range(8)]
    xn4 = [res.results[c]["xn4"] for c in range(8)]
    mx8 = np.array([float(res.results[c]["mx"][0, 0]) for c in range(8)], np.float32)

    nc = build_L4()
    res = run_bass_kernel_spmd(nc, [qw1_inputs(xn4[c], mx8, inp) for c in range(8)], core_ids=_CORES)
    hh = [res.results[c]["h"] for c in range(8)]
    hm8 = np.array([float(res.results[c]["hm"][0, 0]) for c in range(8)], np.float32)

    nc = build_L5()
    res = run_bass_kernel_spmd(nc, [qw2_inputs(hh[c], hm8, x3[c], inp) for c in range(8)], core_ids=_CORES)
    x4 = [res.results[c]["x4"] for c in range(8)]

    out = np.empty((4, 8192, 1024), np.float32)
    for b in range(4):
        for th in range(2):
            out[b, th * 4096:(th + 1) * 4096, :] = x4[2 * b + th].T
    return out
